# version 08
# baseline (speedup 1.0000x reference)
import numpy as np
import ml_dtypes
from contextlib import ExitStack

import concourse.bass as bass
import concourse.mybir as mybir
from concourse.bass_utils import run_bass_kernel_spmd

F32 = mybir.dt.float32
BF16 = mybir.dt.bfloat16
AF = mybir.ActivationFunctionType
ALU = mybir.AluOpType
AXX = mybir.AxisListType.X

CORES = 8
T = 1024
D = 2048
H = 16
HD = 128
S = 8192
NT = T // 128
NC = D // 128
JSH = 6 * D // CORES
NSH = D // CORES
SCALE = float(HD) ** -0.5

_DEBUG = []


class Sem:
    def __init__(self, ctx, nc, name):
        self.h = ctx.enter_context(nc.semaphore(name))
        self.n = 0

    def inc(self, inst, k=1):
        inst.then_inc(self.h, k)
        self.n += k
        return self.n


def _dma(eng, sem, out, in_):
    eng.dma_start(out=out, in_=in_).then_inc(sem.h, 16)
    sem.n += 16
    return sem.n


def build_nc():
    nc = bass.Bass("TRN2", target_bir_lowering=False,
                   detect_race_conditions=False)

    x_in = nc.dram_tensor("x", [T, D], F32, kind="ExternalInput").ap()
    wq4 = nc.dram_tensor("wq4", [NC, 128, NC, 128], F32, kind="ExternalInput").ap()
    wk4 = nc.dram_tensor("wk4", [NC, 128, NC, 128], F32, kind="ExternalInput").ap()
    wo4 = nc.dram_tensor("wo4", [NC, 128, NC, 128], F32, kind="ExternalInput").ap()
    wv_r = nc.dram_tensor("wv_r", [NC, 128, D], F32, kind="ExternalInput").ap()
    bqm_in = nc.dram_tensor("bqm", [128, NC], F32, kind="ExternalInput").ap()
    bkm_in = nc.dram_tensor("bkm", [128, NC], F32, kind="ExternalInput").ap()
    bvrow_in = nc.dram_tensor("bvrow", [4, 512], F32, kind="ExternalInput").ap()
    borow_in = nc.dram_tensor("borow", [4, 512], F32, kind="ExternalInput").ap()
    fbrow_in = nc.dram_tensor("fbrow", [4, 512], F32, kind="ExternalInput").ap()
    fws_in = nc.dram_tensor("fws", [2, 128, D], F32, kind="ExternalInput").ap()
    cm_in = nc.dram_tensor("cm", [128, NC], F32, kind="ExternalInput").ap()
    wmods_in = nc.dram_tensor("wmods", [NC, 128, JSH], F32, kind="ExternalInput").ap()
    cos_in = nc.dram_tensor("cosm", [128, T], F32, kind="ExternalInput").ap()
    sin_in = nc.dram_tensor("sinm", [128, T], F32, kind="ExternalInput").ap()
    eyeb_in = nc.dram_tensor("eyeb", [128, 128], BF16, kind="ExternalInput").ap()
    eyef_in = nc.dram_tensor("eyef", [128, 128], F32, kind="ExternalInput").ap()
    onesf_in = nc.dram_tensor("onesf", [1, 128], F32, kind="ExternalInput").ap()
    e4f_in = nc.dram_tensor("e4f", [4, 512], F32, kind="ExternalInput").ap()
    ehb_in = nc.dram_tensor("ehb", [128, H * H], BF16, kind="ExternalInput").ap()

    out_t = nc.dram_tensor("out", [T, D], F32, kind="ExternalOutput")

    KSZ = NC * 128 * T
    VSZ = H * 128 * NT * 128
    KVN = KSZ + VSZ + H
    kv_in = nc.dram_tensor("kv_in", [KVN], BF16)
    kv_g = nc.dram_tensor("kv_g", [CORES * KVN], BF16, addr_space="Shared")
    q_st = nc.dram_tensor("q_st", [H, 128, T], BF16)
    mod_in = nc.dram_tensor("mod_in", [JSH], F32)
    mod_g = nc.dram_tensor("mod_g", [CORES * JSH], F32, addr_space="Shared")
    gam_in = nc.dram_tensor("gam_in", [NSH], F32)
    gam_g = nc.dram_tensor("gam_g", [CORES * NSH], F32, addr_space="Shared")
    tern_in = nc.dram_tensor("tern_in", [NC, 128, NSH], BF16)
    tern_g = nc.dram_tensor("tern_g", [CORES, NC, 128, NSH], BF16, addr_space="Shared")

    dbg = {}
    def dbg_out(name, shape, dtype=F32):
        if name in _DEBUG:
            dbg[name] = nc.dram_tensor("dbg_" + name, shape, dtype,
                                       kind="ExternalOutput").ap()
            return dbg[name]
        return None

    ctx = ExitStack()
    with ctx:
        e = ctx.enter_context
        eyeb = e(nc.sbuf_tensor("sb_eyeb", [128, 128], BF16))
        eyef = e(nc.sbuf_tensor("sb_eyef", [128, 128], F32))
        onesf = e(nc.sbuf_tensor("sb_onesf", [1, 128], F32))
        e4f = e(nc.sbuf_tensor("sb_e4f", [4, 512], F32))
        ehb = e(nc.sbuf_tensor("sb_ehb", [128, H * H], BF16))
        bqm = e(nc.sbuf_tensor("sb_bqm", [128, NC], F32))
        bkm = e(nc.sbuf_tensor("sb_bkm", [128, NC], F32))
        def row4(name):
            return e(nc.sbuf_tensor(name, [4, 512], F32))
        bv4, bo4, fb4, gabo4, gbg4, gbfb4, rs0, rs1 = (
            row4(n) for n in ("bv4", "bo4", "fb4", "gabo4", "gbg4",
                              "gbfb4", "rs0", "rs1"))
        modr = [e(nc.sbuf_tensor(f"modr{m}", [4, 512], F32)) for m in range(6)]
        slot = [e(nc.sbuf_tensor(f"slot{i}", [128, D], F32)) for i in range(2)]
        negcq = e(nc.sbuf_tensor("negcq", [128, NT * H], F32))
        stats = e(nc.sbuf_tensor("stats", [128, 64], F32))
        kn2 = e(nc.sbuf_tensor("kn2", [H, T], F32))
        qn2 = e(nc.sbuf_tensor("qn2", [H, T], F32))
        bks = e(nc.sbuf_tensor("bks", [H, 8], F32))
        gam2 = e(nc.sbuf_tensor("gam2", [128, 4], F32))
        scalev = e(nc.sbuf_tensor("scalev", [128, 1], F32))
        grow = e(nc.sbuf_tensor("grow", [4, 512], F32))

        GD = Sem(ctx, nc, "GD")
        CC = Sem(ctx, nc, "CC")
        MM = Sem(ctx, nc, "MM")
        VD = Sem(ctx, nc, "VD")
        SD = Sem(ctx, nc, "SD")
        ST = Sem(ctx, nc, "ST")
        LSM = Sem(ctx, nc, "LSM")
        LX = Sem(ctx, nc, "LX")
        LWM = Sem(ctx, nc, "LWM")
        LMOD = Sem(ctx, nc, "LMOD")
        STG = Sem(ctx, nc, "STG")

        block = e(nc.Block())

        @block.vector
        def _(v):
            v.memset(scalev[:], SCALE)

        sA = ExitStack()
        eA = sA.enter_context
        h1f = eA(nc.sbuf_tensor("h1f", [128, NC * 1024], BF16))
        cosm = eA(nc.sbuf_tensor("sb_cosm", [128, T], F32))
        sinm = eA(nc.sbuf_tensor("sb_sinm", [128, T], F32))
        sB = ExitStack()
        eB = sB.enter_context
        xbuf = eB(nc.sbuf_tensor("xbuf", [128, 4 * D], F32))
        h1t = eB(nc.sbuf_tensor("h1t", [128, 4 * D], BF16))
        wmb = eB(nc.sbuf_tensor("wmb", [128, 2 * JSH], F32))
        scr = eB(nc.sbuf_tensor("scr", [128, D], F32))
        cm_sb = eB(nc.sbuf_tensor("cm_sb", [128, NC], F32))
        scm = eB(nc.sbuf_tensor("scm", [128, NC], F32))
        modsl = eB(nc.sbuf_tensor("modsl", [1, JSH], F32))

        @block.sync
        def _(sy):
            _dma(sy, LSM, cm_sb[:], cm_in)
            _dma(sy, LSM, cosm[:], cos_in)
            _dma(sy, LSM, sinm[:], sin_in)
            _dma(sy, LSM, bqm[:], bqm_in)
            _dma(sy, LSM, bkm[:], bkm_in)
            _dma(sy, LSM, bv4[:], bvrow_in)
            _dma(sy, LSM, bo4[:], borow_in)
            _dma(sy, LSM, fb4[:], fbrow_in)
            _dma(sy, LSM, eyeb[:], eyeb_in)
            _dma(sy, LSM, eyef[:], eyef_in)
            _dma(sy, LSM, onesf[:], onesf_in)
            _dma(sy, LSM, e4f[:], e4f_in)
            _dma(sy, LSM, ehb[:], ehb_in)
        ld_p0 = LSM.n

        @block.sync
        def _(sy):
            for i in range(4):
                _dma(sy, LX, xbuf[:, i * D:(i + 1) * D],
                     x_in[i * 128:(i + 1) * 128, :])
        ld_xw = [LX.n]

        with nc.psum_tensor("pmod", [1, JSH], F32) as pmod:
            @block.scalar
            def _(sc):
                sc.wait_ge(LSM.h, ld_p0)
                SD.inc(sc.activation(scm[:], cm_sb[:], AF.Silu))
            sd_silu = SD.n

            ld_wq = []
            mm_wq = []
            for stp in range(8):
                @block.sync
                def _(sy, stp=stp):
                    if stp >= 1:
                        sy.wait_ge(MM.h, mm_wq[stp - 1])
                    for i in range(2):
                        _dma(sy, LWM, wmb[:, i * JSH:(i + 1) * JSH],
                             wmods_in[stp * 2 + i])
                ld_wq.append(LWM.n)

                @block.tensor
                def _(tn, stp=stp):
                    tn.wait_ge(LWM.h, ld_wq[stp])
                    if stp == 0:
                        tn.wait_ge(SD.h, sd_silu)
                    last = None
                    for i in range(2):
                        ic = stp * 2 + i
                        for jt in range(JSH // 512):
                            last = tn.matmul(pmod[:, jt * 512:(jt + 1) * 512],
                                             scm[:, ic:ic + 1],
                                             wmb[:, i * JSH + jt * 512:
                                                 i * JSH + (jt + 1) * 512],
                                             start=(ic == 0), stop=(ic == NC - 1))
                    MM.inc(last)
                mm_wq.append(MM.n)
            mm_mod = MM.n

            @block.scalar
            def _(sc):
                sc.wait_ge(MM.h, mm_mod)
                SD.inc(sc.copy(modsl[:], pmod[:]))
            sd_modsl = SD.n

        @block.gpsimd
        def _(gp):
            gp.wait_ge(SD.h, sd_modsl)
            _dma(gp, GD, mod_in.ap().rearrange("(a b) -> a b", a=1), modsl[:])
            gp.wait_ge(GD.h, GD.n)
            CC.inc(gp.collective_compute(
                "AllGather", ALU.bypass,
                replica_groups=[list(range(CORES))],
                ins=[mod_in.ap()], outs=[mod_g.ap()]))
        cc_mod = CC.n

        @block.sync
        def _(sy):
            sy.wait_ge(CC.h, cc_mod)
            for m in range(6):
                _dma(sy, LMOD, modr[m][:],
                     mod_g.ap()[m * D:(m + 1) * D].rearrange("(p b) -> p b", b=512))
        ld_mod = LMOD.n

        MSA, MCA, MGA, MSB, MCB, MGB = range(6)

        def mrow(m):
            return modr[m][:]

        @block.vector
        def _(v):
            v.wait_ge(LMOD.h, ld_mod)
            v.tensor_scalar_add(mrow(MCA), mrow(MCA), 1.0)
            v.tensor_scalar_add(mrow(MCB), mrow(MCB), 1.0)
            for mg in (MGA, MGB):
                g = mrow(mg)
                v.tensor_scalar(rs0[:], g, 0.0, None, ALU.is_ge)
                v.tensor_scalar(rs0[:], rs0[:], 2.0, -1.0, ALU.mult, ALU.add)
                v.tensor_scalar_mul(rs1[:], g, -1.0)
                v.tensor_max(rs1[:], rs1[:], g)
                v.tensor_scalar_max(rs1[:], rs1[:], 0.25)
                v.tensor_mul(g, rs0[:], rs1[:])
            VD.inc(v.tensor_mul(gabo4[:], mrow(MGA), bo4[:]))
        vd_mod = VD.n

        def bcast_row(row4_ap, dst, tag, wait_sems):
            with nc.psum_tensor("pb_" + tag, [128, D], F32) as pb:
                prev_sd = SD.n

                @block.tensor
                def _(tn):
                    tn.wait_ge(SD.h, prev_sd)
                    for sem, val in wait_sems:
                        tn.wait_ge(sem.h, val)
                    last = None
                    for jt in range(4):
                        last = tn.matmul(pb[:, jt * 512:(jt + 1) * 512],
                                         e4f[:, jt * 128:(jt + 1) * 128],
                                         row4_ap[:], start=True, stop=True)
                    MM.inc(last)
                mmv = MM.n

                @block.scalar
                def _(sc):
                    sc.wait_ge(MM.h, mmv)
                    SD.inc(sc.copy(dst[:], pb[:]))
            return SD.n

        sd_ca1 = bcast_row(mrow(MCA), slot[0], "ca1", [(VD, vd_mod)])
        sd_sa = bcast_row(mrow(MSA), slot[1], "sa", [(VD, vd_mod)])

        vd_h1w = []
        dbg_st_gate = []
        for w in range(2):
            cs = range(w * 4, w * 4 + 4)

            if w == 1:
                @block.sync
                def _(sy):
                    sy.wait_ge(VD.h, vd_h1w[0])
                    for i in range(4):
                        _dma(sy, LX, xbuf[:, i * D:(i + 1) * D],
                             x_in[(4 + i) * 128:(5 + i) * 128, :])
                ld_xw.append(LX.n)

            @block.vector
            def _(v, w=w):
                v.wait_ge(LX.h, ld_xw[w])
                if w == 1 and dbg_st_gate:
                    v.wait_ge(ST.h, dbg_st_gate[-1])
                last = None
                for i in cs:
                    xc = xbuf[:, (i % 4) * D:((i % 4) + 1) * D]
                    last = v.reduce_sum(stats[:, i:i + 1], xc, axis=AXX)
                VD.inc(last)
            vd_s1 = VD.n

            @block.scalar
            def _(sc, w=w):
                sc.wait_ge(LX.h, ld_xw[w])
                last = None
                for i in cs:
                    xc = xbuf[:, (i % 4) * D:((i % 4) + 1) * D]
                    last = sc.activation(scr[:], xc, AF.Square,
                                         accum_out=stats[:, 8 + i:9 + i])
                SD.inc(last)
            sd_s2 = SD.n

            a, b = w * 4, w * 4 + 4

            @block.vector
            def _(v):
                v.wait_ge(VD.h, vd_s1)
                v.wait_ge(SD.h, sd_s2)
                v.tensor_scalar_mul(stats[:, 16 + a:16 + b], stats[:, a:b], 1.0 / D)
                v.tensor_scalar_mul(stats[:, 24 + a:24 + b], stats[:, 8 + a:8 + b], 1.0 / D)
                v.drain()
                v.tensor_mul(stats[:, 32 + a:32 + b], stats[:, 16 + a:16 + b],
                             stats[:, 16 + a:16 + b])
                v.drain()
                v.tensor_sub(stats[:, 24 + a:24 + b], stats[:, 24 + a:24 + b],
                             stats[:, 32 + a:32 + b])
                v.drain()
                VD.inc(v.tensor_scalar_add(stats[:, 24 + a:24 + b],
                                           stats[:, 24 + a:24 + b], 1e-5))
            vd_var = VD.n

            @block.scalar
            def _(sc):
                sc.wait_ge(VD.h, vd_var)
                SD.inc(sc.activation(stats[:, 32 + a:32 + b],
                                     stats[:, 24 + a:24 + b], AF.Sqrt))
            sd_sqrt = SD.n

            @block.vector
            def _(v):
                v.wait_ge(SD.h, sd_sqrt)
                v.reciprocal(stats[:, 40 + a:40 + b], stats[:, 32 + a:32 + b])
                v.drain()
                v.tensor_mul(stats[:, 48 + a:48 + b], stats[:, 16 + a:16 + b],
                             stats[:, 40 + a:40 + b])
                v.drain()
                VD.inc(v.tensor_scalar_mul(stats[:, 48 + a:48 + b],
                                           stats[:, 48 + a:48 + b], -1.0))
            vd_rstd = VD.n

            @block.scalar
            def _(sc):
                sc.wait_ge(VD.h, vd_rstd)
                last = None
                for i in cs:
                    last = sc.activation(xbuf[:, (i % 4) * D:((i % 4) + 1) * D],
                                         xbuf[:, (i % 4) * D:((i % 4) + 1) * D],
                                         AF.Identity,
                                         bias=stats[:, 48 + i:49 + i],
                                         scale=stats[:, 40 + i:41 + i])
                SD.inc(last)
            sd_u = SD.n

            d_h1 = dbg_out("h1", [T, D], BF16) if w == 0 else d_h1

            @block.vector
            def _(v):
                v.wait_ge(SD.h, sd_u)
                v.wait_ge(SD.h, max(sd_ca1, sd_sa))
                last = None
                for i in cs:
                    u = xbuf[:, (i % 4) * D:((i % 4) + 1) * D]
                    v.tensor_mul(u, u, slot[0][:])
                    last = v.tensor_add(h1t[:, (i % 4) * D:((i % 4) + 1) * D],
                                        u, slot[1][:])
                VD.inc(last)
            vd_h1w.append(VD.n)
            vd_h1wv = VD.n

            if d_h1 is not None:
                @block.sync
                def _(sy):
                    sy.wait_ge(VD.h, vd_h1wv)
                    for i in cs:
                        _dma(sy, ST, d_h1[i * 128:(i + 1) * 128, :],
                             h1t[:, (i % 4) * D:((i % 4) + 1) * D])
                dbg_st_gate.append(ST.n)

            with nc.psum_tensor(f"pt1a{w}", [128, 512], BF16) as pt1a, \
                 nc.psum_tensor(f"pt1b{w}", [128, 512], BF16) as pt1b:
                pts = [pt1a, pt1b]

                mm_t1base = MM.n

                @block.tensor
                def _(tn):
                    tn.wait_ge(VD.h, vd_h1wv)
                    for fc in range(NC):
                        if fc >= 2:
                            tn.wait_ge(VD.h, vd_h1wv + fc - 1)
                        p = pts[fc % 2]
                        last = None
                        for i in range(4):
                            last = tn.transpose(p[:, i * 128:(i + 1) * 128],
                                                h1t[:, i * D + fc * 128:
                                                    i * D + (fc + 1) * 128],
                                                eyeb[:])
                        MM.inc(last)
                mm_t1 = MM.n

                @block.vector
                def _(v):
                    for fc in range(NC):
                        v.wait_ge(MM.h, mm_t1base + fc + 1)
                        VD.inc(v.tensor_copy(
                            h1f[:, fc * T + w * 512: fc * T + (w + 1) * 512],
                            pts[fc % 2][:]))
                vd_h1w[-1] = VD.n

        vd_t1 = VD.n

        d_stats = dbg_out("stats", [128, 64], F32)
        if d_stats is not None:
            @block.sync
            def _(sy):
                sy.wait_ge(VD.h, vd_t1)
                _dma(sy, ST, d_stats, stats[:])

        d_slot0 = dbg_out("slot0", [128, D], F32)
        if d_slot0 is not None:
            @block.sync
            def _(sy):
                sy.wait_ge(VD.h, vd_t1)
                _dma(sy, ST, d_slot0, slot[0][:])

        d_slot1 = dbg_out("slot1", [128, D], F32)
        if d_slot1 is not None:
            @block.sync
            def _(sy):
                sy.wait_ge(VD.h, vd_t1)
                _dma(sy, ST, d_slot1, slot[1][:])

        sB.close()

        sC = ExitStack()
        eC = sC.enter_context
        wkb = eC(nc.sbuf_tensor("wkb", [128, 2 * D], BF16))
        bvb = eC(nc.sbuf_tensor("bvb", [128, D], F32))
        qf = eC(nc.sbuf_tensor("qf", [128, T], F32))
        rot = eC(nc.sbuf_tensor("rot", [128, T], F32))
        ta_b = eC(nc.sbuf_tensor("ta_b", [128, T], F32))
        kbf = eC(nc.sbuf_tensor("kbf", [128, T], BF16))
        k2b = eC(nc.sbuf_tensor("k2b", [128, T], BF16))
        knb = eC(nc.sbuf_tensor("knb", [16, 1], BF16))
        sCV = ExitStack()
        eCV = sCV.enter_context
        wvb = eCV(nc.sbuf_tensor("wvb", [128, NC * D], BF16))
        vtm = eCV(nc.sbuf_tensor("vtm", [128, D], BF16))

        LWK = Sem(ctx, nc, "LWK")
        LWV = Sem(ctx, nc, "LWV")
        TSK = Sem(ctx, nc, "TSK")
        SKE = Sem(ctx, nc, "SKE")
        VKR = Sem(ctx, nc, "VKR")
        TKN = Sem(ctx, nc, "TKN")
        TV = Sem(ctx, nc, "TV")
        VV = Sem(ctx, nc, "VV")

        LWKP = [Sem(ctx, nc, "LWKP0"), Sem(ctx, nc, "LWKP1")]

        @block.gpsimd
        def _(gp):
            gp.wait_ge(VD.h, vd_t1)
            gp.wait_ge(MM.h, mm_mod)

        sd_bv = bcast_row(bv4[:], bvb, "bv", [(LSM, ld_p0), (VD, vd_t1),
                                              (MM, mm_mod)])

        def kq_phase(w4_in, bias_m, stage_region, tag, qn_psum_dst, norm_sem_t):
            tsk0, ske0, vkr0 = TSK.n, SKE.n, VKR.n
            nrm0 = norm_sem_t.n
            gd0 = GD.n
            lwk0 = [LWKP[0].n, LWKP[1].n]
            with nc.psum_tensor("pqk_" + tag, [128, 2 * T], F32) as pqk:
                @block.gpsimd
                def _(gp):
                    for oc in range(NC):
                        if oc >= 2:
                            gp.wait_ge(TSK.h, tsk0 + oc - 1)
                        _dma(gp, LWKP[oc % 2],
                             wkb[:, (oc % 2) * D:(oc % 2 + 1) * D],
                             w4_in[oc])
                        if oc >= 1:
                            gp.wait_ge(VKR.h, vkr0 + oc)
                            _dma(gp, GD, stage_region(oc - 1), kbf[:])
                    gp.wait_ge(VKR.h, vkr0 + NC)
                    _dma(gp, GD, stage_region(NC - 1), kbf[:])

                def knorm(tn, oc):
                    tn.wait_ge(VKR.h, vkr0 + oc + 1)
                    last = None
                    for tt in range(2):
                        last = tn.matmul(
                            qn_psum_dst[:, tt * 512:(tt + 1) * 512],
                            ehb[:, oc * H:(oc + 1) * H],
                            k2b[:, tt * 512:(tt + 1) * 512],
                            start=(oc == 0), stop=(oc == NC - 1))
                    norm_sem_t.inc(last)

                @block.tensor
                def _(tn):
                    for oc in range(NC):
                        tn.wait_ge(LWKP[oc % 2].h,
                                   lwk0[oc % 2] + 16 * (oc // 2 + 1))
                        tn.wait_ge(SKE.h, max(ske0 + oc - 1, ske0))
                        tn.wait_ge(SD.h, SD.n)
                        last = None
                        for tt in range(2):
                            for ic in range(NC):
                                last = tn.matmul(
                                    pqk[:, (oc % 2) * T + tt * 512:
                                        (oc % 2) * T + (tt + 1) * 512],
                                    wkb[:, (oc % 2) * D + ic * 128:
                                        (oc % 2) * D + (ic + 1) * 128],
                                    h1f[:, ic * T + tt * 512:
                                        ic * T + (tt + 1) * 512],
                                    start=(ic == 0), stop=(ic == NC - 1))
                        TSK.inc(last)
                        if oc >= 2:
                            knorm(tn, oc - 2)
                    knorm(tn, NC - 2)
                    knorm(tn, NC - 1)

                @block.scalar
                def _(sc):
                    for oc in range(NC):
                        sc.wait_ge(TSK.h, tsk0 + oc + 1)
                        if oc >= 1:
                            sc.wait_ge(VKR.h, vkr0 + oc)
                        SKE.inc(sc.activation(
                            qf[:], pqk[:, (oc % 2) * T:(oc % 2 + 1) * T],
                            AF.Identity, bias=bias_m[:, oc:oc + 1]))

                @block.vector
                def _(v):
                    for oc in range(NC):
                        v.wait_ge(SKE.h, ske0 + oc + 1)
                        v.wait_ge(GD.h, gd0 + 16 * oc)
                        if oc >= 1:
                            v.wait_ge(norm_sem_t.h, nrm0 + oc)
                        v.tensor_scalar_mul(rot[0:64, :], qf[64:128, :], -1.0)
                        v.tensor_copy(rot[64:128, :], qf[0:64, :])
                        v.tensor_mul(ta_b[:], qf[:], cosm[:])
                        v.tensor_mul(rot[:], rot[:], sinm[:])
                        v.tensor_add(kbf[:], ta_b[:], rot[:])
                        VKR.inc(v.tensor_mul(k2b[:], kbf[:], kbf[:]))
            return tsk0, ske0, vkr0

        with nc.psum_tensor("pkn", [16, T], F32) as pkn:
            kq_phase_gd0 = GD.n
            tsk0, ske0, vkr0 = kq_phase(
                wk4, bkm,
                lambda oc: kv_in.ap()[oc * 131072:(oc + 1) * 131072]
                .rearrange("(p f) -> p f", f=T),
                "k", pkn, TKN)

            @block.scalar
            def _(sc):
                sc.wait_ge(TKN.h, TKN.n)
                SKE.inc(sc.copy(kn2[:], pkn[:]))
            ske_kn = SKE.n

        @block.vector
        def _(v):
            v.wait_ge(SKE.h, ske_kn)
            v.reduce_max(bks[:, 0:1], kn2[:], axis=AXX)
            v.drain()
            VKR.inc(v.tensor_copy(knb[:], bks[:, 0:1]))
        vkr_kn = VKR.n

        @block.gpsimd
        def _(gp):
            gp.wait_ge(VKR.h, vkr_kn)
            _dma(gp, GD, kv_in.ap()[KSZ + VSZ:KSZ + VSZ + H]
                 .rearrange("(a b) -> a b", a=16), knb[:])

        @block.gpsimd
        def _(gp):
            for ic in range(NC):
                _dma(gp, LWV, wvb[:, ic * D:(ic + 1) * D], wv_r[ic])
        ld_wv = LWV.n

        vr = kv_in.ap()[KSZ:KSZ + VSZ].rearrange("(h p c d) -> h p c d",
                                                 h=H, p=128, c=NT)
        with nc.psum_tensor("pv", [128, 2 * D], F32) as pv:
            tv0, vv0 = TV.n, VV.n
            gdv0 = GD.n

            @block.tensor
            def _(tn):
                tn.wait_ge(LWV.h, ld_wv)
                for tc in range(NT):
                    if tc >= 2:
                        tn.wait_ge(VV.h, vv0 + tc - 1)
                    last = None
                    for ic in range(NC):
                        for ot in range(4):
                            last = tn.matmul(
                                pv[:, (tc % 2) * D + ot * 512:
                                   (tc % 2) * D + (ot + 1) * 512],
                                h1f[:, ic * T + tc * 128:ic * T + (tc + 1) * 128],
                                wvb[:, ic * D + ot * 512:ic * D + (ot + 1) * 512],
                                start=(ic == 0), stop=(ic == NC - 1))
                    TV.inc(last)

            @block.vector
            def _(v):
                v.wait_ge(SD.h, sd_bv)
                for tc in range(NT):
                    v.wait_ge(TV.h, tv0 + tc + 1)
                    if tc >= 1:
                        v.wait_ge(GD.h, gdv0 + 16 * tc)
                    VV.inc(v.tensor_add(vtm[:], pv[:, (tc % 2) * D:
                                                   (tc % 2 + 1) * D],
                                        bvb[:]))

            @block.gpsimd
            def _(gp):
                for tc in range(NT):
                    gp.wait_ge(VV.h, vv0 + tc + 1)
                    _dma(gp, GD, vr[:, :, tc, :].transpose([1, 0, 2]),
                         vtm[:].rearrange("p (h d) -> p h d", h=H))

        @block.gpsimd
        def _(gp):
            gp.wait_ge(GD.h, GD.n)
            CC.inc(gp.collective_compute(
                "AllGather", ALU.bypass,
                replica_groups=[list(range(CORES))],
                ins=[kv_in.ap()], outs=[kv_g.ap()]))
        cc_kv = CC.n
        sCV.close()

        LFW = Sem(ctx, nc, "LFW")
        STN = Sem(ctx, nc, "STN")
        SVN = Sem(ctx, nc, "SVN")
        SSN = Sem(ctx, nc, "SSN")
        fwsb = eC(nc.sbuf_tensor("fwsb", [128, 2 * D], F32))
        tscr = eC(nc.sbuf_tensor("tscr", [128, D], F32))
        trat = eC(nc.sbuf_tensor("trat", [128, D], F32))
        tgab = eC(nc.sbuf_tensor("tgab", [128, D], BF16))
        tgbb = eC(nc.sbuf_tensor("tgbb", [128, D], BF16))
        wtb = eC(nc.sbuf_tensor("wtb", [128, 2 * D], BF16))
        ttr = eC(nc.sbuf_tensor("ttr", [128, NSH], BF16))

        gd_p5 = GD.n

        @block.sync
        def _(sy):
            sy.wait_ge(TV.h, TV.n)
            sy.wait_ge(VV.h, VV.n)
            sy.wait_ge(GD.h, gd_p5)
            for ch in range(2):
                _dma(sy, LFW, fwsb[:, ch * D:(ch + 1) * D], fws_in[ch])
        ld_fw = LFW.n

        @block.scalar
        def _(sc):
            sc.wait_ge(LFW.h, ld_fw)
            last = None
            for ch in range(2):
                last = sc.activation(tscr[:], fwsb[:, ch * D:(ch + 1) * D],
                                     AF.Abs, accum_out=gam2[:, ch:ch + 1])
            SSN.inc(last)
        ssn_g = SSN.n

        @block.vector
        def _(v):
            v.wait_ge(SSN.h, ssn_g)
            v.tensor_scalar_mul(gam2[:, 0:2], gam2[:, 0:2], 1.0 / D)
            v.drain()
            v.tensor_scalar_add(gam2[:, 2:4], gam2[:, 0:2], 1e-8)
            v.drain()
            v.reciprocal(gam2[:, 2:4], gam2[:, 2:4])
            v.drain()
            last = None
            for ch in range(2):
                v.tensor_scalar_mul(trat[:], fwsb[:, ch * D:(ch + 1) * D],
                                    gam2[:, 2 + ch:3 + ch])
                v.tensor_scalar(tgab[:], trat[:], 0.5, None, ALU.is_ge)
                v.tensor_scalar(tgbb[:], trat[:], -1.0, 0.5, ALU.mult, ALU.is_ge)
                last = v.tensor_sub(wtb[:, ch * D:(ch + 1) * D], tgab[:],
                                    tgbb[:])
            SVN.inc(last)
        svn_t = SVN.n

        with nc.psum_tensor("ptn", [128, NSH], BF16) as ptn:
            gdt0 = GD.n
            @block.tensor
            def _(tn):
                tn.wait_ge(SVN.h, svn_t)
                for kc in range(NC):
                    if kc >= 1:
                        tn.wait_ge(SVN.h, svn_t + kc)
                    last = None
                    for ch in range(2):
                        last = tn.transpose(
                            ptn[:, ch * 128:(ch + 1) * 128],
                            wtb[:, ch * D + kc * 128:ch * D + (kc + 1) * 128],
                            eyeb[:])
                    STN.inc(last)
            stn0 = STN.n - NC

            @block.vector
            def _(v):
                for kc in range(NC):
                    v.wait_ge(STN.h, stn0 + kc + 1)
                    if kc >= 1:
                        v.wait_ge(GD.h, gdt0 + 16 * kc)
                    SVN.inc(v.tensor_copy(ttr[:], ptn[:]))

            @block.gpsimd
            def _(gp):
                for kc in range(NC):
                    gp.wait_ge(SVN.h, svn_t + kc + 1)
                    _dma(gp, GD, tern_in[kc], ttr[:])
                gp.wait_ge(SVN.h, svn_t)
                gp.dma_start(out=gam_in.ap().rearrange("(c p) -> p c", p=128),
                             in_=gam2[:, 0:2],
                             allow_slow_non_contiguous=True).then_inc(GD.h, 16)
                GD.n += 16
                gp.wait_ge(GD.h, GD.n)
                CC.inc(gp.collective_compute(
                    "AllGather", ALU.bypass,
                    replica_groups=[list(range(CORES))],
                    ins=[gam_in.ap()], outs=[gam_g.ap()]))
                CC.inc(gp.collective_compute(
                    "AllGather", ALU.bypass,
                    replica_groups=[list(range(CORES))],
                    ins=[tern_in.ap()], outs=[tern_g.ap()]))
        cc_all = CC.n

        with nc.psum_tensor("pqn", [16, T], F32) as pqn:
            TQN = Sem(ctx, nc, "TQN")
            tskq0, skeq0, vkrq0 = kq_phase(
                wq4, bqm,
                lambda oc: q_st.ap()[oc],
                "q", pqn, TQN)

            @block.scalar
            def _(sc):
                sc.wait_ge(TQN.h, TQN.n)
                SKE.inc(sc.copy(qn2[:], pqn[:]))
            ske_qn = SKE.n

        LKN = Sem(ctx, nc, "LKN")
        knall = eC(nc.sbuf_tensor("knall", [16, 8], BF16))
        qn2t = eC(nc.sbuf_tensor("qn2t", [128, 128], F32))
        bksrow = eC(nc.sbuf_tensor("bksrow", [1, 16], F32))
        bksb = eC(nc.sbuf_tensor("bksb", [128, 16], F32))

        @block.sync
        def _(sy):
            sy.wait_ge(CC.h, cc_kv)
            kn_ap = kv_g.ap().rearrange("(r e) -> e r", r=CORES)[
                KSZ + VSZ:KSZ + VSZ + H, :]
            sy.dma_start(out=knall[:], in_=kn_ap,
                         allow_slow_non_contiguous=True).then_inc(LKN.h, 16)
            LKN.n += 16
        ld_kn = LKN.n

        @block.vector
        def _(v):
            v.wait_ge(LKN.h, ld_kn)
            VKR.inc(v.reduce_max(bks[:, 1:2], knall[:], axis=AXX))
        vkr_bk = VKR.n

        @block.scalar
        def _(sc):
            sc.wait_ge(VKR.h, vkr_bk)
            SKE.inc(sc.activation(bks[:, 2:3], bks[:, 1:2], AF.Sqrt))
        ske_bk = SKE.n

        @block.vector
        def _(v):
            v.wait_ge(SKE.h, ske_bk)
            VKR.inc(v.tensor_scalar_mul(bks[:, 3:4], bks[:, 2:3], SCALE))
        vkr_bk2 = VKR.n

        with nc.psum_tensor("pnq", [128, 128], F32) as pnq, \
             nc.psum_tensor("pbr", [1, 16], F32) as pbr, \
             nc.psum_tensor("pbb", [128, 16], F32) as pbb:
            TNC = Sem(ctx, nc, "TNC")

            @block.tensor
            def _(tn):
                tn.wait_ge(SKE.h, ske_qn)
                last = None
                for qt in range(NT):
                    last = tn.transpose(pnq[:, qt * 16:(qt + 1) * 16],
                                        qn2[:, qt * 128:(qt + 1) * 128],
                                        eyef[0:16, 0:16])
                TNC.inc(last)
                tn.wait_ge(VKR.h, vkr_bk2)
                TNC.inc(tn.transpose(pbr[:], bks[:, 3:4],
                                     eyef[0:16, 0:16]))
            tnc_t = TNC.n

            @block.scalar
            def _(sc):
                sc.wait_ge(TNC.h, tnc_t)
                sc.copy(qn2t[:], pnq[:])
                SKE.inc(sc.copy(bksrow[:], pbr[:]))
            ske_nt = SKE.n

            @block.tensor
            def _(tn):
                tn.wait_ge(SKE.h, ske_nt)
                TNC.inc(tn.matmul(pbb[:], onesf[:], bksrow[:],
                                  start=True, stop=True))
            tnc_bb = TNC.n

            @block.scalar
            def _(sc):
                sc.wait_ge(TNC.h, tnc_bb)
                sc.copy(bksb[:], pbb[:])
                SKE.inc(sc.activation(qn2t[:], qn2t[:], AF.Sqrt))
            ske_sq = SKE.n

            @block.vector
            def _(v):
                v.wait_ge(SKE.h, ske_sq)
                last = None
                for qt in range(NT):
                    last = v.scalar_tensor_tensor(
                        negcq[:, qt * H:(qt + 1) * H],
                        qn2t[:, qt * H:(qt + 1) * H], -1.0, bksb[:],
                        ALU.mult, ALU.mult)
                VKR.inc(last)
            vkr_ncq = VKR.n

        d_kv = dbg_out("kv", [KVN], BF16)
        if d_kv is not None:
            @block.sync
            def _(sy):
                sy.wait_ge(CC.h, cc_kv)
                _dma(sy, ST, d_kv, kv_in.ap())
        d_q = dbg_out("q", [H, 128, T], BF16)
        if d_q is not None:
            @block.sync
            def _(sy):
                sy.wait_ge(SKE.h, ske_qn)
                for hh in range(H):
                    _dma(sy, ST, d_q[hh], q_st.ap()[hh])
        d_tern = dbg_out("tern", [NC, 128, NSH], BF16)
        if d_tern is not None:
            @block.sync
            def _(sy):
                sy.wait_ge(CC.h, cc_all)
                for kc in range(NC):
                    _dma(sy, ST, d_tern[kc], tern_in[kc])
        d_gam = dbg_out("gam", [NSH], F32)
        if d_gam is not None:
            @block.sync
            def _(sy):
                sy.wait_ge(CC.h, cc_all)
                _dma(sy, ST, d_gam, gam_in.ap())
        d_ncq = dbg_out("ncq", [128, NT * H], F32)
        if d_ncq is not None:
            @block.sync
            def _(sy):
                sy.wait_ge(VKR.h, vkr_ncq)
                _dma(sy, ST, d_ncq, negcq[:])

        sC.close()
        sA.close()

        sD1 = ExitStack()
        eD1 = sD1.enter_context
        O_fm = eD1(nc.sbuf_tensor("O_fm", [128, H * T], BF16, side="right"))
        sD2 = ExitStack()
        eD2 = sD2.enter_context
        kh = [eD2(nc.sbuf_tensor(f"kh{i}", [128, S], BF16)) for i in range(2)]
        vh = [eD2(nc.sbuf_tensor(f"vh{i}", [128, S], BF16)) for i in range(2)]
        qh = [eD2(nc.sbuf_tensor(f"qh{i}", [128, T], BF16)) for i in range(2)]
        pbuf = eD2(nc.sbuf_tensor("pbuf", [128, 4 * 4096], BF16))
        ptb = [eD2(nc.sbuf_tensor(f"ptb{i}", [128, 512], BF16)) for i in range(2)]
        dacc = eD2(nc.sbuf_tensor("dacc", [128, 32], F32))
        dent = eD2(nc.sbuf_tensor("dent", [128, 8], F32))
        denrow = eD2(nc.sbuf_tensor("denrow", [4, 128], F32))
        rbs = eD2(nc.sbuf_tensor("rbs", [128, 512], F32))

        LKV = [Sem(ctx, nc, "LKV0"), Sem(ctx, nc, "LKV1")]
        ATS = Sem(ctx, nc, "ATS")
        AEX = Sem(ctx, nc, "AEX")
        ATT = Sem(ctx, nc, "ATT")
        ATE = Sem(ctx, nc, "ATE")
        APV = Sem(ctx, nc, "APV")
        ADN = Sem(ctx, nc, "ADN")
        TDT = Sem(ctx, nc, "TDT")
        SDR = Sem(ctx, nc, "SDR")
        TRB = Sem(ctx, nc, "TRB")
        ARB = Sem(ctx, nc, "ARB")
        AOE = Sem(ctx, nc, "AOE")

        gates = [(VKR, VKR.n), (SKE, SKE.n), (TQN, TQN.n), (SVN, SVN.n),
                 (STN, STN.n), (GD, GD.n), (CC, cc_kv)]
        gd_attn = GD.n

        @block.sync
        def _(sy):
            for sem, val in gates:
                sy.wait_ge(sem.h, val)
            for h in range(H):
                if h >= 2:
                    sy.wait_ge(APV.h, 128 * (h - 1))
                p = h % 2
                for r in range(CORES):
                    _dma(sy, LKV[p], kh[p][:, r * T:(r + 1) * T],
                         kv_g.ap()[r * KVN + h * 131072:
                                   r * KVN + (h + 1) * 131072]
                         .rearrange("(p f) -> p f", f=T))
                    _dma(sy, LKV[p], vh[p][:, r * T:(r + 1) * T],
                         kv_g.ap()[r * KVN + KSZ + h * 131072:
                                   r * KVN + KSZ + (h + 1) * 131072]
                         .rearrange("(p f) -> p f", f=T))
                _dma(sy, LKV[p], qh[p][:], q_st.ap()[h])

        with nc.psum_tensor("s2p", [128, 2 * T], F32) as s2p, \
             nc.psum_tensor("tpA", [128, T], BF16) as tpA, \
             nc.psum_tensor("tpB", [128, T], BF16) as tpB, \
             nc.psum_tensor("o_ps", [128, 512], F32) as o_ps, \
             nc.psum_tensor("pxtr", [128, 512], F32) as pxtr:
            tps = [tpA, tpB]

            NQG = 2 * H

            @block.tensor
            def _(tn):
                for h in range(H):
                    p = h % 2
                    tn.wait_ge(LKV[p].h, 272 * (h // 2 + 1))
                    for qg in range(2):
                        u = h * 2 + qg
                        ats0, aex0 = 32 * u, 32 * u
                        att0, ate0, apv0 = 64 * u, 64 * u, 64 * u

                        def tpv_batch(ksp):
                            for kt in range(8 * ksp, 8 * ksp + 8):
                                if kt == 8 * ksp:
                                    tn.wait_ge(AEX.h, aex0 + 4 * ksp + 4)
                                if ate0 + kt - 1 > 0:
                                    tn.wait_ge(ATE.h, ate0 + kt - 1)
                                last = None
                                for qt in range(4):
                                    last = tn.transpose(
                                        tps[kt % 2][:, qt * 128:(qt + 1) * 128],
                                        pbuf[:, qt * 4096 + (kt % 32) * 128:
                                             qt * 4096 + (kt % 32 + 1) * 128],
                                        eyeb[:])
                                ATT.inc(last)
                                pv_kt = kt - 1
                                if pv_kt >= 0:
                                    emit_pv(pv_kt)

                        def emit_pv(kt):
                            if kt == 0:
                                tn.wait_ge(AOE.h, u)
                            tn.wait_ge(ATE.h, ate0 + kt + 1)
                            APV.inc(tn.matmul(
                                o_ps[:], vh[p][:, kt * 128:(kt + 1) * 128],
                                ptb[kt % 2][:], start=(kt == 0),
                                stop=(kt == 63)))

                        for ks in range(8):
                            for qt in range(4):
                                ui = ks * 4 + qt
                                if aex0 + ui - 1 > 0:
                                    tn.wait_ge(AEX.h, aex0 + ui - 1)
                                last = None
                                for j in range(2):
                                    last = tn.matmul(
                                        s2p[:, (ui % 2) * T + j * 512:
                                            (ui % 2) * T + (j + 1) * 512],
                                        qh[p][:, qg * 512 + qt * 128:
                                              qg * 512 + (qt + 1) * 128],
                                        kh[p][:, ks * T + j * 512:
                                              ks * T + (j + 1) * 512],
                                        start=True, stop=True)
                                ATS.inc(last)
                            if ks >= 2:
                                tpv_batch(ks - 2)
                        tpv_batch(6)
                        tpv_batch(7)
                        emit_pv(63)
                        tn.wait_ge(ADN.h, u + 1)
                        TDT.inc(tn.transpose(pxtr[0:4, 0:128], dent[:, 4:8],
                                             eyef[:]))
                        tn.wait_ge(SDR.h, u + 1)
                        last = None
                        for qt in range(4):
                            last = tn.matmul(
                                pxtr[:, qt * 128:(qt + 1) * 128],
                                e4f[:, qt * 128:(qt + 1) * 128],
                                denrow[:], start=True, stop=True)
                        TRB.inc(last)

            @block.scalar
            def _(sc):
                sc.wait_ge(VKR.h, vkr_ncq)
                for h in range(H):
                    for qg in range(2):
                        u = h * 2 + qg
                        ats0, aex0, att0 = 32 * u, 32 * u, 64 * u
                        sc.wait_ge(ADN.h, u)
                        for ks in range(8):
                            for qt in range(4):
                                ui = ks * 4 + qt
                                sc.wait_ge(ATS.h, ats0 + ui + 1)
                                if ks >= 4 and qt == 0:
                                    sc.wait_ge(ATT.h, att0 + 8 * (ks - 3))
                                AEX.inc(sc.activation(
                                    pbuf[:, qt * 4096 + (ks % 4) * 1024:
                                         qt * 4096 + (ks % 4 + 1) * 1024],
                                    s2p[:, (ui % 2) * T:(ui % 2 + 1) * T],
                                    AF.Exp, scale=scalev[:, 0:1],
                                    bias=negcq[:, (qg * 4 + qt) * H + h:
                                               (qg * 4 + qt) * H + h + 1],
                                    accum_out=dacc[:, qt * 8 + ks:
                                                   qt * 8 + ks + 1]))
                        sc.wait_ge(TDT.h, u + 1)
                        SDR.inc(sc.copy(denrow[:], pxtr[0:4, 0:128]))
                        sc.wait_ge(TRB.h, u + 1)
                        ARB.inc(sc.copy(rbs[:], pxtr[:]))

            @block.vector
            def _(v):
                for h in range(H):
                    for qg in range(2):
                        u = h * 2 + qg
                        att0, ate0, apv0 = 64 * u, 64 * u, 64 * u
                        for kt in range(64):
                            v.wait_ge(ATT.h, att0 + kt + 1)
                            if apv0 + kt - 1 > 0:
                                v.wait_ge(APV.h, apv0 + kt - 1)
                            ATE.inc(v.tensor_copy(ptb[kt % 2][:],
                                                  tps[kt % 2][:, 0:512]))
                        v.wait_ge(AEX.h, 32 * u + 32)
                        if u >= 1:
                            v.wait_ge(TRB.h, u)
                        for qt in range(4):
                            v.reduce_sum(dent[:, qt:qt + 1],
                                         dacc[:, qt * 8:(qt + 1) * 8],
                                         axis=AXX)
                        v.drain()
                        ADN.inc(v.reciprocal(dent[:, 4:8], dent[:, 0:4]))
                        v.wait_ge(APV.h, apv0 + 64)
                        v.wait_ge(ARB.h, u + 1)
                        AOE.inc(v.tensor_mul(
                            O_fm[:, h * T + qg * 512:h * T + (qg + 1) * 512],
                            o_ps[:], rbs[:]))

        d_O = dbg_out("O", [128, H * T], BF16)
        if d_O is not None:
            @block.sync
            def _(sy):
                sy.wait_ge(AOE.h, AOE.n)
                _dma(sy, ST, d_O, O_fm[:])
        aoe_end = AOE.n
        sD2.close()

        x2 = e(nc.sbuf_tensor("x2", [128, NT * D], F32))
        sD3 = ExitStack()
        eD3 = sD3.enter_context
        attn_tm = eD3(nc.sbuf_tensor("attn_tm", [128, NT * D], BF16))
        wob = eD3(nc.sbuf_tensor("wob", [128, 2 * D], BF16))
        afm = eD3(nc.sbuf_tensor("afm", [128, T], BF16))
        xrb = eD3(nc.sbuf_tensor("xrb", [128, D], F32))

        LWO = [Sem(ctx, nc, "LWO0"), Sem(ctx, nc, "LWO1")]
        LXR = Sem(ctx, nc, "LXR")
        TWO = Sem(ctx, nc, "TWO")
        SWO = Sem(ctx, nc, "SWO")
        TWT = Sem(ctx, nc, "TWT")
        VWO = Sem(ctx, nc, "VWO")
        VX2 = Sem(ctx, nc, "VX2")

        sd_gab = bcast_row(mrow(MGA), slot[0], "gab", [(VD, vd_mod), (AOE, aoe_end)])
        sd_gabo = bcast_row(gabo4[:], slot[1], "gabo", [(VD, vd_mod), (AOE, aoe_end)])

        @block.gpsimd
        def _(gp):
            gp.wait_ge(AOE.h, aoe_end)
            for fc in range(NC):
                if fc >= 2:
                    gp.wait_ge(TWO.h, fc - 1)
                _dma(gp, LWO[fc % 2], wob[:, (fc % 2) * D:(fc % 2 + 1) * D],
                     wo4[fc])

        @block.sync
        def _(sy):
            sy.wait_ge(AOE.h, aoe_end)
            for tt in range(NT):
                if tt >= 1:
                    sy.wait_ge(VX2.h, tt)
                _dma(sy, LXR, xrb[:], x_in[tt * 128:(tt + 1) * 128, :])

        with nc.psum_tensor("pwo", [128, 2 * T], F32) as pwo, \
             nc.psum_tensor("ptw", [128, 2 * T], BF16) as ptw:
            def emit_wot(tn, fc):
                tn.wait_ge(SWO.h, fc + 1)
                if fc - 1 > 0:
                    tn.wait_ge(VWO.h, fc - 1)
                last = None
                for tc in range(NT):
                    last = tn.transpose(
                        ptw[:, (fc % 2) * T + tc * 128:
                            (fc % 2) * T + (tc + 1) * 128],
                        afm[:, tc * 128:(tc + 1) * 128], eyeb[:])
                TWT.inc(last)

            @block.tensor
            def _(tn):
                for fc in range(NC):
                    tn.wait_ge(LWO[fc % 2].h, 16 * (fc // 2 + 1))
                    if fc < 2:
                        tn.wait_ge(SD.h, max(sd_gab, sd_gabo))
                    if fc >= 2:
                        tn.wait_ge(SWO.h, fc - 1)
                    last = None
                    for tt2 in range(2):
                        for dc in range(NC):
                            last = tn.matmul(
                                pwo[:, (fc % 2) * T + tt2 * 512:
                                    (fc % 2) * T + (tt2 + 1) * 512],
                                wob[:, (fc % 2) * D + dc * 128:
                                    (fc % 2) * D + (dc + 1) * 128],
                                O_fm[:, dc * T + tt2 * 512:
                                     dc * T + (tt2 + 1) * 512],
                                start=(dc == 0), stop=(dc == NC - 1))
                    TWO.inc(last)
                    if fc >= 1:
                        emit_wot(tn, fc - 1)
                emit_wot(tn, NC - 1)

            @block.scalar
            def _(sc):
                for fc in range(NC):
                    sc.wait_ge(TWO.h, fc + 1)
                    if fc >= 1:
                        sc.wait_ge(TWT.h, fc)
                    SWO.inc(sc.copy(afm[:], pwo[:, (fc % 2) * T:
                                                (fc % 2 + 1) * T]))

            @block.vector
            def _(v):
                for fc in range(NC):
                    v.wait_ge(TWT.h, fc + 1)
                    at_view = attn_tm[:].rearrange(
                        "p (c f) -> p c f", c=NT)[:, :, fc * 128:(fc + 1) * 128]
                    VWO.inc(v.tensor_copy(
                        at_view, ptw[:, (fc % 2) * T:(fc % 2 + 1) * T]
                        .rearrange("p (c f) -> p c f", c=NT)))

        @block.vector
        def _(v):
            v.wait_ge(SD.h, max(sd_gab, sd_gabo))
            v.wait_ge(VWO.h, VWO.n)
            for tt in range(NT):
                v.wait_ge(LXR.h, 16 * (tt + 1))
                x2c = x2[:, tt * D:(tt + 1) * D]
                v.tensor_mul(x2c, attn_tm[:, tt * D:(tt + 1) * D], slot[0][:])
                v.tensor_add(x2c, x2c, slot[1][:])
                VX2.inc(v.tensor_add(x2c, x2c, xrb[:]))

        d_x2 = dbg_out("x2", [T, D], F32)
        if d_x2 is not None:
            @block.sync
            def _(sy):
                sy.wait_ge(VX2.h, VX2.n)
                for i in range(NT):
                    _dma(sy, ST, d_x2[i * 128:(i + 1) * 128, :],
                         x2[:, i * D:(i + 1) * D])

        sD3.close()
        sD1.close()

        sD4 = ExitStack()
        eD4 = sD4.enter_context
        xqf = eD4(nc.sbuf_tensor("xqf", [128, NC * T], BF16))
        tern_sb = eD4(nc.sbuf_tensor("tern_sb", [128, NC * 512], BF16))
        h2s = eD4(nc.sbuf_tensor("h2s", [128, D], F32))
        u9 = eD4(nc.sbuf_tensor("u9", [128, T], F32))
        v9b = eD4(nc.sbuf_tensor("v9b", [128, T], F32))
        d9 = eD4(nc.sbuf_tensor("d9", [128, T], F32))
        xqt = eD4(nc.sbuf_tensor("xqt", [128, D], BF16))
        fout = eD4(nc.sbuf_tensor("fout", [128, 512], F32))
        gbgb = eD4(nc.sbuf_tensor("gbgb", [128, 512], F32))
        gbfbb = eD4(nc.sbuf_tensor("gbfbb", [128, 512], F32))

        DXQ = Sem(ctx, nc, "DXQ")
        d_xqall = dbg_out("xqall", [NT, 128, D], BF16)
        LGG = Sem(ctx, nc, "LGG")
        LTH = Sem(ctx, nc, "LTH")
        S9 = Sem(ctx, nc, "S9")
        V9 = Sem(ctx, nc, "V9")
        T9 = Sem(ctx, nc, "T9")
        TF = Sem(ctx, nc, "TF")
        SF = Sem(ctx, nc, "SF")
        VF = Sem(ctx, nc, "VF")

        @block.sync
        def _(sy):
            sy.wait_ge(CC.h, cc_all)
            _dma(sy, LGG, grow[:], gam_g.ap().rearrange("(p b) -> p b", b=512))
        ld_gg = LGG.n

        @block.vector
        def _(v):
            v.wait_ge(LGG.h, ld_gg)
            v.wait_ge(VX2.h, VX2.n)
            v.tensor_mul(gbg4[:], mrow(MGB), grow[:])
            VD.inc(v.tensor_mul(gbfb4[:], mrow(MGB), fb4[:]))
        vd_gb = VD.n

        sd_cb1 = bcast_row(mrow(MCB), slot[0], "cb1", [(VD, vd_gb)])
        sd_sb = bcast_row(mrow(MSB), slot[1], "sbb", [(VD, vd_gb)])


        def load_tern_q(q, gate_sem, gate_val):
            @block.sync
            def _(sy):
                sy.wait_ge(CC.h, cc_all)
                if gate_sem is not None:
                    sy.wait_ge(gate_sem.h, gate_val)
                for kc in range(NC):
                    for r in range(2):
                        _dma(sy, LTH, tern_sb[:, kc * 512 + r * NSH:
                                              kc * 512 + (r + 1) * NSH],
                             tern_g[q * 2 + r, kc])
            return LTH.n

        ld_th0 = load_tern_q(0, None, 0)

        v9_xq_marks = []
        @block.vector
        def _(v):
            v.wait_ge(SD.h, max(sd_cb1, sd_sb))
            V9.inc(v.tensor_copy(stats[:, 0:1], stats[:, 0:1]))
        v9_bc = V9.n

        with nc.psum_tensor("pxq", [128, D], BF16) as pxq, \
             nc.psum_tensor("pf", [128, 2 * 512], F32) as pf:
            for tt in range(NT):
                v9_prev = V9.n

                @block.scalar
                def _(sc, tt=tt):
                    if tt == 0:
                        sc.wait_ge(V9.h, v9_bc)
                    sc.wait_ge(VX2.h, tt + 1)
                    sc.wait_ge(V9.h, v9_prev)
                    x2c = x2[:, tt * D:(tt + 1) * D]
                    S9.inc(sc.activation(h2s[:], x2c, AF.Square,
                                         accum_out=stats[:, 1:2]))
                s9_sq = S9.n

                @block.vector
                def _(v, tt=tt):
                    x2c = x2[:, tt * D:(tt + 1) * D]
                    v.wait_ge(S9.h, s9_sq)
                    v.reduce_sum(stats[:, 0:1], x2c, axis=AXX)
                    v.drain()
                    v.tensor_scalar_mul(stats[:, 2:3], stats[:, 0:1], 1.0 / D)
                    v.tensor_scalar_mul(stats[:, 3:4], stats[:, 1:2], 1.0 / D)
                    v.drain()
                    v.tensor_mul(stats[:, 4:5], stats[:, 2:3], stats[:, 2:3])
                    v.drain()
                    v.tensor_sub(stats[:, 3:4], stats[:, 3:4], stats[:, 4:5])
                    v.drain()
                    V9.inc(v.tensor_scalar_add(stats[:, 3:4], stats[:, 3:4],
                                               1e-5))
                v9_var = V9.n

                @block.scalar
                def _(sc, tt=tt):
                    sc.wait_ge(V9.h, v9_var)
                    S9.inc(sc.activation(stats[:, 4:5], stats[:, 3:4],
                                         AF.Sqrt))
                s9_sqrt = S9.n

                @block.vector
                def _(v, tt=tt):
                    v.wait_ge(S9.h, s9_sqrt)
                    v.reciprocal(stats[:, 5:6], stats[:, 4:5])
                    v.drain()
                    v.tensor_mul(stats[:, 6:7], stats[:, 2:3], stats[:, 5:6])
                    v.drain()
                    V9.inc(v.tensor_scalar_mul(stats[:, 6:7], stats[:, 6:7],
                                               -1.0))
                v9_rstd = V9.n

                s9_u0 = S9.n
                v9_h20 = V9.n

                @block.scalar
                def _(sc, tt=tt):
                    sc.wait_ge(V9.h, v9_rstd)
                    for jh in range(2):
                        if jh == 1:
                            sc.wait_ge(V9.h, v9_h20 + 1)
                        S9.inc(sc.activation(
                            u9[:], x2[:, tt * D + jh * T:tt * D + (jh + 1) * T],
                            AF.Identity, bias=stats[:, 6:7],
                            scale=stats[:, 5:6]))

                @block.vector
                def _(v, tt=tt):
                    for jh in range(2):
                        v.wait_ge(S9.h, s9_u0 + jh + 1)
                        hs = h2s[:, jh * T:(jh + 1) * T]
                        v.tensor_mul(hs, u9[:], slot[0][:, jh * T:(jh + 1) * T])
                        V9.inc(v.tensor_add(hs, hs,
                                            slot[1][:, jh * T:(jh + 1) * T]))
                v9_h2 = V9.n

                @block.scalar
                def _(sc, tt=tt):
                    sc.wait_ge(V9.h, v9_h2)
                    last = None
                    for jh in range(2):
                        last = sc.activation(u9[:], h2s[:, jh * T:(jh + 1) * T],
                                             AF.Square,
                                             accum_out=stats[:, 13 + jh:14 + jh])
                    S9.inc(last)
                s9_r2 = S9.n

                @block.vector
                def _(v, tt=tt):
                    v.wait_ge(S9.h, s9_r2)
                    v.tensor_add(stats[:, 7:8], stats[:, 13:14], stats[:, 14:15])
                    v.drain()
                    v.tensor_scalar(stats[:, 8:9], stats[:, 7:8], 1.0 / D,
                                    1e-8, ALU.mult, ALU.add)
                    V9.inc(v.engine_nop())
                v9_ra = V9.n

                @block.scalar
                def _(sc, tt=tt):
                    sc.wait_ge(V9.h, v9_ra)
                    S9.inc(sc.activation(stats[:, 8:9], stats[:, 8:9],
                                         AF.Sqrt))
                s9_rs = S9.n

                @block.vector
                def _(v, tt=tt):
                    v.wait_ge(S9.h, s9_rs)
                    V9.inc(v.reciprocal(stats[:, 9:10], stats[:, 8:9]))
                v9_ri = V9.n

                @block.scalar
                def _(sc, tt=tt):
                    sc.wait_ge(V9.h, v9_ri)
                    S9.inc(sc.activation(h2s[:], h2s[:], AF.Identity,
                                         scale=stats[:, 9:10]))
                s9_xn = S9.n

                t9_prev = T9.n

                @block.vector
                def _(v, tt=tt):
                    v.wait_ge(S9.h, s9_xn)
                    v.wait_ge(T9.h, t9_prev)
                    if d_xqall is not None and tt >= 1:
                        v.wait_ge(DXQ.h, 16 * tt)
                    v.reduce_max(stats[:, 10:11], h2s[:], axis=AXX,
                                 apply_absolute_value=True)
                    v.drain()
                    v.tensor_scalar(stats[:, 11:12], stats[:, 10:11],
                                    1.0 / 127.0, 1e-8, ALU.mult, ALU.add)
                    v.drain()
                    v.reciprocal(stats[:, 12:13], stats[:, 11:12])
                    v.drain()
                    v.tensor_scalar_mul(stats[:, 16 + tt:17 + tt],
                                        stats[:, 10:11], 1.0 / 127.0)
                    v.drain()
                    last = None
                    for jh in range(2):
                        uh = u9[:]
                        v.tensor_scalar(uh, h2s[:, jh * T:(jh + 1) * T],
                                        stats[:, 12:13], 128.5,
                                        ALU.mult, ALU.add)
                        v.tensor_copy(v9b[:], uh)
                        for kk in range(7, -1, -1):
                            v.tensor_scalar(d9[:], v9b[:], float(2 ** kk),
                                            float(2 ** kk), ALU.is_ge,
                                            ALU.mult)
                            v.tensor_sub(v9b[:], v9b[:], d9[:])
                        v.scalar_tensor_tensor(d9[:], v9b[:], -1.0, uh,
                                               ALU.mult, ALU.add)
                        last = v.tensor_scalar_add(
                            xqt[:, jh * T:(jh + 1) * T], d9[:], -128.0)
                    V9.inc(last)
                v9_xq = V9.n

                @block.tensor
                def _(tn, tt=tt):
                    tn.wait_ge(V9.h, v9_xq)
                    last = None
                    for kc in range(NC):
                        last = tn.transpose(
                            pxq[:, kc * 128:(kc + 1) * 128],
                            xqt[:, kc * 128:(kc + 1) * 128], eyeb[:])
                    T9.inc(last)
                t9_t = T9.n

                if d_xqall is not None:
                    @block.sync
                    def _(sy, tt=tt):
                        sy.wait_ge(V9.h, v9_xq)
                        _dma(sy, DXQ, d_xqall[tt], xqt[:])

                @block.vector
                def _(v, tt=tt):
                    v.wait_ge(T9.h, t9_t)
                    xq_view = xqf[:].rearrange(
                        "p (c f) -> p c f", c=NC)[:, :, tt * 128:(tt + 1) * 128]
                    V9.inc(v.tensor_copy(
                        xq_view, pxq[:]
                        .rearrange("p (c f) -> p c f", c=NC)))
                v9_xq_marks.append(V9.n)

            st_ffn0 = ST.n
            SGQ = Sem(ctx, nc, "SGQ")
            TGQ = Sem(ctx, nc, "TGQ")
            for q in range(4):
                if q >= 1:
                    load_tern_q(q, TF, q * NT)

                with nc.psum_tensor(f"pgq{q}", [128, 2 * 512], F32) as pgq:
                    @block.tensor
                    def _(tn, q=q):
                        tn.wait_ge(VD.h, vd_gb)
                        if q >= 1:
                            tn.wait_ge(SGQ.h, 2 * q)
                        tn.matmul(pgq[:, 0:512],
                                  e4f[:, q * 128:(q + 1) * 128], gbg4[:],
                                  start=True, stop=True)
                        TGQ.inc(tn.matmul(pgq[:, 512:1024],
                                          e4f[:, q * 128:(q + 1) * 128],
                                          gbfb4[:], start=True, stop=True))

                    @block.scalar
                    def _(sc, q=q):
                        sc.wait_ge(TGQ.h, q + 1)
                        if q >= 1:
                            sc.wait_ge(VF.h, q * NT)
                        sc.copy(gbgb[:], pgq[:, 0:512])
                        SGQ.inc(sc.copy(gbfbb[:], pgq[:, 512:1024]), 2)

                for tt in range(NT):
                    @block.tensor
                    def _(tn, q=q, tt=tt):
                        u = q * NT + tt
                        tn.wait_ge(LTH.h, LTH.n)
                        tn.wait_ge(V9.h, v9_xq_marks[tt])
                        if u >= 2:
                            tn.wait_ge(SF.h, u - 1)
                        last = None
                        for kc in range(NC):
                            last = tn.matmul(
                                pf[:, (u % 2) * 512:(u % 2 + 1) * 512],
                                xqf[:, kc * T + tt * 128:
                                    kc * T + (tt + 1) * 128],
                                tern_sb[:, kc * 512:(kc + 1) * 512],
                                start=(kc == 0), stop=(kc == NC - 1))
                        TF.inc(last)

                    @block.scalar
                    def _(sc, q=q, tt=tt):
                        u = q * NT + tt
                        sc.wait_ge(TF.h, u + 1)
                        if u >= 1:
                            sc.wait_ge(VF.h, u)
                            sc.wait_ge(ST.h, st_ffn0 + 16 * u)
                        SF.inc(sc.activation(fout[:],
                                             pf[:, (u % 2) * 512:
                                                (u % 2 + 1) * 512],
                                             AF.Identity,
                                             scale=stats[:, 16 + tt:17 + tt]))

                    @block.vector
                    def _(v, q=q, tt=tt):
                        u = q * NT + tt
                        v.wait_ge(SF.h, u + 1)
                        v.wait_ge(SGQ.h, 2 * (q + 1))
                        v.tensor_mul(fout[:], fout[:], gbgb[:])
                        v.tensor_add(fout[:], fout[:], gbfbb[:])
                        VF.inc(v.tensor_add(
                            fout[:], fout[:],
                            x2[:, tt * D + q * 512:tt * D + (q + 1) * 512]))

                    @block.sync
                    def _(sy, q=q, tt=tt):
                        u = q * NT + tt
                        sy.wait_ge(VF.h, u + 1)
                        _dma(sy, ST, out_t.ap()[tt * 128:(tt + 1) * 128,
                                                q * 512:(q + 1) * 512],
                             fout[:])


        d_xqt = dbg_out("xqt", [128, D], BF16)
        if d_xqt is not None:
            @block.sync
            def _(sy):
                sy.wait_ge(V9.h, V9.n)
                _dma(sy, ST, d_xqt, xqt[:])

        d_xqf = dbg_out("xqf", [128, NC * T], BF16)
        if d_xqf is not None:
            @block.sync
            def _(sy):
                sy.wait_ge(V9.h, V9.n)
                _dma(sy, ST, d_xqf, xqf[:])

        d_st9 = dbg_out("st9", [128, 24], F32)
        if d_st9 is not None:
            @block.sync
            def _(sy):
                sy.wait_ge(V9.h, V9.n)
                _dma(sy, ST, d_st9, stats[:, 0:24])

        @block.sync
        def _(sy):
            sy.wait_ge(ST.h, ST.n)

        sD4.close()
    return nc


def _rope_tables():
    inv_freq = 1.0 / 10000.0 ** (np.arange(0, HD, 2, dtype=np.float32) / HD)
    pos = np.arange(S, dtype=np.float32)
    freqs = np.outer(pos, inv_freq)
    emb = np.concatenate([freqs, freqs], axis=-1)
    return np.cos(emb).astype(np.float32), np.sin(emb).astype(np.float32)


def _prep_inputs(inputs):
    x = np.asarray(inputs["L0_attn_x"], np.float32)[0]
    WqT = np.asarray(inputs["L0_attn_Wq"], np.float32).T
    WkT = np.asarray(inputs["L0_attn_Wk"], np.float32).T
    WvT = np.asarray(inputs["L0_attn_Wv"], np.float32).T
    WoT = np.asarray(inputs["L0_attn_Wo"], np.float32).T

    def w4(WT):
        return np.ascontiguousarray(
            WT.reshape(NC, 128, NC, 128).transpose(2, 1, 0, 3))

    cos_t, sin_t = _rope_tables()
    common = {
        "wq4": w4(WqT),
        "wk4": w4(WkT),
        "wo4": w4(WoT),
        "wv_r": np.ascontiguousarray(WvT.reshape(NC, 128, D)),
        "bqm": np.ascontiguousarray(
            np.asarray(inputs["L0_attn_bq"], np.float32).reshape(NC, 128).T),
        "bkm": np.ascontiguousarray(
            np.asarray(inputs["L0_attn_bk"], np.float32).reshape(NC, 128).T),
        "bvrow": np.asarray(inputs["L0_attn_bv"], np.float32).reshape(4, 512),
        "borow": np.asarray(inputs["L0_attn_bo"], np.float32).reshape(4, 512),
        "fbrow": np.asarray(inputs["L0_ffn_bias"], np.float32).reshape(4, 512),
        "cm": np.ascontiguousarray(
            np.asarray(inputs["L0_c"], np.float32).reshape(NC, 128).T),
        "eyeb": np.eye(128, dtype=np.float32).astype(ml_dtypes.bfloat16),
        "eyef": np.eye(128, dtype=np.float32),
        "onesf": np.ones((1, 128), np.float32),
        "e4f": _e4f(),
        "ehb": _ehb(),
    }
    fw = np.asarray(inputs["L0_ffn_weight"], np.float32)
    wmod = np.asarray(inputs["L0_wmod"], np.float32)
    in_maps = []
    for r in range(CORES):
        m = dict(common)
        m["x"] = np.ascontiguousarray(x[r * T:(r + 1) * T])
        m["fws"] = np.ascontiguousarray(
            fw[r * NSH:(r + 1) * NSH].reshape(2, 128, D))
        m["wmods"] = np.ascontiguousarray(
            wmod[:, r * JSH:(r + 1) * JSH].reshape(NC, 128, JSH))
        m["cosm"] = np.ascontiguousarray(cos_t[r * T:(r + 1) * T].T)
        m["sinm"] = np.ascontiguousarray(sin_t[r * T:(r + 1) * T].T)
        in_maps.append(m)
    return in_maps


def _e4f():
    m = np.zeros((4, 512), np.float32)
    for jt in range(4):
        m[jt, jt * 128:(jt + 1) * 128] = 1.0
    return m


def _ehb():
    e_all = np.zeros((128, H * H), np.float32)
    for oc in range(H):
        e_all[:, oc * H + oc] = 1.0
    return e_all.astype(ml_dtypes.bfloat16)


def kernel(**inputs):
    in_maps = _prep_inputs(inputs)
    nc = build_nc()
    res = run_bass_kernel_spmd(nc, in_maps, core_ids=list(range(CORES)))
    out = np.concatenate([np.asarray(r["out"]) for r in res.results], axis=0)
    return out.reshape(1, S, D).astype(np.float32)


# version 09
# speedup vs baseline: 1.0820x; 1.0820x over previous
import numpy as np
import ml_dtypes
from contextlib import ExitStack

import concourse.bass as bass
import concourse.mybir as mybir
from concourse.bass_utils import run_bass_kernel_spmd

F32 = mybir.dt.float32
BF16 = mybir.dt.bfloat16
AF = mybir.ActivationFunctionType
ALU = mybir.AluOpType
AXX = mybir.AxisListType.X

CORES = 8
T = 1024
D = 2048
H = 16
HD = 128
S = 8192
NT = T // 128
NC = D // 128
JSH = 6 * D // CORES
NSH = D // CORES
SCALE = float(HD) ** -0.5

_DEBUG = []


class Sem:
    def __init__(self, ctx, nc, name):
        self.h = ctx.enter_context(nc.semaphore(name))
        self.n = 0

    def inc(self, inst, k=1):
        inst.then_inc(self.h, k)
        self.n += k
        return self.n


def _dma(eng, sem, out, in_):
    eng.dma_start(out=out, in_=in_).then_inc(sem.h, 16)
    sem.n += 16
    return sem.n


def build_nc():
    nc = bass.Bass("TRN2", target_bir_lowering=False,
                   detect_race_conditions=False)

    x_in = nc.dram_tensor("x", [T, D], F32, kind="ExternalInput").ap()
    wq4 = nc.dram_tensor("wq4", [NC, 128, NC, 128], F32, kind="ExternalInput").ap()
    wk4 = nc.dram_tensor("wk4", [NC, 128, NC, 128], F32, kind="ExternalInput").ap()
    wo4 = nc.dram_tensor("wo4", [NC, 128, NC, 128], F32, kind="ExternalInput").ap()
    wv_r = nc.dram_tensor("wv_r", [NC, 128, D], F32, kind="ExternalInput").ap()
    bqm_in = nc.dram_tensor("bqm", [128, NC], F32, kind="ExternalInput").ap()
    bkm_in = nc.dram_tensor("bkm", [128, NC], F32, kind="ExternalInput").ap()
    bvrow_in = nc.dram_tensor("bvrow", [4, 512], F32, kind="ExternalInput").ap()
    borow_in = nc.dram_tensor("borow", [4, 512], F32, kind="ExternalInput").ap()
    fbrow_in = nc.dram_tensor("fbrow", [4, 512], F32, kind="ExternalInput").ap()
    fws_in = nc.dram_tensor("fws", [2, 128, D], F32, kind="ExternalInput").ap()
    cm_in = nc.dram_tensor("cm", [128, NC], F32, kind="ExternalInput").ap()
    wmods_in = nc.dram_tensor("wmods", [NC, 128, JSH], F32, kind="ExternalInput").ap()
    cos_in = nc.dram_tensor("cosm", [128, T], F32, kind="ExternalInput").ap()
    sin_in = nc.dram_tensor("sinm", [128, T], F32, kind="ExternalInput").ap()
    eyeb_in = nc.dram_tensor("eyeb", [128, 128], BF16, kind="ExternalInput").ap()
    eyef_in = nc.dram_tensor("eyef", [128, 128], F32, kind="ExternalInput").ap()
    onesf_in = nc.dram_tensor("onesf", [1, 128], F32, kind="ExternalInput").ap()
    e4f_in = nc.dram_tensor("e4f", [4, 512], F32, kind="ExternalInput").ap()
    ehb_in = nc.dram_tensor("ehb", [128, H * H], BF16, kind="ExternalInput").ap()

    out_t = nc.dram_tensor("out", [T, D], F32, kind="ExternalOutput")

    KSZ = NC * 128 * T
    VSZ = H * 128 * NT * 128
    KVN = KSZ + VSZ + H
    kv_in = nc.dram_tensor("kv_in", [KVN], BF16)
    kv_g = nc.dram_tensor("kv_g", [CORES * KVN], BF16, addr_space="Shared")
    q_st = nc.dram_tensor("q_st", [H, 128, T], BF16)
    mod_in = nc.dram_tensor("mod_in", [JSH], F32)
    mod_g = nc.dram_tensor("mod_g", [CORES * JSH], F32, addr_space="Shared")
    gam_in = nc.dram_tensor("gam_in", [NSH], F32)
    gam_g = nc.dram_tensor("gam_g", [CORES * NSH], F32, addr_space="Shared")
    tern_in = nc.dram_tensor("tern_in", [NC, 128, NSH], BF16)
    tern_g = nc.dram_tensor("tern_g", [CORES, NC, 128, NSH], BF16, addr_space="Shared")

    dbg = {}
    def dbg_out(name, shape, dtype=F32):
        if name in _DEBUG:
            dbg[name] = nc.dram_tensor("dbg_" + name, shape, dtype,
                                       kind="ExternalOutput").ap()
            return dbg[name]
        return None

    ctx = ExitStack()
    with ctx:
        e = ctx.enter_context
        eyeb = e(nc.sbuf_tensor("sb_eyeb", [128, 128], BF16))
        eyef = e(nc.sbuf_tensor("sb_eyef", [128, 128], F32))
        onesf = e(nc.sbuf_tensor("sb_onesf", [1, 128], F32))
        e4f = e(nc.sbuf_tensor("sb_e4f", [4, 512], F32))
        ehb = e(nc.sbuf_tensor("sb_ehb", [128, H * H], BF16))
        bqm = e(nc.sbuf_tensor("sb_bqm", [128, NC], F32))
        bkm = e(nc.sbuf_tensor("sb_bkm", [128, NC], F32))
        def row4(name):
            return e(nc.sbuf_tensor(name, [4, 512], F32))
        bv4, bo4, fb4, gabo4, gbg4, gbfb4, rs0, rs1 = (
            row4(n) for n in ("bv4", "bo4", "fb4", "gabo4", "gbg4",
                              "gbfb4", "rs0", "rs1"))
        modr = [e(nc.sbuf_tensor(f"modr{m}", [4, 512], F32)) for m in range(6)]
        slot = [e(nc.sbuf_tensor(f"slot{i}", [128, D], F32)) for i in range(2)]
        negcq = e(nc.sbuf_tensor("negcq", [128, NT * H], F32))
        stats = e(nc.sbuf_tensor("stats", [128, 64], F32))
        kn2 = e(nc.sbuf_tensor("kn2", [H, T], F32))
        qn2 = e(nc.sbuf_tensor("qn2", [H, T], F32))
        bks = e(nc.sbuf_tensor("bks", [H, 8], F32))
        gam2 = e(nc.sbuf_tensor("gam2", [128, 4], F32))
        scalev = e(nc.sbuf_tensor("scalev", [128, 1], F32))
        grow = e(nc.sbuf_tensor("grow", [4, 512], F32))

        GD = Sem(ctx, nc, "GD")
        CC = Sem(ctx, nc, "CC")
        MM = Sem(ctx, nc, "MM")
        VD = Sem(ctx, nc, "VD")
        SD = Sem(ctx, nc, "SD")
        ST = Sem(ctx, nc, "ST")
        LSM = Sem(ctx, nc, "LSM")
        LX = Sem(ctx, nc, "LX")
        LWM = Sem(ctx, nc, "LWM")
        LMOD = Sem(ctx, nc, "LMOD")
        STG = Sem(ctx, nc, "STG")

        block = e(nc.Block())

        @block.vector
        def _(v):
            v.memset(scalev[:], SCALE)

        sA = ExitStack()
        eA = sA.enter_context
        h1f = eA(nc.sbuf_tensor("h1f", [128, NC * 1024], BF16))
        cosm = eA(nc.sbuf_tensor("sb_cosm", [128, T], F32))
        sinm = eA(nc.sbuf_tensor("sb_sinm", [128, T], F32))
        sB = ExitStack()
        eB = sB.enter_context
        xbuf = eB(nc.sbuf_tensor("xbuf", [128, 4 * D], F32))
        h1t = eB(nc.sbuf_tensor("h1t", [128, 4 * D], BF16))
        wmb = eB(nc.sbuf_tensor("wmb", [128, 2 * JSH], F32))
        scr = eB(nc.sbuf_tensor("scr", [128, D], F32))
        cm_sb = eB(nc.sbuf_tensor("cm_sb", [128, NC], F32))
        scm = eB(nc.sbuf_tensor("scm", [128, NC], F32))
        modsl = eB(nc.sbuf_tensor("modsl", [1, JSH], F32))

        @block.sync
        def _(sy):
            _dma(sy, LSM, cm_sb[:], cm_in)
            _dma(sy, LSM, cosm[:], cos_in)
            _dma(sy, LSM, sinm[:], sin_in)
            _dma(sy, LSM, bqm[:], bqm_in)
            _dma(sy, LSM, bkm[:], bkm_in)
            _dma(sy, LSM, bv4[:], bvrow_in)
            _dma(sy, LSM, bo4[:], borow_in)
            _dma(sy, LSM, fb4[:], fbrow_in)
            _dma(sy, LSM, eyeb[:], eyeb_in)
            _dma(sy, LSM, eyef[:], eyef_in)
            _dma(sy, LSM, onesf[:], onesf_in)
            _dma(sy, LSM, e4f[:], e4f_in)
            _dma(sy, LSM, ehb[:], ehb_in)
        ld_p0 = LSM.n

        @block.sync
        def _(sy):
            for i in range(4):
                _dma(sy, LX, xbuf[:, i * D:(i + 1) * D],
                     x_in[i * 128:(i + 1) * 128, :])
        ld_xw = [LX.n]

        with nc.psum_tensor("pmod", [1, JSH], F32) as pmod:
            @block.scalar
            def _(sc):
                sc.wait_ge(LSM.h, ld_p0)
                SD.inc(sc.activation(scm[:], cm_sb[:], AF.Silu))
            sd_silu = SD.n

            ld_wq = []
            mm_wq = []
            for stp in range(8):
                @block.sync
                def _(sy, stp=stp):
                    if stp >= 1:
                        sy.wait_ge(MM.h, mm_wq[stp - 1])
                    for i in range(2):
                        _dma(sy, LWM, wmb[:, i * JSH:(i + 1) * JSH],
                             wmods_in[stp * 2 + i])
                ld_wq.append(LWM.n)

                @block.tensor
                def _(tn, stp=stp):
                    tn.wait_ge(LWM.h, ld_wq[stp])
                    if stp == 0:
                        tn.wait_ge(SD.h, sd_silu)
                    last = None
                    for i in range(2):
                        ic = stp * 2 + i
                        for jt in range(JSH // 512):
                            last = tn.matmul(pmod[:, jt * 512:(jt + 1) * 512],
                                             scm[:, ic:ic + 1],
                                             wmb[:, i * JSH + jt * 512:
                                                 i * JSH + (jt + 1) * 512],
                                             start=(ic == 0), stop=(ic == NC - 1))
                    MM.inc(last)
                mm_wq.append(MM.n)
            mm_mod = MM.n

            @block.scalar
            def _(sc):
                sc.wait_ge(MM.h, mm_mod)
                SD.inc(sc.copy(modsl[:], pmod[:]))
            sd_modsl = SD.n

        @block.gpsimd
        def _(gp):
            gp.wait_ge(SD.h, sd_modsl)
            _dma(gp, GD, mod_in.ap().rearrange("(a b) -> a b", a=1), modsl[:])
            gp.wait_ge(GD.h, GD.n)
            CC.inc(gp.collective_compute(
                "AllGather", ALU.bypass,
                replica_groups=[list(range(CORES))],
                ins=[mod_in.ap()], outs=[mod_g.ap()]))
        cc_mod = CC.n

        @block.sync
        def _(sy):
            sy.wait_ge(CC.h, cc_mod)
            for m in range(6):
                _dma(sy, LMOD, modr[m][:],
                     mod_g.ap()[m * D:(m + 1) * D].rearrange("(p b) -> p b", b=512))
        ld_mod = LMOD.n

        MSA, MCA, MGA, MSB, MCB, MGB = range(6)

        def mrow(m):
            return modr[m][:]

        @block.vector
        def _(v):
            v.wait_ge(LMOD.h, ld_mod)
            v.tensor_scalar_add(mrow(MCA), mrow(MCA), 1.0)
            v.tensor_scalar_add(mrow(MCB), mrow(MCB), 1.0)
            for mg in (MGA, MGB):
                g = mrow(mg)
                v.tensor_scalar(rs0[:], g, 0.0, None, ALU.is_ge)
                v.tensor_scalar(rs0[:], rs0[:], 2.0, -1.0, ALU.mult, ALU.add)
                v.tensor_scalar_mul(rs1[:], g, -1.0)
                v.tensor_max(rs1[:], rs1[:], g)
                v.tensor_scalar_max(rs1[:], rs1[:], 0.25)
                v.tensor_mul(g, rs0[:], rs1[:])
            VD.inc(v.tensor_mul(gabo4[:], mrow(MGA), bo4[:]))
        vd_mod = VD.n

        def bcast_row(row4_ap, dst, tag, wait_sems):
            with nc.psum_tensor("pb_" + tag, [128, D], F32) as pb:
                prev_sd = SD.n

                @block.tensor
                def _(tn):
                    tn.wait_ge(SD.h, prev_sd)
                    for sem, val in wait_sems:
                        tn.wait_ge(sem.h, val)
                    last = None
                    for jt in range(4):
                        last = tn.matmul(pb[:, jt * 512:(jt + 1) * 512],
                                         e4f[:, jt * 128:(jt + 1) * 128],
                                         row4_ap[:], start=True, stop=True)
                    MM.inc(last)
                mmv = MM.n

                @block.scalar
                def _(sc):
                    sc.wait_ge(MM.h, mmv)
                    SD.inc(sc.copy(dst[:], pb[:]))
            return SD.n

        sd_ca1 = bcast_row(mrow(MCA), slot[0], "ca1", [(VD, vd_mod)])
        sd_sa = bcast_row(mrow(MSA), slot[1], "sa", [(VD, vd_mod)])

        vd_h1w = []
        dbg_st_gate = []
        for w in range(2):
            cs = range(w * 4, w * 4 + 4)

            if w == 1:
                @block.sync
                def _(sy):
                    sy.wait_ge(VD.h, vd_h1w[0])
                    for i in range(4):
                        _dma(sy, LX, xbuf[:, i * D:(i + 1) * D],
                             x_in[(4 + i) * 128:(5 + i) * 128, :])
                ld_xw.append(LX.n)

            @block.vector
            def _(v, w=w):
                v.wait_ge(LX.h, ld_xw[w])
                if w == 1 and dbg_st_gate:
                    v.wait_ge(ST.h, dbg_st_gate[-1])
                last = None
                for i in cs:
                    xc = xbuf[:, (i % 4) * D:((i % 4) + 1) * D]
                    last = v.reduce_sum(stats[:, i:i + 1], xc, axis=AXX)
                VD.inc(last)
            vd_s1 = VD.n

            @block.scalar
            def _(sc, w=w):
                sc.wait_ge(LX.h, ld_xw[w])
                last = None
                for i in cs:
                    xc = xbuf[:, (i % 4) * D:((i % 4) + 1) * D]
                    last = sc.activation(scr[:], xc, AF.Square,
                                         accum_out=stats[:, 8 + i:9 + i])
                SD.inc(last)
            sd_s2 = SD.n

            a, b = w * 4, w * 4 + 4

            @block.vector
            def _(v):
                v.wait_ge(VD.h, vd_s1)
                v.wait_ge(SD.h, sd_s2)
                v.tensor_scalar_mul(stats[:, 16 + a:16 + b], stats[:, a:b], 1.0 / D)
                v.tensor_scalar_mul(stats[:, 24 + a:24 + b], stats[:, 8 + a:8 + b], 1.0 / D)
                v.drain()
                v.tensor_mul(stats[:, 32 + a:32 + b], stats[:, 16 + a:16 + b],
                             stats[:, 16 + a:16 + b])
                v.drain()
                v.tensor_sub(stats[:, 24 + a:24 + b], stats[:, 24 + a:24 + b],
                             stats[:, 32 + a:32 + b])
                v.drain()
                VD.inc(v.tensor_scalar_add(stats[:, 24 + a:24 + b],
                                           stats[:, 24 + a:24 + b], 1e-5))
            vd_var = VD.n

            @block.scalar
            def _(sc):
                sc.wait_ge(VD.h, vd_var)
                SD.inc(sc.activation(stats[:, 32 + a:32 + b],
                                     stats[:, 24 + a:24 + b], AF.Sqrt))
            sd_sqrt = SD.n

            @block.vector
            def _(v):
                v.wait_ge(SD.h, sd_sqrt)
                v.reciprocal(stats[:, 40 + a:40 + b], stats[:, 32 + a:32 + b])
                v.drain()
                v.tensor_mul(stats[:, 48 + a:48 + b], stats[:, 16 + a:16 + b],
                             stats[:, 40 + a:40 + b])
                v.drain()
                VD.inc(v.tensor_scalar_mul(stats[:, 48 + a:48 + b],
                                           stats[:, 48 + a:48 + b], -1.0))
            vd_rstd = VD.n

            @block.scalar
            def _(sc):
                sc.wait_ge(VD.h, vd_rstd)
                last = None
                for i in cs:
                    last = sc.activation(xbuf[:, (i % 4) * D:((i % 4) + 1) * D],
                                         xbuf[:, (i % 4) * D:((i % 4) + 1) * D],
                                         AF.Identity,
                                         bias=stats[:, 48 + i:49 + i],
                                         scale=stats[:, 40 + i:41 + i])
                SD.inc(last)
            sd_u = SD.n

            d_h1 = dbg_out("h1", [T, D], BF16) if w == 0 else d_h1

            @block.vector
            def _(v):
                v.wait_ge(SD.h, sd_u)
                v.wait_ge(SD.h, max(sd_ca1, sd_sa))
                last = None
                for i in cs:
                    u = xbuf[:, (i % 4) * D:((i % 4) + 1) * D]
                    v.tensor_mul(u, u, slot[0][:])
                    last = v.tensor_add(h1t[:, (i % 4) * D:((i % 4) + 1) * D],
                                        u, slot[1][:])
                VD.inc(last)
            vd_h1w.append(VD.n)
            vd_h1wv = VD.n

            if d_h1 is not None:
                @block.sync
                def _(sy):
                    sy.wait_ge(VD.h, vd_h1wv)
                    for i in cs:
                        _dma(sy, ST, d_h1[i * 128:(i + 1) * 128, :],
                             h1t[:, (i % 4) * D:((i % 4) + 1) * D])
                dbg_st_gate.append(ST.n)

            with nc.psum_tensor(f"pt1a{w}", [128, 512], BF16) as pt1a, \
                 nc.psum_tensor(f"pt1b{w}", [128, 512], BF16) as pt1b:
                pts = [pt1a, pt1b]

                mm_t1base = MM.n

                @block.tensor
                def _(tn):
                    tn.wait_ge(VD.h, vd_h1wv)
                    for fc in range(NC):
                        if fc >= 2:
                            tn.wait_ge(VD.h, vd_h1wv + fc - 1)
                        p = pts[fc % 2]
                        last = None
                        for i in range(4):
                            last = tn.transpose(p[:, i * 128:(i + 1) * 128],
                                                h1t[:, i * D + fc * 128:
                                                    i * D + (fc + 1) * 128],
                                                eyeb[:])
                        MM.inc(last)
                mm_t1 = MM.n

                @block.vector
                def _(v):
                    for fc in range(NC):
                        v.wait_ge(MM.h, mm_t1base + fc + 1)
                        VD.inc(v.tensor_copy(
                            h1f[:, fc * T + w * 512: fc * T + (w + 1) * 512],
                            pts[fc % 2][:]))
                vd_h1w[-1] = VD.n

        vd_t1 = VD.n

        d_stats = dbg_out("stats", [128, 64], F32)
        if d_stats is not None:
            @block.sync
            def _(sy):
                sy.wait_ge(VD.h, vd_t1)
                _dma(sy, ST, d_stats, stats[:])

        d_slot0 = dbg_out("slot0", [128, D], F32)
        if d_slot0 is not None:
            @block.sync
            def _(sy):
                sy.wait_ge(VD.h, vd_t1)
                _dma(sy, ST, d_slot0, slot[0][:])

        d_slot1 = dbg_out("slot1", [128, D], F32)
        if d_slot1 is not None:
            @block.sync
            def _(sy):
                sy.wait_ge(VD.h, vd_t1)
                _dma(sy, ST, d_slot1, slot[1][:])

        sB.close()

        sC = ExitStack()
        eC = sC.enter_context
        wkb = eC(nc.sbuf_tensor("wkb", [128, 2 * D], BF16))
        bvb = eC(nc.sbuf_tensor("bvb", [128, D], F32))
        qf = eC(nc.sbuf_tensor("qf", [128, T], F32))
        rot = eC(nc.sbuf_tensor("rot", [128, T], F32))
        ta_b = eC(nc.sbuf_tensor("ta_b", [128, T], F32))
        kbf = eC(nc.sbuf_tensor("kbf", [128, T], BF16))
        k2b = eC(nc.sbuf_tensor("k2b", [128, T], BF16))
        knb = eC(nc.sbuf_tensor("knb", [16, 1], BF16))
        sCV = ExitStack()
        eCV = sCV.enter_context
        wvb = eCV(nc.sbuf_tensor("wvb", [128, NC * D], BF16))
        vtm = eCV(nc.sbuf_tensor("vtm", [128, D], BF16))

        LWK = Sem(ctx, nc, "LWK")
        LWV = Sem(ctx, nc, "LWV")
        TSK = Sem(ctx, nc, "TSK")
        SKE = Sem(ctx, nc, "SKE")
        VKR = Sem(ctx, nc, "VKR")
        TKN = Sem(ctx, nc, "TKN")
        TV = Sem(ctx, nc, "TV")
        VV = Sem(ctx, nc, "VV")

        LWKP = [Sem(ctx, nc, "LWKP0"), Sem(ctx, nc, "LWKP1")]

        @block.gpsimd
        def _(gp):
            gp.wait_ge(VD.h, vd_t1)
            gp.wait_ge(MM.h, mm_mod)

        sd_bv = bcast_row(bv4[:], bvb, "bv", [(LSM, ld_p0), (VD, vd_t1),
                                              (MM, mm_mod)])

        def kq_phase(w4_in, bias_m, stage_region, tag, qn_psum_dst, norm_sem_t):
            tsk0, ske0, vkr0 = TSK.n, SKE.n, VKR.n
            nrm0 = norm_sem_t.n
            gd0 = GD.n
            lwk0 = [LWKP[0].n, LWKP[1].n]
            with nc.psum_tensor("pqk_" + tag, [128, 2 * T], F32) as pqk:
                @block.gpsimd
                def _(gp):
                    for oc in range(NC):
                        if oc >= 2:
                            gp.wait_ge(TSK.h, tsk0 + oc - 1)
                        _dma(gp, LWKP[oc % 2],
                             wkb[:, (oc % 2) * D:(oc % 2 + 1) * D],
                             w4_in[oc])
                        if oc >= 1:
                            gp.wait_ge(VKR.h, vkr0 + oc)
                            _dma(gp, GD, stage_region(oc - 1), kbf[:])
                    gp.wait_ge(VKR.h, vkr0 + NC)
                    _dma(gp, GD, stage_region(NC - 1), kbf[:])

                def knorm(tn, oc):
                    tn.wait_ge(VKR.h, vkr0 + oc + 1)
                    last = None
                    for tt in range(2):
                        last = tn.matmul(
                            qn_psum_dst[:, tt * 512:(tt + 1) * 512],
                            ehb[:, oc * H:(oc + 1) * H],
                            k2b[:, tt * 512:(tt + 1) * 512],
                            start=(oc == 0), stop=(oc == NC - 1))
                    norm_sem_t.inc(last)

                @block.tensor
                def _(tn):
                    for oc in range(NC):
                        tn.wait_ge(LWKP[oc % 2].h,
                                   lwk0[oc % 2] + 16 * (oc // 2 + 1))
                        tn.wait_ge(SKE.h, max(ske0 + oc - 1, ske0))
                        tn.wait_ge(SD.h, SD.n)
                        last = None
                        for tt in range(2):
                            for ic in range(NC):
                                last = tn.matmul(
                                    pqk[:, (oc % 2) * T + tt * 512:
                                        (oc % 2) * T + (tt + 1) * 512],
                                    wkb[:, (oc % 2) * D + ic * 128:
                                        (oc % 2) * D + (ic + 1) * 128],
                                    h1f[:, ic * T + tt * 512:
                                        ic * T + (tt + 1) * 512],
                                    start=(ic == 0), stop=(ic == NC - 1))
                        TSK.inc(last)
                        if oc >= 2:
                            knorm(tn, oc - 2)
                    knorm(tn, NC - 2)
                    knorm(tn, NC - 1)

                @block.scalar
                def _(sc):
                    for oc in range(NC):
                        sc.wait_ge(TSK.h, tsk0 + oc + 1)
                        if oc >= 1:
                            sc.wait_ge(VKR.h, vkr0 + oc)
                        SKE.inc(sc.activation(
                            qf[:], pqk[:, (oc % 2) * T:(oc % 2 + 1) * T],
                            AF.Identity, bias=bias_m[:, oc:oc + 1]))

                @block.vector
                def _(v):
                    for oc in range(NC):
                        v.wait_ge(SKE.h, ske0 + oc + 1)
                        v.wait_ge(GD.h, gd0 + 16 * oc)
                        if oc >= 1:
                            v.wait_ge(norm_sem_t.h, nrm0 + oc)
                        v.tensor_scalar_mul(rot[0:64, :], qf[64:128, :], -1.0)
                        v.tensor_copy(rot[64:128, :], qf[0:64, :])
                        v.tensor_mul(ta_b[:], qf[:], cosm[:])
                        v.tensor_mul(rot[:], rot[:], sinm[:])
                        v.tensor_add(kbf[:], ta_b[:], rot[:])
                        VKR.inc(v.tensor_mul(k2b[:], kbf[:], kbf[:]))
            return tsk0, ske0, vkr0

        with nc.psum_tensor("pkn", [16, T], F32) as pkn:
            kq_phase_gd0 = GD.n
            tsk0, ske0, vkr0 = kq_phase(
                wk4, bkm,
                lambda oc: kv_in.ap()[oc * 131072:(oc + 1) * 131072]
                .rearrange("(p f) -> p f", f=T),
                "k", pkn, TKN)

            @block.scalar
            def _(sc):
                sc.wait_ge(TKN.h, TKN.n)
                SKE.inc(sc.copy(kn2[:], pkn[:]))
            ske_kn = SKE.n

        @block.vector
        def _(v):
            v.wait_ge(SKE.h, ske_kn)
            v.reduce_max(bks[:, 0:1], kn2[:], axis=AXX)
            v.drain()
            VKR.inc(v.tensor_copy(knb[:], bks[:, 0:1]))
        vkr_kn = VKR.n

        @block.gpsimd
        def _(gp):
            gp.wait_ge(VKR.h, vkr_kn)
            _dma(gp, GD, kv_in.ap()[KSZ + VSZ:KSZ + VSZ + H]
                 .rearrange("(a b) -> a b", a=16), knb[:])

        @block.gpsimd
        def _(gp):
            for ic in range(NC):
                _dma(gp, LWV, wvb[:, ic * D:(ic + 1) * D], wv_r[ic])
        ld_wv = LWV.n

        vr = kv_in.ap()[KSZ:KSZ + VSZ].rearrange("(h p c d) -> h p c d",
                                                 h=H, p=128, c=NT)
        with nc.psum_tensor("pv", [128, 2 * D], F32) as pv:
            tv0, vv0 = TV.n, VV.n
            gdv0 = GD.n

            @block.tensor
            def _(tn):
                tn.wait_ge(LWV.h, ld_wv)
                for tc in range(NT):
                    if tc >= 2:
                        tn.wait_ge(VV.h, vv0 + tc - 1)
                    last = None
                    for ic in range(NC):
                        for ot in range(4):
                            last = tn.matmul(
                                pv[:, (tc % 2) * D + ot * 512:
                                   (tc % 2) * D + (ot + 1) * 512],
                                h1f[:, ic * T + tc * 128:ic * T + (tc + 1) * 128],
                                wvb[:, ic * D + ot * 512:ic * D + (ot + 1) * 512],
                                start=(ic == 0), stop=(ic == NC - 1))
                    TV.inc(last)

            @block.vector
            def _(v):
                v.wait_ge(SD.h, sd_bv)
                for tc in range(NT):
                    v.wait_ge(TV.h, tv0 + tc + 1)
                    if tc >= 1:
                        v.wait_ge(GD.h, gdv0 + 16 * tc)
                    VV.inc(v.tensor_add(vtm[:], pv[:, (tc % 2) * D:
                                                   (tc % 2 + 1) * D],
                                        bvb[:]))

            @block.gpsimd
            def _(gp):
                for tc in range(NT):
                    gp.wait_ge(VV.h, vv0 + tc + 1)
                    _dma(gp, GD, vr[:, :, tc, :].transpose([1, 0, 2]),
                         vtm[:].rearrange("p (h d) -> p h d", h=H))

        @block.gpsimd
        def _(gp):
            gp.wait_ge(GD.h, GD.n)
            CC.inc(gp.collective_compute(
                "AllGather", ALU.bypass,
                replica_groups=[list(range(CORES))],
                ins=[kv_in.ap()], outs=[kv_g.ap()]))
        cc_kv = CC.n
        sCV.close()

        LFW = Sem(ctx, nc, "LFW")
        STN = Sem(ctx, nc, "STN")
        SVN = Sem(ctx, nc, "SVN")
        SSN = Sem(ctx, nc, "SSN")
        fwsb = eC(nc.sbuf_tensor("fwsb", [128, 2 * D], F32))
        tscr = eC(nc.sbuf_tensor("tscr", [128, D], F32))
        trat = eC(nc.sbuf_tensor("trat", [128, D], F32))
        tgab = eC(nc.sbuf_tensor("tgab", [128, D], BF16))
        tgbb = eC(nc.sbuf_tensor("tgbb", [128, D], BF16))
        wtb = eC(nc.sbuf_tensor("wtb", [128, 2 * D], BF16))
        ttr = eC(nc.sbuf_tensor("ttr", [128, NSH], BF16))

        gd_p5 = GD.n

        @block.sync
        def _(sy):
            sy.wait_ge(TV.h, TV.n)
            sy.wait_ge(VV.h, VV.n)
            sy.wait_ge(GD.h, gd_p5)
            for ch in range(2):
                _dma(sy, LFW, fwsb[:, ch * D:(ch + 1) * D], fws_in[ch])
        ld_fw = LFW.n

        @block.scalar
        def _(sc):
            sc.wait_ge(LFW.h, ld_fw)
            last = None
            for ch in range(2):
                last = sc.activation(tscr[:], fwsb[:, ch * D:(ch + 1) * D],
                                     AF.Abs, accum_out=gam2[:, ch:ch + 1])
            SSN.inc(last)
        ssn_g = SSN.n

        @block.vector
        def _(v):
            v.wait_ge(SSN.h, ssn_g)
            v.tensor_scalar_mul(gam2[:, 0:2], gam2[:, 0:2], 1.0 / D)
            v.drain()
            v.tensor_scalar_add(gam2[:, 2:4], gam2[:, 0:2], 1e-8)
            v.drain()
            v.reciprocal(gam2[:, 2:4], gam2[:, 2:4])
            v.drain()
            last = None
            for ch in range(2):
                v.tensor_scalar_mul(trat[:], fwsb[:, ch * D:(ch + 1) * D],
                                    gam2[:, 2 + ch:3 + ch])
                v.tensor_scalar(tgab[:], trat[:], 0.5, None, ALU.is_ge)
                v.tensor_scalar(tgbb[:], trat[:], -1.0, 0.5, ALU.mult, ALU.is_ge)
                last = v.tensor_sub(wtb[:, ch * D:(ch + 1) * D], tgab[:],
                                    tgbb[:])
            SVN.inc(last)
        svn_t = SVN.n

        with nc.psum_tensor("ptn", [128, NSH], BF16) as ptn:
            gdt0 = GD.n
            @block.tensor
            def _(tn):
                tn.wait_ge(SVN.h, svn_t)
                for kc in range(NC):
                    if kc >= 1:
                        tn.wait_ge(SVN.h, svn_t + kc)
                    last = None
                    for ch in range(2):
                        last = tn.transpose(
                            ptn[:, ch * 128:(ch + 1) * 128],
                            wtb[:, ch * D + kc * 128:ch * D + (kc + 1) * 128],
                            eyeb[:])
                    STN.inc(last)
            stn0 = STN.n - NC

            @block.vector
            def _(v):
                for kc in range(NC):
                    v.wait_ge(STN.h, stn0 + kc + 1)
                    if kc >= 1:
                        v.wait_ge(GD.h, gdt0 + 16 * kc)
                    SVN.inc(v.tensor_copy(ttr[:], ptn[:]))

            @block.gpsimd
            def _(gp):
                for kc in range(NC):
                    gp.wait_ge(SVN.h, svn_t + kc + 1)
                    _dma(gp, GD, tern_in[kc], ttr[:])
                gp.wait_ge(SVN.h, svn_t)
                gp.dma_start(out=gam_in.ap().rearrange("(c p) -> p c", p=128),
                             in_=gam2[:, 0:2],
                             allow_slow_non_contiguous=True).then_inc(GD.h, 16)
                GD.n += 16
                gp.wait_ge(GD.h, GD.n)
                CC.inc(gp.collective_compute(
                    "AllGather", ALU.bypass,
                    replica_groups=[list(range(CORES))],
                    ins=[gam_in.ap()], outs=[gam_g.ap()]))
                CC.inc(gp.collective_compute(
                    "AllGather", ALU.bypass,
                    replica_groups=[list(range(CORES))],
                    ins=[tern_in.ap()], outs=[tern_g.ap()]))
        cc_all = CC.n

        with nc.psum_tensor("pqn", [16, T], F32) as pqn:
            TQN = Sem(ctx, nc, "TQN")
            tskq0, skeq0, vkrq0 = kq_phase(
                wq4, bqm,
                lambda oc: q_st.ap()[oc],
                "q", pqn, TQN)

            @block.scalar
            def _(sc):
                sc.wait_ge(TQN.h, TQN.n)
                SKE.inc(sc.copy(qn2[:], pqn[:]))
            ske_qn = SKE.n

        LKN = Sem(ctx, nc, "LKN")
        knall = eC(nc.sbuf_tensor("knall", [16, 8], BF16))
        qn2t = eC(nc.sbuf_tensor("qn2t", [128, 128], F32))
        bksrow = eC(nc.sbuf_tensor("bksrow", [1, 16], F32))
        bksb = eC(nc.sbuf_tensor("bksb", [128, 16], F32))

        @block.sync
        def _(sy):
            sy.wait_ge(CC.h, cc_kv)
            kn_ap = kv_g.ap().rearrange("(r e) -> e r", r=CORES)[
                KSZ + VSZ:KSZ + VSZ + H, :]
            sy.dma_start(out=knall[:], in_=kn_ap,
                         allow_slow_non_contiguous=True).then_inc(LKN.h, 16)
            LKN.n += 16
        ld_kn = LKN.n

        @block.vector
        def _(v):
            v.wait_ge(LKN.h, ld_kn)
            VKR.inc(v.reduce_max(bks[:, 1:2], knall[:], axis=AXX))
        vkr_bk = VKR.n

        @block.scalar
        def _(sc):
            sc.wait_ge(VKR.h, vkr_bk)
            SKE.inc(sc.activation(bks[:, 2:3], bks[:, 1:2], AF.Sqrt))
        ske_bk = SKE.n

        @block.vector
        def _(v):
            v.wait_ge(SKE.h, ske_bk)
            VKR.inc(v.tensor_scalar_mul(bks[:, 3:4], bks[:, 2:3], SCALE))
        vkr_bk2 = VKR.n

        with nc.psum_tensor("pnq", [128, 128], F32) as pnq, \
             nc.psum_tensor("pbr", [1, 16], F32) as pbr, \
             nc.psum_tensor("pbb", [128, 16], F32) as pbb:
            TNC = Sem(ctx, nc, "TNC")

            @block.tensor
            def _(tn):
                tn.wait_ge(SKE.h, ske_qn)
                last = None
                for qt in range(NT):
                    last = tn.transpose(pnq[:, qt * 16:(qt + 1) * 16],
                                        qn2[:, qt * 128:(qt + 1) * 128],
                                        eyef[0:16, 0:16])
                TNC.inc(last)
                tn.wait_ge(VKR.h, vkr_bk2)
                TNC.inc(tn.transpose(pbr[:], bks[:, 3:4],
                                     eyef[0:16, 0:16]))
            tnc_t = TNC.n

            @block.scalar
            def _(sc):
                sc.wait_ge(TNC.h, tnc_t)
                sc.copy(qn2t[:], pnq[:])
                SKE.inc(sc.copy(bksrow[:], pbr[:]))
            ske_nt = SKE.n

            @block.tensor
            def _(tn):
                tn.wait_ge(SKE.h, ske_nt)
                TNC.inc(tn.matmul(pbb[:], onesf[:], bksrow[:],
                                  start=True, stop=True))
            tnc_bb = TNC.n

            @block.scalar
            def _(sc):
                sc.wait_ge(TNC.h, tnc_bb)
                sc.copy(bksb[:], pbb[:])
                SKE.inc(sc.activation(qn2t[:], qn2t[:], AF.Sqrt))
            ske_sq = SKE.n

            @block.vector
            def _(v):
                v.wait_ge(SKE.h, ske_sq)
                last = None
                for qt in range(NT):
                    last = v.scalar_tensor_tensor(
                        negcq[:, qt * H:(qt + 1) * H],
                        qn2t[:, qt * H:(qt + 1) * H], -1.0, bksb[:],
                        ALU.mult, ALU.mult)
                VKR.inc(last)
            vkr_ncq = VKR.n

        d_kv = dbg_out("kv", [KVN], BF16)
        if d_kv is not None:
            @block.sync
            def _(sy):
                sy.wait_ge(CC.h, cc_kv)
                _dma(sy, ST, d_kv, kv_in.ap())
        d_q = dbg_out("q", [H, 128, T], BF16)
        if d_q is not None:
            @block.sync
            def _(sy):
                sy.wait_ge(SKE.h, ske_qn)
                for hh in range(H):
                    _dma(sy, ST, d_q[hh], q_st.ap()[hh])
        d_tern = dbg_out("tern", [NC, 128, NSH], BF16)
        if d_tern is not None:
            @block.sync
            def _(sy):
                sy.wait_ge(CC.h, cc_all)
                for kc in range(NC):
                    _dma(sy, ST, d_tern[kc], tern_in[kc])
        d_gam = dbg_out("gam", [NSH], F32)
        if d_gam is not None:
            @block.sync
            def _(sy):
                sy.wait_ge(CC.h, cc_all)
                _dma(sy, ST, d_gam, gam_in.ap())
        d_ncq = dbg_out("ncq", [128, NT * H], F32)
        if d_ncq is not None:
            @block.sync
            def _(sy):
                sy.wait_ge(VKR.h, vkr_ncq)
                _dma(sy, ST, d_ncq, negcq[:])

        sC.close()
        sA.close()

        sD1 = ExitStack()
        eD1 = sD1.enter_context
        O_fm = eD1(nc.sbuf_tensor("O_fm", [128, H * T], BF16, side="right"))
        sD2 = ExitStack()
        eD2 = sD2.enter_context
        kh = [eD2(nc.sbuf_tensor(f"kh{i}", [128, S], BF16)) for i in range(2)]
        vh = [eD2(nc.sbuf_tensor(f"vh{i}", [128, S], BF16)) for i in range(2)]
        qh = [eD2(nc.sbuf_tensor(f"qh{i}", [128, T], BF16)) for i in range(2)]
        pbuf = eD2(nc.sbuf_tensor("pbuf", [128, 4 * 4096], BF16))
        ptb = [eD2(nc.sbuf_tensor(f"ptb{i}", [128, 512], BF16)) for i in range(2)]
        dacc = eD2(nc.sbuf_tensor("dacc", [128, 32], F32))
        dent = eD2(nc.sbuf_tensor("dent", [128, 8], F32))
        denrow = eD2(nc.sbuf_tensor("denrow", [4, 128], F32))
        rbs = eD2(nc.sbuf_tensor("rbs", [128, 512], F32))

        LKV = [Sem(ctx, nc, "LKV0"), Sem(ctx, nc, "LKV1")]
        ATS = Sem(ctx, nc, "ATS")
        AEX = Sem(ctx, nc, "AEX")
        ATT = Sem(ctx, nc, "ATT")
        ATE = Sem(ctx, nc, "ATE")
        APV = Sem(ctx, nc, "APV")
        ADN = Sem(ctx, nc, "ADN")
        TDT = Sem(ctx, nc, "TDT")
        SDR = Sem(ctx, nc, "SDR")
        TRB = Sem(ctx, nc, "TRB")
        ARB = Sem(ctx, nc, "ARB")
        AOE = Sem(ctx, nc, "AOE")

        gates = [(VKR, VKR.n), (SKE, SKE.n), (TQN, TQN.n), (SVN, SVN.n),
                 (STN, STN.n), (GD, GD.n), (CC, cc_kv)]
        gd_attn = GD.n

        @block.sync
        def _(sy):
            for sem, val in gates:
                sy.wait_ge(sem.h, val)
            for h in range(H):
                if h >= 2:
                    sy.wait_ge(APV.h, 128 * (h - 1))
                p = h % 2
                for r in range(CORES):
                    _dma(sy, LKV[p], kh[p][:, r * T:(r + 1) * T],
                         kv_g.ap()[r * KVN + h * 131072:
                                   r * KVN + (h + 1) * 131072]
                         .rearrange("(p f) -> p f", f=T))
                    _dma(sy, LKV[p], vh[p][:, r * T:(r + 1) * T],
                         kv_g.ap()[r * KVN + KSZ + h * 131072:
                                   r * KVN + KSZ + (h + 1) * 131072]
                         .rearrange("(p f) -> p f", f=T))
                _dma(sy, LKV[p], qh[p][:], q_st.ap()[h])

        with nc.psum_tensor("s2p", [128, 2 * T], F32) as s2p, \
             nc.psum_tensor("tpA", [128, T], BF16) as tpA, \
             nc.psum_tensor("tpB", [128, T], BF16) as tpB, \
             nc.psum_tensor("o_ps", [128, 512], F32) as o_ps, \
             nc.psum_tensor("pxtr", [128, 512], F32) as pxtr:
            tps = [tpA, tpB]

            NQG = 2 * H

            @block.tensor
            def _(tn):
                for h in range(H):
                    p = h % 2
                    tn.wait_ge(LKV[p].h, 272 * (h // 2 + 1))
                    for qg in range(2):
                        u = h * 2 + qg
                        ats0, aex0 = 32 * u, 32 * u
                        att0, ate0, apv0 = 64 * u, 64 * u, 64 * u

                        def tpv_batch(ksp):
                            for kt in range(8 * ksp, 8 * ksp + 8):
                                if kt == 8 * ksp:
                                    tn.wait_ge(AEX.h, aex0 + 4 * ksp + 4)
                                if ate0 + kt - 1 > 0:
                                    tn.wait_ge(ATE.h, ate0 + kt - 1)
                                last = None
                                for qt in range(4):
                                    last = tn.transpose(
                                        tps[kt % 2][:, qt * 128:(qt + 1) * 128],
                                        pbuf[:, qt * 4096 + (kt % 32) * 128:
                                             qt * 4096 + (kt % 32 + 1) * 128],
                                        eyeb[:])
                                ATT.inc(last)
                                pv_kt = kt - 1
                                if pv_kt >= 0:
                                    emit_pv(pv_kt)

                        def emit_pv(kt):
                            if kt == 0:
                                tn.wait_ge(AOE.h, u)
                            tn.wait_ge(ATE.h, ate0 + kt + 1)
                            APV.inc(tn.matmul(
                                o_ps[:], vh[p][:, kt * 128:(kt + 1) * 128],
                                ptb[kt % 2][:], start=(kt == 0),
                                stop=(kt == 63)))

                        for ks in range(8):
                            for qt in range(4):
                                ui = ks * 4 + qt
                                if aex0 + ui - 1 > 0:
                                    tn.wait_ge(AEX.h, aex0 + ui - 1)
                                last = None
                                for j in range(2):
                                    last = tn.matmul(
                                        s2p[:, (ui % 2) * T + j * 512:
                                            (ui % 2) * T + (j + 1) * 512],
                                        qh[p][:, qg * 512 + qt * 128:
                                              qg * 512 + (qt + 1) * 128],
                                        kh[p][:, ks * T + j * 512:
                                              ks * T + (j + 1) * 512],
                                        start=True, stop=True)
                                ATS.inc(last)
                            if ks >= 2:
                                tpv_batch(ks - 2)
                        tpv_batch(6)
                        tpv_batch(7)
                        emit_pv(63)
                        tn.wait_ge(ADN.h, u + 1)
                        TDT.inc(tn.transpose(pxtr[0:4, 0:128], dent[:, 4:8],
                                             eyef[:]))
                        tn.wait_ge(SDR.h, u + 1)
                        last = None
                        for qt in range(4):
                            last = tn.matmul(
                                pxtr[:, qt * 128:(qt + 1) * 128],
                                e4f[:, qt * 128:(qt + 1) * 128],
                                denrow[:], start=True, stop=True)
                        TRB.inc(last)

            @block.scalar
            def _(sc):
                sc.wait_ge(VKR.h, vkr_ncq)
                for h in range(H):
                    for qg in range(2):
                        u = h * 2 + qg
                        ats0, aex0, att0 = 32 * u, 32 * u, 64 * u
                        sc.wait_ge(ADN.h, u)
                        for ks in range(8):
                            for qt in range(4):
                                ui = ks * 4 + qt
                                sc.wait_ge(ATS.h, ats0 + ui + 1)
                                if ks >= 4 and qt == 0:
                                    sc.wait_ge(ATT.h, att0 + 8 * (ks - 3))
                                AEX.inc(sc.activation(
                                    pbuf[:, qt * 4096 + (ks % 4) * 1024:
                                         qt * 4096 + (ks % 4 + 1) * 1024],
                                    s2p[:, (ui % 2) * T:(ui % 2 + 1) * T],
                                    AF.Exp, scale=scalev[:, 0:1],
                                    bias=negcq[:, (qg * 4 + qt) * H + h:
                                               (qg * 4 + qt) * H + h + 1],
                                    accum_out=dacc[:, qt * 8 + ks:
                                                   qt * 8 + ks + 1]))
                        sc.wait_ge(TDT.h, u + 1)
                        SDR.inc(sc.copy(denrow[:], pxtr[0:4, 0:128]))
                        sc.wait_ge(TRB.h, u + 1)
                        ARB.inc(sc.copy(rbs[:], pxtr[:]))

            @block.vector
            def _(v):
                for h in range(H):
                    for qg in range(2):
                        u = h * 2 + qg
                        att0, ate0, apv0 = 64 * u, 64 * u, 64 * u
                        for kt in range(64):
                            v.wait_ge(ATT.h, att0 + kt + 1)
                            if apv0 + kt - 1 > 0:
                                v.wait_ge(APV.h, apv0 + kt - 1)
                            ATE.inc(v.tensor_copy(ptb[kt % 2][:],
                                                  tps[kt % 2][:, 0:512]))
                        v.wait_ge(AEX.h, 32 * u + 32)
                        if u >= 1:
                            v.wait_ge(TRB.h, u)
                        for qt in range(4):
                            v.reduce_sum(dent[:, qt:qt + 1],
                                         dacc[:, qt * 8:(qt + 1) * 8],
                                         axis=AXX)
                        v.drain()
                        ADN.inc(v.reciprocal(dent[:, 4:8], dent[:, 0:4]))
                        v.wait_ge(APV.h, apv0 + 64)
                        v.wait_ge(ARB.h, u + 1)
                        AOE.inc(v.tensor_mul(
                            O_fm[:, h * T + qg * 512:h * T + (qg + 1) * 512],
                            o_ps[:], rbs[:]))

        d_O = dbg_out("O", [128, H * T], BF16)
        if d_O is not None:
            @block.sync
            def _(sy):
                sy.wait_ge(AOE.h, AOE.n)
                _dma(sy, ST, d_O, O_fm[:])
        aoe_end = AOE.n
        sD2.close()

        x2 = e(nc.sbuf_tensor("x2", [128, NT * D], F32))
        sD3 = ExitStack()
        eD3 = sD3.enter_context
        attn_tm = eD3(nc.sbuf_tensor("attn_tm", [128, NT * D], BF16))
        wob = eD3(nc.sbuf_tensor("wob", [128, 2 * D], BF16))
        afm = eD3(nc.sbuf_tensor("afm", [128, T], BF16))
        xrb = eD3(nc.sbuf_tensor("xrb", [128, D], F32))

        LWO = [Sem(ctx, nc, "LWO0"), Sem(ctx, nc, "LWO1")]
        LXR = Sem(ctx, nc, "LXR")
        TWO = Sem(ctx, nc, "TWO")
        SWO = Sem(ctx, nc, "SWO")
        TWT = Sem(ctx, nc, "TWT")
        VWO = Sem(ctx, nc, "VWO")
        VX2 = Sem(ctx, nc, "VX2")

        sd_gab = bcast_row(mrow(MGA), slot[0], "gab", [(VD, vd_mod), (AOE, aoe_end)])
        sd_gabo = bcast_row(gabo4[:], slot[1], "gabo", [(VD, vd_mod), (AOE, aoe_end)])

        @block.gpsimd
        def _(gp):
            gp.wait_ge(AOE.h, aoe_end)
            for fc in range(NC):
                if fc >= 2:
                    gp.wait_ge(TWO.h, fc - 1)
                _dma(gp, LWO[fc % 2], wob[:, (fc % 2) * D:(fc % 2 + 1) * D],
                     wo4[fc])

        @block.sync
        def _(sy):
            sy.wait_ge(AOE.h, aoe_end)
            for tt in range(NT):
                if tt >= 1:
                    sy.wait_ge(VX2.h, tt)
                _dma(sy, LXR, xrb[:], x_in[tt * 128:(tt + 1) * 128, :])

        with nc.psum_tensor("pwo", [128, 2 * T], F32) as pwo, \
             nc.psum_tensor("ptw", [128, 2 * T], BF16) as ptw:
            def emit_wot(tn, fc):
                tn.wait_ge(SWO.h, fc + 1)
                if fc - 1 > 0:
                    tn.wait_ge(VWO.h, fc - 1)
                last = None
                for tc in range(NT):
                    last = tn.transpose(
                        ptw[:, (fc % 2) * T + tc * 128:
                            (fc % 2) * T + (tc + 1) * 128],
                        afm[:, tc * 128:(tc + 1) * 128], eyeb[:])
                TWT.inc(last)

            @block.tensor
            def _(tn):
                for fc in range(NC):
                    tn.wait_ge(LWO[fc % 2].h, 16 * (fc // 2 + 1))
                    if fc < 2:
                        tn.wait_ge(SD.h, max(sd_gab, sd_gabo))
                    if fc >= 2:
                        tn.wait_ge(SWO.h, fc - 1)
                    last = None
                    for tt2 in range(2):
                        for dc in range(NC):
                            last = tn.matmul(
                                pwo[:, (fc % 2) * T + tt2 * 512:
                                    (fc % 2) * T + (tt2 + 1) * 512],
                                wob[:, (fc % 2) * D + dc * 128:
                                    (fc % 2) * D + (dc + 1) * 128],
                                O_fm[:, dc * T + tt2 * 512:
                                     dc * T + (tt2 + 1) * 512],
                                start=(dc == 0), stop=(dc == NC - 1))
                    TWO.inc(last)
                    if fc >= 1:
                        emit_wot(tn, fc - 1)
                emit_wot(tn, NC - 1)

            @block.scalar
            def _(sc):
                for fc in range(NC):
                    sc.wait_ge(TWO.h, fc + 1)
                    if fc >= 1:
                        sc.wait_ge(TWT.h, fc)
                    SWO.inc(sc.copy(afm[:], pwo[:, (fc % 2) * T:
                                                (fc % 2 + 1) * T]))

            @block.vector
            def _(v):
                for fc in range(NC):
                    v.wait_ge(TWT.h, fc + 1)
                    at_view = attn_tm[:].rearrange(
                        "p (c f) -> p c f", c=NT)[:, :, fc * 128:(fc + 1) * 128]
                    VWO.inc(v.tensor_copy(
                        at_view, ptw[:, (fc % 2) * T:(fc % 2 + 1) * T]
                        .rearrange("p (c f) -> p c f", c=NT)))

        @block.vector
        def _(v):
            v.wait_ge(SD.h, max(sd_gab, sd_gabo))
            v.wait_ge(VWO.h, VWO.n)
            for tt in range(NT):
                v.wait_ge(LXR.h, 16 * (tt + 1))
                x2c = x2[:, tt * D:(tt + 1) * D]
                v.tensor_mul(x2c, attn_tm[:, tt * D:(tt + 1) * D], slot[0][:])
                v.tensor_add(x2c, x2c, slot[1][:])
                VX2.inc(v.tensor_add(x2c, x2c, xrb[:]))

        d_x2 = dbg_out("x2", [T, D], F32)
        if d_x2 is not None:
            @block.sync
            def _(sy):
                sy.wait_ge(VX2.h, VX2.n)
                for i in range(NT):
                    _dma(sy, ST, d_x2[i * 128:(i + 1) * 128, :],
                         x2[:, i * D:(i + 1) * D])

        sD3.close()
        sD1.close()

        sD4 = ExitStack()
        eD4 = sD4.enter_context
        xqf = eD4(nc.sbuf_tensor("xqf", [128, NC * T], BF16))
        tern_sb = eD4(nc.sbuf_tensor("tern_sb", [128, NC * 512], BF16))
        h2s = eD4(nc.sbuf_tensor("h2s", [128, D], F32))
        u9 = eD4(nc.sbuf_tensor("u9", [128, T], F32))
        v9b = eD4(nc.sbuf_tensor("v9b", [128, T], F32))
        d9 = eD4(nc.sbuf_tensor("d9", [128, T], F32))
        xqt = eD4(nc.sbuf_tensor("xqt", [128, D], BF16))
        fout = eD4(nc.sbuf_tensor("fout", [128, 512], F32))
        gbgb = eD4(nc.sbuf_tensor("gbgb", [128, 512], F32))
        gbfbb = eD4(nc.sbuf_tensor("gbfbb", [128, 512], F32))

        DXQ = Sem(ctx, nc, "DXQ")
        d_xqall = dbg_out("xqall", [NT, 128, D], BF16)
        LGG = Sem(ctx, nc, "LGG")
        LTH = Sem(ctx, nc, "LTH")
        S9 = Sem(ctx, nc, "S9")
        V9 = Sem(ctx, nc, "V9")
        T9 = Sem(ctx, nc, "T9")
        TF = Sem(ctx, nc, "TF")
        SF = Sem(ctx, nc, "SF")
        VF = Sem(ctx, nc, "VF")

        @block.sync
        def _(sy):
            sy.wait_ge(CC.h, cc_all)
            _dma(sy, LGG, grow[:], gam_g.ap().rearrange("(p b) -> p b", b=512))
        ld_gg = LGG.n

        @block.vector
        def _(v):
            v.wait_ge(LGG.h, ld_gg)
            v.wait_ge(VX2.h, VX2.n)
            v.tensor_mul(gbg4[:], mrow(MGB), grow[:])
            VD.inc(v.tensor_mul(gbfb4[:], mrow(MGB), fb4[:]))
        vd_gb = VD.n

        sd_cb1 = bcast_row(mrow(MCB), slot[0], "cb1", [(VD, vd_gb)])
        sd_sb = bcast_row(mrow(MSB), slot[1], "sbb", [(VD, vd_gb)])


        def load_tern_q(q, gate_sem, gate_val):
            @block.sync
            def _(sy):
                sy.wait_ge(CC.h, cc_all)
                if gate_sem is not None:
                    sy.wait_ge(gate_sem.h, gate_val)
                for kc in range(NC):
                    for r in range(2):
                        _dma(sy, LTH, tern_sb[:, kc * 512 + r * NSH:
                                              kc * 512 + (r + 1) * NSH],
                             tern_g[q * 2 + r, kc])
            return LTH.n

        ld_th0 = load_tern_q(0, None, 0)

        v9_xq_marks = []
        @block.vector
        def _(v):
            v.wait_ge(SD.h, max(sd_cb1, sd_sb))
            V9.inc(v.tensor_copy(stats[:, 0:1], stats[:, 0:1]))
        v9_bc = V9.n

        with nc.psum_tensor("pxq", [128, D], BF16) as pxq, \
             nc.psum_tensor("pf", [128, 2 * 512], F32) as pf:
            for tt in range(NT):
                v9_prev = V9.n

                @block.scalar
                def _(sc, tt=tt):
                    if tt == 0:
                        sc.wait_ge(V9.h, v9_bc)
                    sc.wait_ge(VX2.h, tt + 1)
                    sc.wait_ge(V9.h, v9_prev)
                    x2c = x2[:, tt * D:(tt + 1) * D]
                    S9.inc(sc.activation(h2s[:], x2c, AF.Square,
                                         accum_out=stats[:, 1:2]))
                s9_sq = S9.n

                @block.vector
                def _(v, tt=tt):
                    x2c = x2[:, tt * D:(tt + 1) * D]
                    v.wait_ge(S9.h, s9_sq)
                    v.reduce_sum(stats[:, 0:1], x2c, axis=AXX)
                    v.drain()
                    v.tensor_scalar_mul(stats[:, 2:3], stats[:, 0:1], 1.0 / D)
                    v.tensor_scalar_mul(stats[:, 3:4], stats[:, 1:2], 1.0 / D)
                    v.drain()
                    v.tensor_mul(stats[:, 4:5], stats[:, 2:3], stats[:, 2:3])
                    v.drain()
                    v.tensor_sub(stats[:, 3:4], stats[:, 3:4], stats[:, 4:5])
                    v.drain()
                    V9.inc(v.tensor_scalar_add(stats[:, 3:4], stats[:, 3:4],
                                               1e-5))
                v9_var = V9.n

                @block.scalar
                def _(sc, tt=tt):
                    sc.wait_ge(V9.h, v9_var)
                    S9.inc(sc.activation(stats[:, 4:5], stats[:, 3:4],
                                         AF.Sqrt))
                s9_sqrt = S9.n

                @block.vector
                def _(v, tt=tt):
                    v.wait_ge(S9.h, s9_sqrt)
                    v.reciprocal(stats[:, 5:6], stats[:, 4:5])
                    v.drain()
                    v.tensor_mul(stats[:, 6:7], stats[:, 2:3], stats[:, 5:6])
                    v.drain()
                    V9.inc(v.tensor_scalar_mul(stats[:, 6:7], stats[:, 6:7],
                                               -1.0))
                v9_rstd = V9.n

                s9_u0 = S9.n
                v9_h20 = V9.n

                @block.scalar
                def _(sc, tt=tt):
                    sc.wait_ge(V9.h, v9_rstd)
                    for jh in range(2):
                        if jh == 1:
                            sc.wait_ge(V9.h, v9_h20 + 1)
                        S9.inc(sc.activation(
                            u9[:], x2[:, tt * D + jh * T:tt * D + (jh + 1) * T],
                            AF.Identity, bias=stats[:, 6:7],
                            scale=stats[:, 5:6]))

                @block.vector
                def _(v, tt=tt):
                    for jh in range(2):
                        v.wait_ge(S9.h, s9_u0 + jh + 1)
                        hs = h2s[:, jh * T:(jh + 1) * T]
                        v.tensor_mul(hs, u9[:], slot[0][:, jh * T:(jh + 1) * T])
                        V9.inc(v.tensor_add(hs, hs,
                                            slot[1][:, jh * T:(jh + 1) * T]))
                v9_h2 = V9.n

                @block.scalar
                def _(sc, tt=tt):
                    sc.wait_ge(V9.h, v9_h2)
                    last = None
                    for jh in range(2):
                        last = sc.activation(u9[:], h2s[:, jh * T:(jh + 1) * T],
                                             AF.Square,
                                             accum_out=stats[:, 13 + jh:14 + jh])
                    S9.inc(last)
                s9_r2 = S9.n

                @block.vector
                def _(v, tt=tt):
                    v.wait_ge(S9.h, s9_r2)
                    v.tensor_add(stats[:, 7:8], stats[:, 13:14], stats[:, 14:15])
                    v.drain()
                    v.tensor_scalar(stats[:, 8:9], stats[:, 7:8], 1.0 / D,
                                    1e-8, ALU.mult, ALU.add)
                    V9.inc(v.engine_nop())
                v9_ra = V9.n

                @block.scalar
                def _(sc, tt=tt):
                    sc.wait_ge(V9.h, v9_ra)
                    S9.inc(sc.activation(stats[:, 8:9], stats[:, 8:9],
                                         AF.Sqrt))
                s9_rs = S9.n

                @block.vector
                def _(v, tt=tt):
                    v.wait_ge(S9.h, s9_rs)
                    V9.inc(v.reciprocal(stats[:, 9:10], stats[:, 8:9]))
                v9_ri = V9.n

                @block.scalar
                def _(sc, tt=tt):
                    sc.wait_ge(V9.h, v9_ri)
                    S9.inc(sc.activation(h2s[:], h2s[:], AF.Identity,
                                         scale=stats[:, 9:10]))
                s9_xn = S9.n

                t9_prev = T9.n

                @block.vector
                def _(v, tt=tt):
                    v.wait_ge(S9.h, s9_xn)
                    v.wait_ge(T9.h, t9_prev)
                    if d_xqall is not None and tt >= 1:
                        v.wait_ge(DXQ.h, 16 * tt)
                    v.reduce_max(stats[:, 10:11], h2s[:], axis=AXX,
                                 apply_absolute_value=True)
                    v.drain()
                    v.tensor_scalar(stats[:, 11:12], stats[:, 10:11],
                                    1.0 / 127.0, 1e-8, ALU.mult, ALU.add)
                    v.drain()
                    v.reciprocal(stats[:, 12:13], stats[:, 11:12])
                    v.drain()
                    v.tensor_scalar_mul(stats[:, 16 + tt:17 + tt],
                                        stats[:, 10:11], 1.0 / 127.0)
                    v.drain()
                    MAGIC = 12582912.0
                    last = None
                    for jh in range(2):
                        v.tensor_scalar(u9[:], h2s[:, jh * T:(jh + 1) * T],
                                        stats[:, 12:13], MAGIC,
                                        ALU.mult, ALU.add)
                        last = v.tensor_scalar_add(
                            xqt[:, jh * T:(jh + 1) * T], u9[:], -MAGIC)
                    V9.inc(last)
                v9_xq = V9.n

                @block.tensor
                def _(tn, tt=tt):
                    tn.wait_ge(V9.h, v9_xq)
                    last = None
                    for kc in range(NC):
                        last = tn.transpose(
                            pxq[:, kc * 128:(kc + 1) * 128],
                            xqt[:, kc * 128:(kc + 1) * 128], eyeb[:])
                    T9.inc(last)
                t9_t = T9.n

                if d_xqall is not None:
                    @block.sync
                    def _(sy, tt=tt):
                        sy.wait_ge(V9.h, v9_xq)
                        _dma(sy, DXQ, d_xqall[tt], xqt[:])

                @block.vector
                def _(v, tt=tt):
                    v.wait_ge(T9.h, t9_t)
                    xq_view = xqf[:].rearrange(
                        "p (c f) -> p c f", c=NC)[:, :, tt * 128:(tt + 1) * 128]
                    V9.inc(v.tensor_copy(
                        xq_view, pxq[:]
                        .rearrange("p (c f) -> p c f", c=NC)))
                v9_xq_marks.append(V9.n)

            st_ffn0 = ST.n
            SGQ = Sem(ctx, nc, "SGQ")
            TGQ = Sem(ctx, nc, "TGQ")
            for q in range(4):
                if q >= 1:
                    load_tern_q(q, TF, q * NT)

                with nc.psum_tensor(f"pgq{q}", [128, 2 * 512], F32) as pgq:
                    @block.tensor
                    def _(tn, q=q):
                        tn.wait_ge(VD.h, vd_gb)
                        if q >= 1:
                            tn.wait_ge(SGQ.h, 2 * q)
                        tn.matmul(pgq[:, 0:512],
                                  e4f[:, q * 128:(q + 1) * 128], gbg4[:],
                                  start=True, stop=True)
                        TGQ.inc(tn.matmul(pgq[:, 512:1024],
                                          e4f[:, q * 128:(q + 1) * 128],
                                          gbfb4[:], start=True, stop=True))

                    @block.scalar
                    def _(sc, q=q):
                        sc.wait_ge(TGQ.h, q + 1)
                        if q >= 1:
                            sc.wait_ge(VF.h, q * NT)
                        sc.copy(gbgb[:], pgq[:, 0:512])
                        SGQ.inc(sc.copy(gbfbb[:], pgq[:, 512:1024]), 2)

                for tt in range(NT):
                    @block.tensor
                    def _(tn, q=q, tt=tt):
                        u = q * NT + tt
                        tn.wait_ge(LTH.h, LTH.n)
                        tn.wait_ge(V9.h, v9_xq_marks[tt])
                        if u >= 2:
                            tn.wait_ge(SF.h, u - 1)
                        last = None
                        for kc in range(NC):
                            last = tn.matmul(
                                pf[:, (u % 2) * 512:(u % 2 + 1) * 512],
                                xqf[:, kc * T + tt * 128:
                                    kc * T + (tt + 1) * 128],
                                tern_sb[:, kc * 512:(kc + 1) * 512],
                                start=(kc == 0), stop=(kc == NC - 1))
                        TF.inc(last)

                    @block.scalar
                    def _(sc, q=q, tt=tt):
                        u = q * NT + tt
                        sc.wait_ge(TF.h, u + 1)
                        if u >= 1:
                            sc.wait_ge(VF.h, u)
                            sc.wait_ge(ST.h, st_ffn0 + 16 * u)
                        SF.inc(sc.activation(fout[:],
                                             pf[:, (u % 2) * 512:
                                                (u % 2 + 1) * 512],
                                             AF.Identity,
                                             scale=stats[:, 16 + tt:17 + tt]))

                    @block.vector
                    def _(v, q=q, tt=tt):
                        u = q * NT + tt
                        v.wait_ge(SF.h, u + 1)
                        v.wait_ge(SGQ.h, 2 * (q + 1))
                        v.tensor_mul(fout[:], fout[:], gbgb[:])
                        v.tensor_add(fout[:], fout[:], gbfbb[:])
                        VF.inc(v.tensor_add(
                            fout[:], fout[:],
                            x2[:, tt * D + q * 512:tt * D + (q + 1) * 512]))

                    @block.sync
                    def _(sy, q=q, tt=tt):
                        u = q * NT + tt
                        sy.wait_ge(VF.h, u + 1)
                        _dma(sy, ST, out_t.ap()[tt * 128:(tt + 1) * 128,
                                                q * 512:(q + 1) * 512],
                             fout[:])


        d_xqt = dbg_out("xqt", [128, D], BF16)
        if d_xqt is not None:
            @block.sync
            def _(sy):
                sy.wait_ge(V9.h, V9.n)
                _dma(sy, ST, d_xqt, xqt[:])

        d_xqf = dbg_out("xqf", [128, NC * T], BF16)
        if d_xqf is not None:
            @block.sync
            def _(sy):
                sy.wait_ge(V9.h, V9.n)
                _dma(sy, ST, d_xqf, xqf[:])

        d_st9 = dbg_out("st9", [128, 24], F32)
        if d_st9 is not None:
            @block.sync
            def _(sy):
                sy.wait_ge(V9.h, V9.n)
                _dma(sy, ST, d_st9, stats[:, 0:24])

        @block.sync
        def _(sy):
            sy.wait_ge(ST.h, ST.n)

        sD4.close()
    return nc


def _rope_tables():
    inv_freq = 1.0 / 10000.0 ** (np.arange(0, HD, 2, dtype=np.float32) / HD)
    pos = np.arange(S, dtype=np.float32)
    freqs = np.outer(pos, inv_freq)
    emb = np.concatenate([freqs, freqs], axis=-1)
    return np.cos(emb).astype(np.float32), np.sin(emb).astype(np.float32)


def _prep_inputs(inputs):
    x = np.asarray(inputs["L0_attn_x"], np.float32)[0]
    WqT = np.asarray(inputs["L0_attn_Wq"], np.float32).T
    WkT = np.asarray(inputs["L0_attn_Wk"], np.float32).T
    WvT = np.asarray(inputs["L0_attn_Wv"], np.float32).T
    WoT = np.asarray(inputs["L0_attn_Wo"], np.float32).T

    def w4(WT):
        return np.ascontiguousarray(
            WT.reshape(NC, 128, NC, 128).transpose(2, 1, 0, 3))

    cos_t, sin_t = _rope_tables()
    common = {
        "wq4": w4(WqT),
        "wk4": w4(WkT),
        "wo4": w4(WoT),
        "wv_r": np.ascontiguousarray(WvT.reshape(NC, 128, D)),
        "bqm": np.ascontiguousarray(
            np.asarray(inputs["L0_attn_bq"], np.float32).reshape(NC, 128).T),
        "bkm": np.ascontiguousarray(
            np.asarray(inputs["L0_attn_bk"], np.float32).reshape(NC, 128).T),
        "bvrow": np.asarray(inputs["L0_attn_bv"], np.float32).reshape(4, 512),
        "borow": np.asarray(inputs["L0_attn_bo"], np.float32).reshape(4, 512),
        "fbrow": np.asarray(inputs["L0_ffn_bias"], np.float32).reshape(4, 512),
        "cm": np.ascontiguousarray(
            np.asarray(inputs["L0_c"], np.float32).reshape(NC, 128).T),
        "eyeb": np.eye(128, dtype=np.float32).astype(ml_dtypes.bfloat16),
        "eyef": np.eye(128, dtype=np.float32),
        "onesf": np.ones((1, 128), np.float32),
        "e4f": _e4f(),
        "ehb": _ehb(),
    }
    fw = np.asarray(inputs["L0_ffn_weight"], np.float32)
    wmod = np.asarray(inputs["L0_wmod"], np.float32)
    in_maps = []
    for r in range(CORES):
        m = dict(common)
        m["x"] = np.ascontiguousarray(x[r * T:(r + 1) * T])
        m["fws"] = np.ascontiguousarray(
            fw[r * NSH:(r + 1) * NSH].reshape(2, 128, D))
        m["wmods"] = np.ascontiguousarray(
            wmod[:, r * JSH:(r + 1) * JSH].reshape(NC, 128, JSH))
        m["cosm"] = np.ascontiguousarray(cos_t[r * T:(r + 1) * T].T)
        m["sinm"] = np.ascontiguousarray(sin_t[r * T:(r + 1) * T].T)
        in_maps.append(m)
    return in_maps


def _e4f():
    m = np.zeros((4, 512), np.float32)
    for jt in range(4):
        m[jt, jt * 128:(jt + 1) * 128] = 1.0
    return m


def _ehb():
    e_all = np.zeros((128, H * H), np.float32)
    for oc in range(H):
        e_all[:, oc * H + oc] = 1.0
    return e_all.astype(ml_dtypes.bfloat16)


def kernel(**inputs):
    in_maps = _prep_inputs(inputs)
    nc = build_nc()
    res = run_bass_kernel_spmd(nc, in_maps, core_ids=list(range(CORES)))
    out = np.concatenate([np.asarray(r["out"]) for r in res.results], axis=0)
    return out.reshape(1, S, D).astype(np.float32)


# version 15
# speedup vs baseline: 1.1114x; 1.0272x over previous
import numpy as np
import ml_dtypes
from contextlib import ExitStack

import concourse.bass as bass
import concourse.mybir as mybir
from concourse.bass_utils import run_bass_kernel_spmd

F32 = mybir.dt.float32
BF16 = mybir.dt.bfloat16
AF = mybir.ActivationFunctionType
ALU = mybir.AluOpType
AXX = mybir.AxisListType.X

CORES = 8
T = 1024
D = 2048
H = 16
HD = 128
S = 8192
NT = T // 128
NC = D // 128
JSH = 6 * D // CORES
NSH = D // CORES
SCALE = float(HD) ** -0.5

_DEBUG = []


class Sem:
    def __init__(self, ctx, nc, name):
        self.h = ctx.enter_context(nc.semaphore(name))
        self.n = 0

    def inc(self, inst, k=1):
        inst.then_inc(self.h, k)
        self.n += k
        return self.n


def _dma(eng, sem, out, in_):
    eng.dma_start(out=out, in_=in_).then_inc(sem.h, 16)
    sem.n += 16
    return sem.n


def build_nc():
    nc = bass.Bass("TRN2", target_bir_lowering=False,
                   detect_race_conditions=False)

    x_in = nc.dram_tensor("x", [T, D], F32, kind="ExternalInput").ap()
    wq4 = nc.dram_tensor("wq4", [NC, 128, NC, 128], F32, kind="ExternalInput").ap()
    wk4 = nc.dram_tensor("wk4", [NC, 128, NC, 128], F32, kind="ExternalInput").ap()
    wo4 = nc.dram_tensor("wo4", [NC, 128, NC, 128], F32, kind="ExternalInput").ap()
    wv_r = nc.dram_tensor("wv_r", [NC, 128, D], F32, kind="ExternalInput").ap()
    bqm_in = nc.dram_tensor("bqm", [128, NC], F32, kind="ExternalInput").ap()
    bkm_in = nc.dram_tensor("bkm", [128, NC], F32, kind="ExternalInput").ap()
    bvrow_in = nc.dram_tensor("bvrow", [4, 512], F32, kind="ExternalInput").ap()
    borow_in = nc.dram_tensor("borow", [4, 512], F32, kind="ExternalInput").ap()
    fbrow_in = nc.dram_tensor("fbrow", [4, 512], F32, kind="ExternalInput").ap()
    fws_in = nc.dram_tensor("fws", [2, 128, D], F32, kind="ExternalInput").ap()
    cm_in = nc.dram_tensor("cm", [128, NC], F32, kind="ExternalInput").ap()
    wmods_in = nc.dram_tensor("wmods", [NC, 128, JSH], F32, kind="ExternalInput").ap()
    cos_in = nc.dram_tensor("cosm", [128, T], F32, kind="ExternalInput").ap()
    sin_in = nc.dram_tensor("sinm", [128, T], F32, kind="ExternalInput").ap()
    eyeb_in = nc.dram_tensor("eyeb", [128, 128], BF16, kind="ExternalInput").ap()
    eyef_in = nc.dram_tensor("eyef", [128, 128], F32, kind="ExternalInput").ap()
    onesf_in = nc.dram_tensor("onesf", [1, 128], F32, kind="ExternalInput").ap()
    e4f_in = nc.dram_tensor("e4f", [4, 512], F32, kind="ExternalInput").ap()
    ehb_in = nc.dram_tensor("ehb", [128, H * H], BF16, kind="ExternalInput").ap()

    out_t = nc.dram_tensor("out", [T, D], F32, kind="ExternalOutput")

    KSZ = NC * 128 * T
    VSZ = H * 128 * NT * 128
    KVN = KSZ + VSZ + H
    kv_in_k = nc.dram_tensor("kv_in_k", [KSZ + H], BF16)
    kv_in_v = nc.dram_tensor("kv_in_v", [VSZ], BF16)
    kvg_k = nc.dram_tensor("kvg_k", [CORES * (KSZ + H)], BF16,
                           addr_space="Shared")
    kvg_v = nc.dram_tensor("kvg_v", [CORES * VSZ], BF16,
                           addr_space="Shared")
    q_st = nc.dram_tensor("q_st", [H, 128, T], BF16)
    mod_in = nc.dram_tensor("mod_in", [JSH], F32)
    mod_g = nc.dram_tensor("mod_g", [CORES * JSH], F32, addr_space="Shared")
    gam_in = nc.dram_tensor("gam_in", [NSH], F32)
    gam_g = nc.dram_tensor("gam_g", [CORES * NSH], F32, addr_space="Shared")
    tern_in = nc.dram_tensor("tern_in", [NC, 128, NSH], BF16)
    tern_g = nc.dram_tensor("tern_g", [CORES, NC, 128, NSH], BF16, addr_space="Shared")

    dbg = {}
    def dbg_out(name, shape, dtype=F32):
        if name in _DEBUG:
            dbg[name] = nc.dram_tensor("dbg_" + name, shape, dtype,
                                       kind="ExternalOutput").ap()
            return dbg[name]
        return None

    ctx = ExitStack()
    with ctx:
        e = ctx.enter_context
        eyeb = e(nc.sbuf_tensor("sb_eyeb", [128, 128], BF16))
        eyef = e(nc.sbuf_tensor("sb_eyef", [128, 128], F32))
        onesf = e(nc.sbuf_tensor("sb_onesf", [1, 128], F32))
        e4f = e(nc.sbuf_tensor("sb_e4f", [4, 512], F32))
        ehb = e(nc.sbuf_tensor("sb_ehb", [128, H * H], BF16))
        bqm = e(nc.sbuf_tensor("sb_bqm", [128, NC], F32))
        bkm = e(nc.sbuf_tensor("sb_bkm", [128, NC], F32))
        def row4(name):
            return e(nc.sbuf_tensor(name, [4, 512], F32))
        bv4, bo4, fb4, gabo4, gbg4, gbfb4, rs0, rs1 = (
            row4(n) for n in ("bv4", "bo4", "fb4", "gabo4", "gbg4",
                              "gbfb4", "rs0", "rs1"))
        modr = [e(nc.sbuf_tensor(f"modr{m}", [4, 512], F32)) for m in range(6)]
        slot = [e(nc.sbuf_tensor(f"slot{i}", [128, D], F32)) for i in range(2)]
        negcq = e(nc.sbuf_tensor("negcq", [128, NT * H], F32))
        stats = e(nc.sbuf_tensor("stats", [128, 64], F32))
        kn2 = e(nc.sbuf_tensor("kn2", [H, T], F32))
        qn2 = e(nc.sbuf_tensor("qn2", [H, T], F32))
        bks = e(nc.sbuf_tensor("bks", [H, 8], F32))
        gam2 = e(nc.sbuf_tensor("gam2", [128, 4], F32))
        scalev = e(nc.sbuf_tensor("scalev", [128, 1], F32))
        grow = e(nc.sbuf_tensor("grow", [4, 512], F32))

        GD = Sem(ctx, nc, "GD")
        CC = Sem(ctx, nc, "CC")
        MM = Sem(ctx, nc, "MM")
        VD = Sem(ctx, nc, "VD")
        SD = Sem(ctx, nc, "SD")
        ST = Sem(ctx, nc, "ST")
        LSM = Sem(ctx, nc, "LSM")
        LX = Sem(ctx, nc, "LX")
        LWM = Sem(ctx, nc, "LWM")
        LMOD = Sem(ctx, nc, "LMOD")
        STG = Sem(ctx, nc, "STG")

        block = e(nc.Block())

        @block.vector
        def _(v):
            v.memset(scalev[:], SCALE)

        sA = ExitStack()
        eA = sA.enter_context
        h1f = eA(nc.sbuf_tensor("h1f", [128, NC * 1024], BF16))
        cosm = eA(nc.sbuf_tensor("sb_cosm", [128, T], F32))
        sinm = eA(nc.sbuf_tensor("sb_sinm", [128, T], F32))
        sB = ExitStack()
        eB = sB.enter_context
        xbuf = eB(nc.sbuf_tensor("xbuf", [128, 4 * D], F32))
        h1t = eB(nc.sbuf_tensor("h1t", [128, 4 * D], BF16))
        wmb = eB(nc.sbuf_tensor("wmb", [128, 2 * JSH], F32))
        scr = eB(nc.sbuf_tensor("scr", [128, D], F32))
        cm_sb = eB(nc.sbuf_tensor("cm_sb", [128, NC], F32))
        scm = eB(nc.sbuf_tensor("scm", [128, NC], F32))
        modsl = eB(nc.sbuf_tensor("modsl", [1, JSH], F32))

        @block.sync
        def _(sy):
            _dma(sy, LSM, cm_sb[:], cm_in)
            _dma(sy, LSM, cosm[:], cos_in)
            _dma(sy, LSM, sinm[:], sin_in)
            _dma(sy, LSM, bqm[:], bqm_in)
            _dma(sy, LSM, bkm[:], bkm_in)
            _dma(sy, LSM, bv4[:], bvrow_in)
            _dma(sy, LSM, bo4[:], borow_in)
            _dma(sy, LSM, fb4[:], fbrow_in)
            _dma(sy, LSM, eyeb[:], eyeb_in)
            _dma(sy, LSM, eyef[:], eyef_in)
            _dma(sy, LSM, onesf[:], onesf_in)
            _dma(sy, LSM, e4f[:], e4f_in)
            _dma(sy, LSM, ehb[:], ehb_in)
        ld_p0 = LSM.n

        @block.sync
        def _(sy):
            for i in range(4):
                _dma(sy, LX, xbuf[:, i * D:(i + 1) * D],
                     x_in[i * 128:(i + 1) * 128, :])
        ld_xw = [LX.n]

        with nc.psum_tensor("pmod", [1, JSH], F32) as pmod:
            @block.scalar
            def _(sc):
                sc.wait_ge(LSM.h, ld_p0)
                SD.inc(sc.activation(scm[:], cm_sb[:], AF.Silu))
            sd_silu = SD.n

            ld_wq = []
            mm_wq = []
            for stp in range(8):
                @block.sync
                def _(sy, stp=stp):
                    if stp >= 1:
                        sy.wait_ge(MM.h, mm_wq[stp - 1])
                    for i in range(2):
                        _dma(sy, LWM, wmb[:, i * JSH:(i + 1) * JSH],
                             wmods_in[stp * 2 + i])
                ld_wq.append(LWM.n)

                @block.tensor
                def _(tn, stp=stp):
                    tn.wait_ge(LWM.h, ld_wq[stp])
                    if stp == 0:
                        tn.wait_ge(SD.h, sd_silu)
                    last = None
                    for i in range(2):
                        ic = stp * 2 + i
                        for jt in range(JSH // 512):
                            last = tn.matmul(pmod[:, jt * 512:(jt + 1) * 512],
                                             scm[:, ic:ic + 1],
                                             wmb[:, i * JSH + jt * 512:
                                                 i * JSH + (jt + 1) * 512],
                                             start=(ic == 0), stop=(ic == NC - 1))
                    MM.inc(last)
                mm_wq.append(MM.n)
            mm_mod = MM.n

            @block.scalar
            def _(sc):
                sc.wait_ge(MM.h, mm_mod)
                SD.inc(sc.copy(modsl[:], pmod[:]))
            sd_modsl = SD.n

        @block.gpsimd
        def _(gp):
            gp.wait_ge(SD.h, sd_modsl)
            _dma(gp, GD, mod_in.ap().rearrange("(a b) -> a b", a=1), modsl[:])
            gp.wait_ge(GD.h, GD.n)
            CC.inc(gp.collective_compute(
                "AllGather", ALU.bypass,
                replica_groups=[list(range(CORES))],
                ins=[mod_in.ap()], outs=[mod_g.ap()]))
        cc_mod = CC.n

        @block.sync
        def _(sy):
            sy.wait_ge(CC.h, cc_mod)
            for m in range(6):
                _dma(sy, LMOD, modr[m][:],
                     mod_g.ap()[m * D:(m + 1) * D].rearrange("(p b) -> p b", b=512))
        ld_mod = LMOD.n

        MSA, MCA, MGA, MSB, MCB, MGB = range(6)

        def mrow(m):
            return modr[m][:]

        @block.vector
        def _(v):
            v.wait_ge(LMOD.h, ld_mod)
            v.tensor_scalar_add(mrow(MCA), mrow(MCA), 1.0)
            v.tensor_scalar_add(mrow(MCB), mrow(MCB), 1.0)
            for mg in (MGA, MGB):
                g = mrow(mg)
                v.tensor_scalar(rs0[:], g, 0.0, None, ALU.is_ge)
                v.tensor_scalar(rs0[:], rs0[:], 2.0, -1.0, ALU.mult, ALU.add)
                v.tensor_scalar_mul(rs1[:], g, -1.0)
                v.tensor_max(rs1[:], rs1[:], g)
                v.tensor_scalar_max(rs1[:], rs1[:], 0.25)
                v.tensor_mul(g, rs0[:], rs1[:])
            VD.inc(v.tensor_mul(gabo4[:], mrow(MGA), bo4[:]))
        vd_mod = VD.n

        def bcast_row(row4_ap, dst, tag, wait_sems):
            with nc.psum_tensor("pb_" + tag, [128, D], F32) as pb:
                prev_sd = SD.n

                @block.tensor
                def _(tn):
                    tn.wait_ge(SD.h, prev_sd)
                    for sem, val in wait_sems:
                        tn.wait_ge(sem.h, val)
                    last = None
                    for jt in range(4):
                        last = tn.matmul(pb[:, jt * 512:(jt + 1) * 512],
                                         e4f[:, jt * 128:(jt + 1) * 128],
                                         row4_ap[:], start=True, stop=True)
                    MM.inc(last)
                mmv = MM.n

                @block.scalar
                def _(sc):
                    sc.wait_ge(MM.h, mmv)
                    SD.inc(sc.copy(dst[:], pb[:]))
            return SD.n

        sd_ca1 = bcast_row(mrow(MCA), slot[0], "ca1", [(VD, vd_mod)])
        sd_sa = bcast_row(mrow(MSA), slot[1], "sa", [(VD, vd_mod)])

        vd_h1w = []
        dbg_st_gate = []
        for w in range(2):
            cs = range(w * 4, w * 4 + 4)

            if w == 1:
                @block.sync
                def _(sy):
                    sy.wait_ge(VD.h, vd_h1w[0])
                    for i in range(4):
                        _dma(sy, LX, xbuf[:, i * D:(i + 1) * D],
                             x_in[(4 + i) * 128:(5 + i) * 128, :])
                ld_xw.append(LX.n)

            @block.vector
            def _(v, w=w):
                v.wait_ge(LX.h, ld_xw[w])
                if w == 1 and dbg_st_gate:
                    v.wait_ge(ST.h, dbg_st_gate[-1])
                last = None
                for i in cs:
                    xc = xbuf[:, (i % 4) * D:((i % 4) + 1) * D]
                    last = v.reduce_sum(stats[:, i:i + 1], xc, axis=AXX)
                VD.inc(last)
            vd_s1 = VD.n

            @block.scalar
            def _(sc, w=w):
                sc.wait_ge(LX.h, ld_xw[w])
                last = None
                for i in cs:
                    xc = xbuf[:, (i % 4) * D:((i % 4) + 1) * D]
                    last = sc.activation(scr[:], xc, AF.Square,
                                         accum_out=stats[:, 8 + i:9 + i])
                SD.inc(last)
            sd_s2 = SD.n

            a, b = w * 4, w * 4 + 4

            @block.vector
            def _(v):
                v.wait_ge(VD.h, vd_s1)
                v.wait_ge(SD.h, sd_s2)
                v.tensor_scalar_mul(stats[:, 16 + a:16 + b], stats[:, a:b], 1.0 / D)
                v.tensor_scalar_mul(stats[:, 24 + a:24 + b], stats[:, 8 + a:8 + b], 1.0 / D)
                v.drain()
                v.tensor_mul(stats[:, 32 + a:32 + b], stats[:, 16 + a:16 + b],
                             stats[:, 16 + a:16 + b])
                v.drain()
                v.tensor_sub(stats[:, 24 + a:24 + b], stats[:, 24 + a:24 + b],
                             stats[:, 32 + a:32 + b])
                v.drain()
                VD.inc(v.tensor_scalar_add(stats[:, 24 + a:24 + b],
                                           stats[:, 24 + a:24 + b], 1e-5))
            vd_var = VD.n

            @block.scalar
            def _(sc):
                sc.wait_ge(VD.h, vd_var)
                SD.inc(sc.activation(stats[:, 32 + a:32 + b],
                                     stats[:, 24 + a:24 + b], AF.Sqrt))
            sd_sqrt = SD.n

            @block.vector
            def _(v):
                v.wait_ge(SD.h, sd_sqrt)
                v.reciprocal(stats[:, 40 + a:40 + b], stats[:, 32 + a:32 + b])
                v.drain()
                v.tensor_mul(stats[:, 48 + a:48 + b], stats[:, 16 + a:16 + b],
                             stats[:, 40 + a:40 + b])
                v.drain()
                VD.inc(v.tensor_scalar_mul(stats[:, 48 + a:48 + b],
                                           stats[:, 48 + a:48 + b], -1.0))
            vd_rstd = VD.n

            @block.scalar
            def _(sc):
                sc.wait_ge(VD.h, vd_rstd)
                last = None
                for i in cs:
                    last = sc.activation(xbuf[:, (i % 4) * D:((i % 4) + 1) * D],
                                         xbuf[:, (i % 4) * D:((i % 4) + 1) * D],
                                         AF.Identity,
                                         bias=stats[:, 48 + i:49 + i],
                                         scale=stats[:, 40 + i:41 + i])
                SD.inc(last)
            sd_u = SD.n

            d_h1 = dbg_out("h1", [T, D], BF16) if w == 0 else d_h1

            @block.vector
            def _(v):
                v.wait_ge(SD.h, sd_u)
                v.wait_ge(SD.h, max(sd_ca1, sd_sa))
                last = None
                for i in cs:
                    u = xbuf[:, (i % 4) * D:((i % 4) + 1) * D]
                    v.tensor_mul(u, u, slot[0][:])
                    last = v.tensor_add(h1t[:, (i % 4) * D:((i % 4) + 1) * D],
                                        u, slot[1][:])
                VD.inc(last)
            vd_h1w.append(VD.n)
            vd_h1wv = VD.n

            if d_h1 is not None:
                @block.sync
                def _(sy):
                    sy.wait_ge(VD.h, vd_h1wv)
                    for i in cs:
                        _dma(sy, ST, d_h1[i * 128:(i + 1) * 128, :],
                             h1t[:, (i % 4) * D:((i % 4) + 1) * D])
                dbg_st_gate.append(ST.n)

            with nc.psum_tensor(f"pt1a{w}", [128, 512], BF16) as pt1a, \
                 nc.psum_tensor(f"pt1b{w}", [128, 512], BF16) as pt1b:
                pts = [pt1a, pt1b]

                mm_t1base = MM.n

                @block.tensor
                def _(tn):
                    tn.wait_ge(VD.h, vd_h1wv)
                    for fc in range(NC):
                        if fc >= 2:
                            tn.wait_ge(VD.h, vd_h1wv + fc - 1)
                        p = pts[fc % 2]
                        last = None
                        for i in range(4):
                            last = tn.transpose(p[:, i * 128:(i + 1) * 128],
                                                h1t[:, i * D + fc * 128:
                                                    i * D + (fc + 1) * 128],
                                                eyeb[:])
                        MM.inc(last)
                mm_t1 = MM.n

                @block.vector
                def _(v):
                    for fc in range(NC):
                        v.wait_ge(MM.h, mm_t1base + fc + 1)
                        VD.inc(v.tensor_copy(
                            h1f[:, fc * T + w * 512: fc * T + (w + 1) * 512],
                            pts[fc % 2][:]))
                vd_h1w[-1] = VD.n

        vd_t1 = VD.n

        d_stats = dbg_out("stats", [128, 64], F32)
        if d_stats is not None:
            @block.sync
            def _(sy):
                sy.wait_ge(VD.h, vd_t1)
                _dma(sy, ST, d_stats, stats[:])

        d_slot0 = dbg_out("slot0", [128, D], F32)
        if d_slot0 is not None:
            @block.sync
            def _(sy):
                sy.wait_ge(VD.h, vd_t1)
                _dma(sy, ST, d_slot0, slot[0][:])

        d_slot1 = dbg_out("slot1", [128, D], F32)
        if d_slot1 is not None:
            @block.sync
            def _(sy):
                sy.wait_ge(VD.h, vd_t1)
                _dma(sy, ST, d_slot1, slot[1][:])

        sB.close()

        sC = ExitStack()
        eC = sC.enter_context
        wkb = eC(nc.sbuf_tensor("wkb", [128, 2 * D], BF16))
        bvb = eC(nc.sbuf_tensor("bvb", [128, D], F32))
        qf = eC(nc.sbuf_tensor("qf", [128, T], F32))
        rot = eC(nc.sbuf_tensor("rot", [128, T], F32))
        ta_b = eC(nc.sbuf_tensor("ta_b", [128, T], F32))
        kbf = eC(nc.sbuf_tensor("kbf", [128, T], BF16))
        k2b = eC(nc.sbuf_tensor("k2b", [128, T], BF16))
        knb = eC(nc.sbuf_tensor("knb", [16, 1], BF16))
        sCV = ExitStack()
        eCV = sCV.enter_context
        wvb = eCV(nc.sbuf_tensor("wvb", [128, NC * D], BF16))
        vtm = eCV(nc.sbuf_tensor("vtm", [128, D], BF16))

        LWK = Sem(ctx, nc, "LWK")
        LWV = Sem(ctx, nc, "LWV")
        TSK = Sem(ctx, nc, "TSK")
        SKE = Sem(ctx, nc, "SKE")
        VKR = Sem(ctx, nc, "VKR")
        TKN = Sem(ctx, nc, "TKN")
        TV = Sem(ctx, nc, "TV")
        VV = Sem(ctx, nc, "VV")

        LWKP = [Sem(ctx, nc, "LWKP0"), Sem(ctx, nc, "LWKP1")]

        @block.gpsimd
        def _(gp):
            gp.wait_ge(VD.h, vd_t1)
            gp.wait_ge(MM.h, mm_mod)

        sd_bv = bcast_row(bv4[:], bvb, "bv", [(LSM, ld_p0), (VD, vd_t1),
                                              (MM, mm_mod)])

        def kq_phase(w4_in, bias_m, stage_region, tag, qn_psum_dst, norm_sem_t):
            tsk0, ske0, vkr0 = TSK.n, SKE.n, VKR.n
            nrm0 = norm_sem_t.n
            gd0 = GD.n
            lwk0 = [LWKP[0].n, LWKP[1].n]
            with nc.psum_tensor("pqk_" + tag, [128, 2 * T], F32) as pqk:
                @block.gpsimd
                def _(gp):
                    for oc in range(NC):
                        if oc >= 2:
                            gp.wait_ge(TSK.h, tsk0 + oc - 1)
                        _dma(gp, LWKP[oc % 2],
                             wkb[:, (oc % 2) * D:(oc % 2 + 1) * D],
                             w4_in[oc])
                        if oc >= 1:
                            gp.wait_ge(VKR.h, vkr0 + oc)
                            _dma(gp, GD, stage_region(oc - 1), kbf[:])
                    gp.wait_ge(VKR.h, vkr0 + NC)
                    _dma(gp, GD, stage_region(NC - 1), kbf[:])

                def knorm(tn, oc):
                    tn.wait_ge(VKR.h, vkr0 + oc + 1)
                    last = None
                    for tt in range(2):
                        last = tn.matmul(
                            qn_psum_dst[:, tt * 512:(tt + 1) * 512],
                            ehb[:, oc * H:(oc + 1) * H],
                            k2b[:, tt * 512:(tt + 1) * 512],
                            start=(oc == 0), stop=(oc == NC - 1))
                    norm_sem_t.inc(last)

                @block.tensor
                def _(tn):
                    for oc in range(NC):
                        tn.wait_ge(LWKP[oc % 2].h,
                                   lwk0[oc % 2] + 16 * (oc // 2 + 1))
                        tn.wait_ge(SKE.h, max(ske0 + oc - 1, ske0))
                        tn.wait_ge(SD.h, SD.n)
                        last = None
                        for tt in range(2):
                            for ic in range(NC):
                                last = tn.matmul(
                                    pqk[:, (oc % 2) * T + tt * 512:
                                        (oc % 2) * T + (tt + 1) * 512],
                                    wkb[:, (oc % 2) * D + ic * 128:
                                        (oc % 2) * D + (ic + 1) * 128],
                                    h1f[:, ic * T + tt * 512:
                                        ic * T + (tt + 1) * 512],
                                    start=(ic == 0), stop=(ic == NC - 1))
                        TSK.inc(last)
                        if oc >= 2:
                            knorm(tn, oc - 2)
                    knorm(tn, NC - 2)
                    knorm(tn, NC - 1)

                @block.scalar
                def _(sc):
                    for oc in range(NC):
                        sc.wait_ge(TSK.h, tsk0 + oc + 1)
                        if oc >= 1:
                            sc.wait_ge(VKR.h, vkr0 + oc)
                        SKE.inc(sc.activation(
                            qf[:], pqk[:, (oc % 2) * T:(oc % 2 + 1) * T],
                            AF.Identity, bias=bias_m[:, oc:oc + 1]))

                @block.vector
                def _(v):
                    for oc in range(NC):
                        v.wait_ge(SKE.h, ske0 + oc + 1)
                        v.wait_ge(GD.h, gd0 + 16 * oc)
                        if oc >= 1:
                            v.wait_ge(norm_sem_t.h, nrm0 + oc)
                        v.tensor_scalar_mul(rot[0:64, :], qf[64:128, :], -1.0)
                        v.tensor_copy(rot[64:128, :], qf[0:64, :])
                        v.tensor_mul(ta_b[:], qf[:], cosm[:])
                        v.tensor_mul(rot[:], rot[:], sinm[:])
                        v.tensor_add(kbf[:], ta_b[:], rot[:])
                        VKR.inc(v.tensor_mul(k2b[:], kbf[:], kbf[:]))
            return tsk0, ske0, vkr0

        with nc.psum_tensor("pkn", [16, T], F32) as pkn:
            kq_phase_gd0 = GD.n
            tsk0, ske0, vkr0 = kq_phase(
                wk4, bkm,
                lambda oc: kv_in_k.ap()[oc * 131072:(oc + 1) * 131072]
                .rearrange("(p f) -> p f", f=T),
                "k", pkn, TKN)

            @block.scalar
            def _(sc):
                sc.wait_ge(TKN.h, TKN.n)
                SKE.inc(sc.copy(kn2[:], pkn[:]))
            ske_kn = SKE.n

        @block.vector
        def _(v):
            v.wait_ge(SKE.h, ske_kn)
            v.reduce_max(bks[:, 0:1], kn2[:], axis=AXX)
            v.drain()
            VKR.inc(v.tensor_copy(knb[:], bks[:, 0:1]))
        vkr_kn = VKR.n

        @block.gpsimd
        def _(gp):
            gp.wait_ge(VKR.h, vkr_kn)
            _dma(gp, GD, kv_in_k.ap()[KSZ:KSZ + H]
                 .rearrange("(a b) -> a b", a=16), knb[:])

        @block.gpsimd
        def _(gp):
            gp.wait_ge(GD.h, GD.n)
            CC.inc(gp.collective_compute(
                "AllGather", ALU.bypass,
                replica_groups=[list(range(CORES))],
                ins=[kv_in_k.ap()], outs=[kvg_k.ap()]))
        cc_agk = CC.n

        @block.gpsimd
        def _(gp):
            for ic in range(NC):
                _dma(gp, LWV, wvb[:, ic * D:(ic + 1) * D], wv_r[ic])
        ld_wv = LWV.n

        vr = kv_in_v.ap()[0:VSZ].rearrange("(h p c d) -> h p c d",
                                           h=H, p=128, c=NT)
        with nc.psum_tensor("pv", [128, 2 * D], F32) as pv:
            tv0, vv0 = TV.n, VV.n
            gdv0 = GD.n

            @block.tensor
            def _(tn):
                tn.wait_ge(LWV.h, ld_wv)
                for tc in range(NT):
                    if tc >= 2:
                        tn.wait_ge(VV.h, vv0 + tc - 1)
                    last = None
                    for ic in range(NC):
                        for ot in range(4):
                            last = tn.matmul(
                                pv[:, (tc % 2) * D + ot * 512:
                                   (tc % 2) * D + (ot + 1) * 512],
                                h1f[:, ic * T + tc * 128:ic * T + (tc + 1) * 128],
                                wvb[:, ic * D + ot * 512:ic * D + (ot + 1) * 512],
                                start=(ic == 0), stop=(ic == NC - 1))
                    TV.inc(last)

            @block.vector
            def _(v):
                v.wait_ge(SD.h, sd_bv)
                for tc in range(NT):
                    v.wait_ge(TV.h, tv0 + tc + 1)
                    if tc >= 1:
                        v.wait_ge(GD.h, gdv0 + 16 * tc)
                    VV.inc(v.tensor_add(vtm[:], pv[:, (tc % 2) * D:
                                                   (tc % 2 + 1) * D],
                                        bvb[:]))

            @block.gpsimd
            def _(gp):
                for tc in range(NT):
                    gp.wait_ge(VV.h, vv0 + tc + 1)
                    _dma(gp, GD, vr[:, :, tc, :].transpose([1, 0, 2]),
                         vtm[:].rearrange("p (h d) -> p h d", h=H))

        @block.gpsimd
        def _(gp):
            gp.wait_ge(GD.h, GD.n)
            CC.inc(gp.collective_compute(
                "AllGather", ALU.bypass,
                replica_groups=[list(range(CORES))],
                ins=[kv_in_v.ap()], outs=[kvg_v.ap()]))
        cc_kv = CC.n
        sCV.close()

        LFW = Sem(ctx, nc, "LFW")
        STN = Sem(ctx, nc, "STN")
        SVN = Sem(ctx, nc, "SVN")
        SSN = Sem(ctx, nc, "SSN")
        fwsb = eC(nc.sbuf_tensor("fwsb", [128, 2 * D], F32))
        tscr = eC(nc.sbuf_tensor("tscr", [128, D], F32))
        trat = eC(nc.sbuf_tensor("trat", [128, D], F32))
        tgab = eC(nc.sbuf_tensor("tgab", [128, D], BF16))
        tgbb = eC(nc.sbuf_tensor("tgbb", [128, D], BF16))
        wtb = eC(nc.sbuf_tensor("wtb", [128, 2 * D], BF16))
        ttr = eC(nc.sbuf_tensor("ttr", [128, NSH], BF16))

        gd_p5 = GD.n

        @block.sync
        def _(sy):
            sy.wait_ge(TV.h, TV.n)
            sy.wait_ge(VV.h, VV.n)
            sy.wait_ge(GD.h, gd_p5)
            for ch in range(2):
                _dma(sy, LFW, fwsb[:, ch * D:(ch + 1) * D], fws_in[ch])
        ld_fw = LFW.n

        @block.scalar
        def _(sc):
            sc.wait_ge(LFW.h, ld_fw)
            last = None
            for ch in range(2):
                last = sc.activation(tscr[:], fwsb[:, ch * D:(ch + 1) * D],
                                     AF.Abs, accum_out=gam2[:, ch:ch + 1])
            SSN.inc(last)
        ssn_g = SSN.n

        @block.vector
        def _(v):
            v.wait_ge(SSN.h, ssn_g)
            v.tensor_scalar_mul(gam2[:, 0:2], gam2[:, 0:2], 1.0 / D)
            v.drain()
            v.tensor_scalar_add(gam2[:, 2:4], gam2[:, 0:2], 1e-8)
            v.drain()
            v.reciprocal(gam2[:, 2:4], gam2[:, 2:4])
            v.drain()
            last = None
            for ch in range(2):
                v.tensor_scalar_mul(trat[:], fwsb[:, ch * D:(ch + 1) * D],
                                    gam2[:, 2 + ch:3 + ch])
                v.tensor_scalar(tgab[:], trat[:], 0.5, None, ALU.is_ge)
                v.tensor_scalar(tgbb[:], trat[:], -1.0, 0.5, ALU.mult, ALU.is_ge)
                last = v.tensor_sub(wtb[:, ch * D:(ch + 1) * D], tgab[:],
                                    tgbb[:])
            SVN.inc(last)
        svn_t = SVN.n

        with nc.psum_tensor("ptn", [128, NSH], BF16) as ptn:
            gdt0 = GD.n
            @block.tensor
            def _(tn):
                tn.wait_ge(SVN.h, svn_t)
                for kc in range(NC):
                    if kc >= 1:
                        tn.wait_ge(SVN.h, svn_t + kc)
                    last = None
                    for ch in range(2):
                        last = tn.transpose(
                            ptn[:, ch * 128:(ch + 1) * 128],
                            wtb[:, ch * D + kc * 128:ch * D + (kc + 1) * 128],
                            eyeb[:])
                    STN.inc(last)
            stn0 = STN.n - NC

            @block.vector
            def _(v):
                for kc in range(NC):
                    v.wait_ge(STN.h, stn0 + kc + 1)
                    if kc >= 1:
                        v.wait_ge(GD.h, gdt0 + 16 * kc)
                    SVN.inc(v.tensor_copy(ttr[:], ptn[:]))

            @block.gpsimd
            def _(gp):
                for kc in range(NC):
                    gp.wait_ge(SVN.h, svn_t + kc + 1)
                    _dma(gp, GD, tern_in[kc], ttr[:])
                gp.wait_ge(SVN.h, svn_t)
                gp.dma_start(out=gam_in.ap().rearrange("(c p) -> p c", p=128),
                             in_=gam2[:, 0:2],
                             allow_slow_non_contiguous=True).then_inc(GD.h, 16)
                GD.n += 16
                gp.wait_ge(GD.h, GD.n)
                CC.inc(gp.collective_compute(
                    "AllGather", ALU.bypass,
                    replica_groups=[list(range(CORES))],
                    ins=[gam_in.ap()], outs=[gam_g.ap()]))
                CC.inc(gp.collective_compute(
                    "AllGather", ALU.bypass,
                    replica_groups=[list(range(CORES))],
                    ins=[tern_in.ap()], outs=[tern_g.ap()]))
        cc_all = CC.n

        with nc.psum_tensor("pqn", [16, T], F32) as pqn:
            TQN = Sem(ctx, nc, "TQN")
            tskq0, skeq0, vkrq0 = kq_phase(
                wq4, bqm,
                lambda oc: q_st.ap()[oc],
                "q", pqn, TQN)

            @block.scalar
            def _(sc):
                sc.wait_ge(TQN.h, TQN.n)
                SKE.inc(sc.copy(qn2[:], pqn[:]))
            ske_qn = SKE.n

        LKN = Sem(ctx, nc, "LKN")
        knall = eC(nc.sbuf_tensor("knall", [16, 8], BF16))
        qn2t = eC(nc.sbuf_tensor("qn2t", [128, 128], F32))
        bksrow = eC(nc.sbuf_tensor("bksrow", [1, 16], F32))
        bksb = eC(nc.sbuf_tensor("bksb", [128, 16], F32))

        @block.sync
        def _(sy):
            sy.wait_ge(CC.h, cc_agk)
            kn_ap = kvg_k.ap().rearrange("(r e) -> e r", r=CORES)[
                KSZ:KSZ + H, :]
            sy.dma_start(out=knall[:], in_=kn_ap,
                         allow_slow_non_contiguous=True).then_inc(LKN.h, 16)
            LKN.n += 16
        ld_kn = LKN.n

        @block.vector
        def _(v):
            v.wait_ge(LKN.h, ld_kn)
            VKR.inc(v.reduce_max(bks[:, 1:2], knall[:], axis=AXX))
        vkr_bk = VKR.n

        @block.scalar
        def _(sc):
            sc.wait_ge(VKR.h, vkr_bk)
            SKE.inc(sc.activation(bks[:, 2:3], bks[:, 1:2], AF.Sqrt))
        ske_bk = SKE.n

        @block.vector
        def _(v):
            v.wait_ge(SKE.h, ske_bk)
            VKR.inc(v.tensor_scalar_mul(bks[:, 3:4], bks[:, 2:3], SCALE))
        vkr_bk2 = VKR.n

        with nc.psum_tensor("pnq", [128, 128], F32) as pnq, \
             nc.psum_tensor("pbr", [1, 16], F32) as pbr, \
             nc.psum_tensor("pbb", [128, 16], F32) as pbb:
            TNC = Sem(ctx, nc, "TNC")

            @block.tensor
            def _(tn):
                tn.wait_ge(SKE.h, ske_qn)
                last = None
                for qt in range(NT):
                    last = tn.transpose(pnq[:, qt * 16:(qt + 1) * 16],
                                        qn2[:, qt * 128:(qt + 1) * 128],
                                        eyef[0:16, 0:16])
                TNC.inc(last)
                tn.wait_ge(VKR.h, vkr_bk2)
                TNC.inc(tn.transpose(pbr[:], bks[:, 3:4],
                                     eyef[0:16, 0:16]))
            tnc_t = TNC.n

            @block.scalar
            def _(sc):
                sc.wait_ge(TNC.h, tnc_t)
                sc.copy(qn2t[:], pnq[:])
                SKE.inc(sc.copy(bksrow[:], pbr[:]))
            ske_nt = SKE.n

            @block.tensor
            def _(tn):
                tn.wait_ge(SKE.h, ske_nt)
                TNC.inc(tn.matmul(pbb[:], onesf[:], bksrow[:],
                                  start=True, stop=True))
            tnc_bb = TNC.n

            @block.scalar
            def _(sc):
                sc.wait_ge(TNC.h, tnc_bb)
                sc.copy(bksb[:], pbb[:])
                SKE.inc(sc.activation(qn2t[:], qn2t[:], AF.Sqrt))
            ske_sq = SKE.n

            @block.vector
            def _(v):
                v.wait_ge(SKE.h, ske_sq)
                last = None
                for qt in range(NT):
                    last = v.scalar_tensor_tensor(
                        negcq[:, qt * H:(qt + 1) * H],
                        qn2t[:, qt * H:(qt + 1) * H], -1.0, bksb[:],
                        ALU.mult, ALU.mult)
                VKR.inc(last)
            vkr_ncq = VKR.n

        d_kv = dbg_out("kv", [KVN], BF16)
        if d_kv is not None:
            @block.sync
            def _(sy):
                sy.wait_ge(CC.h, cc_kv)
                _dma(sy, ST, d_kv[0:KSZ], kv_in_k.ap()[0:KSZ])
                _dma(sy, ST, d_kv[KSZ:KSZ + VSZ], kv_in_v.ap())
                _dma(sy, ST, d_kv[KSZ + VSZ:KSZ + VSZ + H],
                     kv_in_k.ap()[KSZ:KSZ + H])
        d_q = dbg_out("q", [H, 128, T], BF16)
        if d_q is not None:
            @block.sync
            def _(sy):
                sy.wait_ge(SKE.h, ske_qn)
                for hh in range(H):
                    _dma(sy, ST, d_q[hh], q_st.ap()[hh])
        d_tern = dbg_out("tern", [NC, 128, NSH], BF16)
        if d_tern is not None:
            @block.sync
            def _(sy):
                sy.wait_ge(CC.h, cc_all)
                for kc in range(NC):
                    _dma(sy, ST, d_tern[kc], tern_in[kc])
        d_gam = dbg_out("gam", [NSH], F32)
        if d_gam is not None:
            @block.sync
            def _(sy):
                sy.wait_ge(CC.h, cc_all)
                _dma(sy, ST, d_gam, gam_in.ap())
        d_ncq = dbg_out("ncq", [128, NT * H], F32)
        if d_ncq is not None:
            @block.sync
            def _(sy):
                sy.wait_ge(VKR.h, vkr_ncq)
                _dma(sy, ST, d_ncq, negcq[:])

        sC.close()
        sA.close()

        sD1 = ExitStack()
        eD1 = sD1.enter_context
        O_fm = eD1(nc.sbuf_tensor("O_fm", [128, H * T], BF16, side="right"))
        sD2 = ExitStack()
        eD2 = sD2.enter_context
        kh = [eD2(nc.sbuf_tensor(f"kh{i}", [128, S], BF16)) for i in range(2)]
        vh = [eD2(nc.sbuf_tensor(f"vh{i}", [128, S], BF16)) for i in range(2)]
        qh = [eD2(nc.sbuf_tensor(f"qh{i}", [128, T], BF16)) for i in range(2)]
        pbuf = eD2(nc.sbuf_tensor("pbuf", [128, 4 * 4096], BF16))
        ptb = [eD2(nc.sbuf_tensor(f"ptb{i}", [128, 512], BF16)) for i in range(2)]
        dacc = eD2(nc.sbuf_tensor("dacc", [128, 32], F32))
        dent = eD2(nc.sbuf_tensor("dent", [128, 8], F32))
        denrow = eD2(nc.sbuf_tensor("denrow", [4, 128], F32))
        rbs = eD2(nc.sbuf_tensor("rbs", [128, 512], F32))

        LKV = [Sem(ctx, nc, "LKV0"), Sem(ctx, nc, "LKV1")]
        LVV = [Sem(ctx, nc, "LVV0"), Sem(ctx, nc, "LVV1")]
        ATS = Sem(ctx, nc, "ATS")
        AEX = Sem(ctx, nc, "AEX")
        ATT = Sem(ctx, nc, "ATT")
        ATE = Sem(ctx, nc, "ATE")
        APV = Sem(ctx, nc, "APV")
        ADN = Sem(ctx, nc, "ADN")
        TDT = Sem(ctx, nc, "TDT")
        SDR = Sem(ctx, nc, "SDR")
        TRB = Sem(ctx, nc, "TRB")
        ARB = Sem(ctx, nc, "ARB")
        AOE = Sem(ctx, nc, "AOE")

        gates = [(VKR, VKR.n), (SKE, SKE.n), (TQN, TQN.n), (SVN, SVN.n),
                 (STN, STN.n), (GD, GD.n), (CC, cc_agk)]
        gd_attn = GD.n

        @block.sync
        def _(sy):
            for sem, val in gates:
                sy.wait_ge(sem.h, val)
            KN2 = KSZ + H
            for h in range(H):
                if h >= 2:
                    sy.wait_ge(APV.h, 128 * (h - 1))
                p = h % 2
                for r in range(CORES):
                    _dma(sy, LKV[p], kh[p][:, r * T:(r + 1) * T],
                         kvg_k.ap()[r * KN2 + h * 131072:
                                    r * KN2 + (h + 1) * 131072]
                         .rearrange("(p f) -> p f", f=T))
                _dma(sy, LKV[p], qh[p][:], q_st.ap()[h])
                if h == 0:
                    sy.wait_ge(CC.h, cc_kv)
                for r in range(CORES):
                    _dma(sy, LVV[p], vh[p][:, r * T:(r + 1) * T],
                         kvg_v.ap()[r * VSZ + h * 131072:
                                    r * VSZ + (h + 1) * 131072]
                         .rearrange("(p f) -> p f", f=T))

        with nc.psum_tensor("s2p", [128, 2 * T], F32) as s2p, \
             nc.psum_tensor("tpA", [128, T], BF16) as tpA, \
             nc.psum_tensor("tpB", [128, T], BF16) as tpB, \
             nc.psum_tensor("o_ps", [128, 512], F32) as o_ps, \
             nc.psum_tensor("pxtr", [128, 512], F32) as pxtr:
            tps = [tpA, tpB]

            NQG = 2 * H

            @block.tensor
            def _(tn):
                for h in range(H):
                    p = h % 2
                    tn.wait_ge(LKV[p].h, 144 * (h // 2 + 1))
                    for qg in range(2):
                        u = h * 2 + qg
                        ats0, aex0 = 32 * u, 32 * u
                        att0, ate0, apv0 = 64 * u, 64 * u, 64 * u

                        def tpv_batch(ksp):
                            for kt in range(8 * ksp, 8 * ksp + 8):
                                if kt == 8 * ksp:
                                    tn.wait_ge(AEX.h, aex0 + 4 * ksp + 4)
                                if ate0 + kt - 1 > 0:
                                    tn.wait_ge(ATE.h, ate0 + kt - 1)
                                last = None
                                for qt in range(4):
                                    last = tn.transpose(
                                        tps[kt % 2][:, qt * 128:(qt + 1) * 128],
                                        pbuf[:, qt * 4096 + (kt % 32) * 128:
                                             qt * 4096 + (kt % 32 + 1) * 128],
                                        eyeb[:])
                                ATT.inc(last)
                                pv_kt = kt - 1
                                if pv_kt >= 0:
                                    emit_pv(pv_kt)

                        def emit_pv(kt):
                            if kt == 0:
                                tn.wait_ge(AOE.h, u)
                            tn.wait_ge(ATE.h, ate0 + kt + 1)
                            APV.inc(tn.matmul(
                                o_ps[:], vh[p][:, kt * 128:(kt + 1) * 128],
                                ptb[kt % 2][:], start=(kt == 0),
                                stop=(kt == 63)))

                        for ks in range(8):
                            for qt in range(4):
                                ui = ks * 4 + qt
                                if aex0 + ui - 1 > 0:
                                    tn.wait_ge(AEX.h, aex0 + ui - 1)
                                last = None
                                for j in range(2):
                                    last = tn.matmul(
                                        s2p[:, (ui % 2) * T + j * 512:
                                            (ui % 2) * T + (j + 1) * 512],
                                        qh[p][:, qg * 512 + qt * 128:
                                              qg * 512 + (qt + 1) * 128],
                                        kh[p][:, ks * T + j * 512:
                                              ks * T + (j + 1) * 512],
                                        start=True, stop=True)
                                ATS.inc(last)
                            if ks >= 2:
                                if ks == 2 and qg == 0:
                                    tn.wait_ge(LVV[p].h, 128 * (h // 2 + 1))
                                tpv_batch(ks - 2)
                        tpv_batch(6)
                        tpv_batch(7)
                        emit_pv(63)
                        tn.wait_ge(ADN.h, u + 1)
                        TDT.inc(tn.transpose(pxtr[0:4, 0:128], dent[:, 4:8],
                                             eyef[:]))
                        tn.wait_ge(SDR.h, u + 1)
                        last = None
                        for qt in range(4):
                            last = tn.matmul(
                                pxtr[:, qt * 128:(qt + 1) * 128],
                                e4f[:, qt * 128:(qt + 1) * 128],
                                denrow[:], start=True, stop=True)
                        TRB.inc(last)

            @block.scalar
            def _(sc):
                sc.wait_ge(VKR.h, vkr_ncq)
                for h in range(H):
                    for qg in range(2):
                        u = h * 2 + qg
                        ats0, aex0, att0 = 32 * u, 32 * u, 64 * u
                        sc.wait_ge(ADN.h, u)
                        for ks in range(8):
                            for qt in range(4):
                                ui = ks * 4 + qt
                                sc.wait_ge(ATS.h, ats0 + ui + 1)
                                if ks >= 4 and qt == 0:
                                    sc.wait_ge(ATT.h, att0 + 8 * (ks - 3))
                                AEX.inc(sc.activation(
                                    pbuf[:, qt * 4096 + (ks % 4) * 1024:
                                         qt * 4096 + (ks % 4 + 1) * 1024],
                                    s2p[:, (ui % 2) * T:(ui % 2 + 1) * T],
                                    AF.Exp, scale=scalev[:, 0:1],
                                    bias=negcq[:, (qg * 4 + qt) * H + h:
                                               (qg * 4 + qt) * H + h + 1],
                                    accum_out=dacc[:, qt * 8 + ks:
                                                   qt * 8 + ks + 1]))
                        sc.wait_ge(TDT.h, u + 1)
                        SDR.inc(sc.copy(denrow[:], pxtr[0:4, 0:128]))
                        sc.wait_ge(TRB.h, u + 1)
                        ARB.inc(sc.copy(rbs[:], pxtr[:]))

            @block.vector
            def _(v):
                for h in range(H):
                    for qg in range(2):
                        u = h * 2 + qg
                        att0, ate0, apv0 = 64 * u, 64 * u, 64 * u
                        for kt in range(64):
                            v.wait_ge(ATT.h, att0 + kt + 1)
                            if apv0 + kt - 1 > 0:
                                v.wait_ge(APV.h, apv0 + kt - 1)
                            ATE.inc(v.tensor_copy(ptb[kt % 2][:],
                                                  tps[kt % 2][:, 0:512]))
                        v.wait_ge(AEX.h, 32 * u + 32)
                        if u >= 1:
                            v.wait_ge(TRB.h, u)
                        for qt in range(4):
                            v.reduce_sum(dent[:, qt:qt + 1],
                                         dacc[:, qt * 8:(qt + 1) * 8],
                                         axis=AXX)
                        v.drain()
                        ADN.inc(v.reciprocal(dent[:, 4:8], dent[:, 0:4]))
                        v.wait_ge(APV.h, apv0 + 64)
                        v.wait_ge(ARB.h, u + 1)
                        AOE.inc(v.tensor_mul(
                            O_fm[:, h * T + qg * 512:h * T + (qg + 1) * 512],
                            o_ps[:], rbs[:]))

        d_O = dbg_out("O", [128, H * T], BF16)
        if d_O is not None:
            @block.sync
            def _(sy):
                sy.wait_ge(AOE.h, AOE.n)
                _dma(sy, ST, d_O, O_fm[:])
        aoe_end = AOE.n
        sD2.close()

        x2 = e(nc.sbuf_tensor("x2", [128, NT * D], F32))
        sD3 = ExitStack()
        eD3 = sD3.enter_context
        attn_tm = eD3(nc.sbuf_tensor("attn_tm", [128, NT * D], BF16))
        wob = eD3(nc.sbuf_tensor("wob", [128, 2 * D], BF16))
        afm = eD3(nc.sbuf_tensor("afm", [128, T], BF16))
        xrb = eD3(nc.sbuf_tensor("xrb", [128, D], F32))

        LWO = [Sem(ctx, nc, "LWO0"), Sem(ctx, nc, "LWO1")]
        LXR = Sem(ctx, nc, "LXR")
        TWO = Sem(ctx, nc, "TWO")
        SWO = Sem(ctx, nc, "SWO")
        TWT = Sem(ctx, nc, "TWT")
        VWO = Sem(ctx, nc, "VWO")
        VX2 = Sem(ctx, nc, "VX2")

        sd_gab = bcast_row(mrow(MGA), slot[0], "gab", [(VD, vd_mod), (AOE, aoe_end)])
        sd_gabo = bcast_row(gabo4[:], slot[1], "gabo", [(VD, vd_mod), (AOE, aoe_end)])

        @block.gpsimd
        def _(gp):
            gp.wait_ge(AOE.h, aoe_end)
            for fc in range(NC):
                if fc >= 2:
                    gp.wait_ge(TWO.h, fc - 1)
                _dma(gp, LWO[fc % 2], wob[:, (fc % 2) * D:(fc % 2 + 1) * D],
                     wo4[fc])

        @block.sync
        def _(sy):
            sy.wait_ge(AOE.h, aoe_end)
            for tt in range(NT):
                if tt >= 1:
                    sy.wait_ge(VX2.h, tt)
                _dma(sy, LXR, xrb[:], x_in[tt * 128:(tt + 1) * 128, :])

        with nc.psum_tensor("pwo", [128, 2 * T], F32) as pwo, \
             nc.psum_tensor("ptw", [128, 2 * T], BF16) as ptw:
            def emit_wot(tn, fc):
                tn.wait_ge(SWO.h, fc + 1)
                if fc - 1 > 0:
                    tn.wait_ge(VWO.h, fc - 1)
                last = None
                for tc in range(NT):
                    last = tn.transpose(
                        ptw[:, (fc % 2) * T + tc * 128:
                            (fc % 2) * T + (tc + 1) * 128],
                        afm[:, tc * 128:(tc + 1) * 128], eyeb[:])
                TWT.inc(last)

            @block.tensor
            def _(tn):
                for fc in range(NC):
                    tn.wait_ge(LWO[fc % 2].h, 16 * (fc // 2 + 1))
                    if fc < 2:
                        tn.wait_ge(SD.h, max(sd_gab, sd_gabo))
                    if fc >= 2:
                        tn.wait_ge(SWO.h, fc - 1)
                    last = None
                    for tt2 in range(2):
                        for dc in range(NC):
                            last = tn.matmul(
                                pwo[:, (fc % 2) * T + tt2 * 512:
                                    (fc % 2) * T + (tt2 + 1) * 512],
                                wob[:, (fc % 2) * D + dc * 128:
                                    (fc % 2) * D + (dc + 1) * 128],
                                O_fm[:, dc * T + tt2 * 512:
                                     dc * T + (tt2 + 1) * 512],
                                start=(dc == 0), stop=(dc == NC - 1))
                    TWO.inc(last)
                    if fc >= 1:
                        emit_wot(tn, fc - 1)
                emit_wot(tn, NC - 1)

            @block.scalar
            def _(sc):
                for fc in range(NC):
                    sc.wait_ge(TWO.h, fc + 1)
                    if fc >= 1:
                        sc.wait_ge(TWT.h, fc)
                    SWO.inc(sc.copy(afm[:], pwo[:, (fc % 2) * T:
                                                (fc % 2 + 1) * T]))

            @block.vector
            def _(v):
                for fc in range(NC):
                    v.wait_ge(TWT.h, fc + 1)
                    at_view = attn_tm[:].rearrange(
                        "p (c f) -> p c f", c=NT)[:, :, fc * 128:(fc + 1) * 128]
                    VWO.inc(v.tensor_copy(
                        at_view, ptw[:, (fc % 2) * T:(fc % 2 + 1) * T]
                        .rearrange("p (c f) -> p c f", c=NT)))

        @block.vector
        def _(v):
            v.wait_ge(SD.h, max(sd_gab, sd_gabo))
            v.wait_ge(VWO.h, VWO.n)
            for tt in range(NT):
                v.wait_ge(LXR.h, 16 * (tt + 1))
                x2c = x2[:, tt * D:(tt + 1) * D]
                v.tensor_mul(x2c, attn_tm[:, tt * D:(tt + 1) * D], slot[0][:])
                v.tensor_add(x2c, x2c, slot[1][:])
                VX2.inc(v.tensor_add(x2c, x2c, xrb[:]))

        d_x2 = dbg_out("x2", [T, D], F32)
        if d_x2 is not None:
            @block.sync
            def _(sy):
                sy.wait_ge(VX2.h, VX2.n)
                for i in range(NT):
                    _dma(sy, ST, d_x2[i * 128:(i + 1) * 128, :],
                         x2[:, i * D:(i + 1) * D])

        sD3.close()
        sD1.close()

        sD4 = ExitStack()
        eD4 = sD4.enter_context
        xqf = eD4(nc.sbuf_tensor("xqf", [128, NC * T], BF16))
        tern_sb = eD4(nc.sbuf_tensor("tern_sb", [128, NC * 512], BF16))
        h2s = eD4(nc.sbuf_tensor("h2s", [128, D], F32))
        u9 = eD4(nc.sbuf_tensor("u9", [128, T], F32))
        v9b = eD4(nc.sbuf_tensor("v9b", [128, T], F32))
        d9 = eD4(nc.sbuf_tensor("d9", [128, T], F32))
        xqt = eD4(nc.sbuf_tensor("xqt", [128, D], BF16))
        fout = eD4(nc.sbuf_tensor("fout", [128, 512], F32))
        gbgb = eD4(nc.sbuf_tensor("gbgb", [128, 512], F32))
        gbfbb = eD4(nc.sbuf_tensor("gbfbb", [128, 512], F32))

        DXQ = Sem(ctx, nc, "DXQ")
        d_xqall = dbg_out("xqall", [NT, 128, D], BF16)
        LGG = Sem(ctx, nc, "LGG")
        LTH = Sem(ctx, nc, "LTH")
        S9 = Sem(ctx, nc, "S9")
        V9 = Sem(ctx, nc, "V9")
        T9 = Sem(ctx, nc, "T9")
        TF = Sem(ctx, nc, "TF")
        SF = Sem(ctx, nc, "SF")
        VF = Sem(ctx, nc, "VF")

        @block.sync
        def _(sy):
            sy.wait_ge(CC.h, cc_all)
            _dma(sy, LGG, grow[:], gam_g.ap().rearrange("(p b) -> p b", b=512))
        ld_gg = LGG.n

        @block.vector
        def _(v):
            v.wait_ge(LGG.h, ld_gg)
            v.wait_ge(VX2.h, VX2.n)
            v.tensor_mul(gbg4[:], mrow(MGB), grow[:])
            VD.inc(v.tensor_mul(gbfb4[:], mrow(MGB), fb4[:]))
        vd_gb = VD.n

        sd_cb1 = bcast_row(mrow(MCB), slot[0], "cb1", [(VD, vd_gb)])
        sd_sb = bcast_row(mrow(MSB), slot[1], "sbb", [(VD, vd_gb)])


        def load_tern_q(q, gate_sem, gate_val):
            @block.sync
            def _(sy):
                sy.wait_ge(CC.h, cc_all)
                if gate_sem is not None:
                    sy.wait_ge(gate_sem.h, gate_val)
                for kc in range(NC):
                    for r in range(2):
                        _dma(sy, LTH, tern_sb[:, kc * 512 + r * NSH:
                                              kc * 512 + (r + 1) * NSH],
                             tern_g[q * 2 + r, kc])
            return LTH.n

        ld_th0 = load_tern_q(0, None, 0)

        v9_xq_marks = []
        @block.vector
        def _(v):
            v.wait_ge(SD.h, max(sd_cb1, sd_sb))
            V9.inc(v.tensor_copy(stats[:, 0:1], stats[:, 0:1]))
        v9_bc = V9.n

        with nc.psum_tensor("pxq", [128, D], BF16) as pxq, \
             nc.psum_tensor("pf", [128, 2 * 512], F32) as pf:
            for tt in range(NT):
                v9_prev = V9.n

                @block.scalar
                def _(sc, tt=tt):
                    if tt == 0:
                        sc.wait_ge(V9.h, v9_bc)
                    sc.wait_ge(VX2.h, tt + 1)
                    sc.wait_ge(V9.h, v9_prev)
                    x2c = x2[:, tt * D:(tt + 1) * D]
                    S9.inc(sc.activation(h2s[:], x2c, AF.Square,
                                         accum_out=stats[:, 1:2]))
                s9_sq = S9.n

                @block.vector
                def _(v, tt=tt):
                    x2c = x2[:, tt * D:(tt + 1) * D]
                    v.wait_ge(S9.h, s9_sq)
                    v.reduce_sum(stats[:, 0:1], x2c, axis=AXX)
                    v.drain()
                    v.tensor_scalar_mul(stats[:, 2:3], stats[:, 0:1], 1.0 / D)
                    v.tensor_scalar_mul(stats[:, 3:4], stats[:, 1:2], 1.0 / D)
                    v.drain()
                    v.tensor_mul(stats[:, 4:5], stats[:, 2:3], stats[:, 2:3])
                    v.drain()
                    v.tensor_sub(stats[:, 3:4], stats[:, 3:4], stats[:, 4:5])
                    v.drain()
                    V9.inc(v.tensor_scalar_add(stats[:, 3:4], stats[:, 3:4],
                                               1e-5))
                v9_var = V9.n

                @block.scalar
                def _(sc, tt=tt):
                    sc.wait_ge(V9.h, v9_var)
                    S9.inc(sc.activation(stats[:, 4:5], stats[:, 3:4],
                                         AF.Sqrt))
                s9_sqrt = S9.n

                @block.vector
                def _(v, tt=tt):
                    v.wait_ge(S9.h, s9_sqrt)
                    v.reciprocal(stats[:, 5:6], stats[:, 4:5])
                    v.drain()
                    v.tensor_mul(stats[:, 6:7], stats[:, 2:3], stats[:, 5:6])
                    v.drain()
                    V9.inc(v.tensor_scalar_mul(stats[:, 6:7], stats[:, 6:7],
                                               -1.0))
                v9_rstd = V9.n

                s9_u0 = S9.n
                v9_h20 = V9.n

                @block.scalar
                def _(sc, tt=tt):
                    sc.wait_ge(V9.h, v9_rstd)
                    for jh in range(2):
                        if jh == 1:
                            sc.wait_ge(V9.h, v9_h20 + 1)
                        S9.inc(sc.activation(
                            u9[:], x2[:, tt * D + jh * T:tt * D + (jh + 1) * T],
                            AF.Identity, bias=stats[:, 6:7],
                            scale=stats[:, 5:6]))

                @block.vector
                def _(v, tt=tt):
                    for jh in range(2):
                        v.wait_ge(S9.h, s9_u0 + jh + 1)
                        hs = h2s[:, jh * T:(jh + 1) * T]
                        v.tensor_mul(hs, u9[:], slot[0][:, jh * T:(jh + 1) * T])
                        V9.inc(v.tensor_add(hs, hs,
                                            slot[1][:, jh * T:(jh + 1) * T]))
                v9_h2 = V9.n

                @block.scalar
                def _(sc, tt=tt):
                    sc.wait_ge(V9.h, v9_h2)
                    last = None
                    for jh in range(2):
                        last = sc.activation(u9[:], h2s[:, jh * T:(jh + 1) * T],
                                             AF.Square,
                                             accum_out=stats[:, 13 + jh:14 + jh])
                    S9.inc(last)
                s9_r2 = S9.n

                @block.vector
                def _(v, tt=tt):
                    v.wait_ge(S9.h, s9_r2)
                    v.tensor_add(stats[:, 7:8], stats[:, 13:14], stats[:, 14:15])
                    v.drain()
                    v.tensor_scalar(stats[:, 8:9], stats[:, 7:8], 1.0 / D,
                                    1e-8, ALU.mult, ALU.add)
                    V9.inc(v.engine_nop())
                v9_ra = V9.n

                @block.scalar
                def _(sc, tt=tt):
                    sc.wait_ge(V9.h, v9_ra)
                    S9.inc(sc.activation(stats[:, 8:9], stats[:, 8:9],
                                         AF.Sqrt))
                s9_rs = S9.n

                @block.vector
                def _(v, tt=tt):
                    v.wait_ge(S9.h, s9_rs)
                    V9.inc(v.reciprocal(stats[:, 9:10], stats[:, 8:9]))
                v9_ri = V9.n

                @block.scalar
                def _(sc, tt=tt):
                    sc.wait_ge(V9.h, v9_ri)
                    S9.inc(sc.activation(h2s[:], h2s[:], AF.Identity,
                                         scale=stats[:, 9:10]))
                s9_xn = S9.n

                t9_prev = T9.n

                @block.vector
                def _(v, tt=tt):
                    v.wait_ge(S9.h, s9_xn)
                    v.wait_ge(T9.h, t9_prev)
                    if d_xqall is not None and tt >= 1:
                        v.wait_ge(DXQ.h, 16 * tt)
                    v.reduce_max(stats[:, 10:11], h2s[:], axis=AXX,
                                 apply_absolute_value=True)
                    v.drain()
                    v.tensor_scalar(stats[:, 11:12], stats[:, 10:11],
                                    1.0 / 127.0, 1e-8, ALU.mult, ALU.add)
                    v.drain()
                    v.reciprocal(stats[:, 12:13], stats[:, 11:12])
                    v.drain()
                    v.tensor_scalar_mul(stats[:, 16 + tt:17 + tt],
                                        stats[:, 10:11], 1.0 / 127.0)
                    v.drain()
                    MAGIC = 12582912.0
                    last = None
                    for jh in range(2):
                        v.tensor_scalar(u9[:], h2s[:, jh * T:(jh + 1) * T],
                                        stats[:, 12:13], MAGIC,
                                        ALU.mult, ALU.add)
                        last = v.tensor_scalar_add(
                            xqt[:, jh * T:(jh + 1) * T], u9[:], -MAGIC)
                    V9.inc(last)
                v9_xq = V9.n

                @block.tensor
                def _(tn, tt=tt):
                    tn.wait_ge(V9.h, v9_xq)
                    last = None
                    for kc in range(NC):
                        last = tn.transpose(
                            pxq[:, kc * 128:(kc + 1) * 128],
                            xqt[:, kc * 128:(kc + 1) * 128], eyeb[:])
                    T9.inc(last)
                t9_t = T9.n

                if d_xqall is not None:
                    @block.sync
                    def _(sy, tt=tt):
                        sy.wait_ge(V9.h, v9_xq)
                        _dma(sy, DXQ, d_xqall[tt], xqt[:])

                @block.vector
                def _(v, tt=tt):
                    v.wait_ge(T9.h, t9_t)
                    xq_view = xqf[:].rearrange(
                        "p (c f) -> p c f", c=NC)[:, :, tt * 128:(tt + 1) * 128]
                    V9.inc(v.tensor_copy(
                        xq_view, pxq[:]
                        .rearrange("p (c f) -> p c f", c=NC)))
                v9_xq_marks.append(V9.n)

            st_ffn0 = ST.n
            SGQ = Sem(ctx, nc, "SGQ")
            TGQ = Sem(ctx, nc, "TGQ")
            for q in range(4):
                if q >= 1:
                    load_tern_q(q, TF, q * NT)

                with nc.psum_tensor(f"pgq{q}", [128, 2 * 512], F32) as pgq:
                    @block.tensor
                    def _(tn, q=q):
                        tn.wait_ge(VD.h, vd_gb)
                        if q >= 1:
                            tn.wait_ge(SGQ.h, 2 * q)
                        tn.matmul(pgq[:, 0:512],
                                  e4f[:, q * 128:(q + 1) * 128], gbg4[:],
                                  start=True, stop=True)
                        TGQ.inc(tn.matmul(pgq[:, 512:1024],
                                          e4f[:, q * 128:(q + 1) * 128],
                                          gbfb4[:], start=True, stop=True))

                    @block.scalar
                    def _(sc, q=q):
                        sc.wait_ge(TGQ.h, q + 1)
                        if q >= 1:
                            sc.wait_ge(VF.h, q * NT)
                        sc.copy(gbgb[:], pgq[:, 0:512])
                        SGQ.inc(sc.copy(gbfbb[:], pgq[:, 512:1024]), 2)

                for tt in range(NT):
                    @block.tensor
                    def _(tn, q=q, tt=tt):
                        u = q * NT + tt
                        tn.wait_ge(LTH.h, LTH.n)
                        tn.wait_ge(V9.h, v9_xq_marks[tt])
                        if u >= 2:
                            tn.wait_ge(SF.h, u - 1)
                        last = None
                        for kc in range(NC):
                            last = tn.matmul(
                                pf[:, (u % 2) * 512:(u % 2 + 1) * 512],
                                xqf[:, kc * T + tt * 128:
                                    kc * T + (tt + 1) * 128],
                                tern_sb[:, kc * 512:(kc + 1) * 512],
                                start=(kc == 0), stop=(kc == NC - 1))
                        TF.inc(last)

                    @block.scalar
                    def _(sc, q=q, tt=tt):
                        u = q * NT + tt
                        sc.wait_ge(TF.h, u + 1)
                        if u >= 1:
                            sc.wait_ge(VF.h, u)
                            sc.wait_ge(ST.h, st_ffn0 + 16 * u)
                        SF.inc(sc.activation(fout[:],
                                             pf[:, (u % 2) * 512:
                                                (u % 2 + 1) * 512],
                                             AF.Identity,
                                             scale=stats[:, 16 + tt:17 + tt]))

                    @block.vector
                    def _(v, q=q, tt=tt):
                        u = q * NT + tt
                        v.wait_ge(SF.h, u + 1)
                        v.wait_ge(SGQ.h, 2 * (q + 1))
                        v.tensor_mul(fout[:], fout[:], gbgb[:])
                        v.tensor_add(fout[:], fout[:], gbfbb[:])
                        VF.inc(v.tensor_add(
                            fout[:], fout[:],
                            x2[:, tt * D + q * 512:tt * D + (q + 1) * 512]))

                    @block.sync
                    def _(sy, q=q, tt=tt):
                        u = q * NT + tt
                        sy.wait_ge(VF.h, u + 1)
                        _dma(sy, ST, out_t.ap()[tt * 128:(tt + 1) * 128,
                                                q * 512:(q + 1) * 512],
                             fout[:])


        d_xqt = dbg_out("xqt", [128, D], BF16)
        if d_xqt is not None:
            @block.sync
            def _(sy):
                sy.wait_ge(V9.h, V9.n)
                _dma(sy, ST, d_xqt, xqt[:])

        d_xqf = dbg_out("xqf", [128, NC * T], BF16)
        if d_xqf is not None:
            @block.sync
            def _(sy):
                sy.wait_ge(V9.h, V9.n)
                _dma(sy, ST, d_xqf, xqf[:])

        d_st9 = dbg_out("st9", [128, 24], F32)
        if d_st9 is not None:
            @block.sync
            def _(sy):
                sy.wait_ge(V9.h, V9.n)
                _dma(sy, ST, d_st9, stats[:, 0:24])

        @block.sync
        def _(sy):
            sy.wait_ge(ST.h, ST.n)

        sD4.close()
    return nc


def _rope_tables():
    inv_freq = 1.0 / 10000.0 ** (np.arange(0, HD, 2, dtype=np.float32) / HD)
    pos = np.arange(S, dtype=np.float32)
    freqs = np.outer(pos, inv_freq)
    emb = np.concatenate([freqs, freqs], axis=-1)
    return np.cos(emb).astype(np.float32), np.sin(emb).astype(np.float32)


def _prep_inputs(inputs):
    x = np.asarray(inputs["L0_attn_x"], np.float32)[0]
    WqT = np.asarray(inputs["L0_attn_Wq"], np.float32).T
    WkT = np.asarray(inputs["L0_attn_Wk"], np.float32).T
    WvT = np.asarray(inputs["L0_attn_Wv"], np.float32).T
    WoT = np.asarray(inputs["L0_attn_Wo"], np.float32).T

    def w4(WT):
        return np.ascontiguousarray(
            WT.reshape(NC, 128, NC, 128).transpose(2, 1, 0, 3))

    cos_t, sin_t = _rope_tables()
    common = {
        "wq4": w4(WqT),
        "wk4": w4(WkT),
        "wo4": w4(WoT),
        "wv_r": np.ascontiguousarray(WvT.reshape(NC, 128, D)),
        "bqm": np.ascontiguousarray(
            np.asarray(inputs["L0_attn_bq"], np.float32).reshape(NC, 128).T),
        "bkm": np.ascontiguousarray(
            np.asarray(inputs["L0_attn_bk"], np.float32).reshape(NC, 128).T),
        "bvrow": np.asarray(inputs["L0_attn_bv"], np.float32).reshape(4, 512),
        "borow": np.asarray(inputs["L0_attn_bo"], np.float32).reshape(4, 512),
        "fbrow": np.asarray(inputs["L0_ffn_bias"], np.float32).reshape(4, 512),
        "cm": np.ascontiguousarray(
            np.asarray(inputs["L0_c"], np.float32).reshape(NC, 128).T),
        "eyeb": np.eye(128, dtype=np.float32).astype(ml_dtypes.bfloat16),
        "eyef": np.eye(128, dtype=np.float32),
        "onesf": np.ones((1, 128), np.float32),
        "e4f": _e4f(),
        "ehb": _ehb(),
    }
    fw = np.asarray(inputs["L0_ffn_weight"], np.float32)
    wmod = np.asarray(inputs["L0_wmod"], np.float32)
    in_maps = []
    for r in range(CORES):
        m = dict(common)
        m["x"] = np.ascontiguousarray(x[r * T:(r + 1) * T])
        m["fws"] = np.ascontiguousarray(
            fw[r * NSH:(r + 1) * NSH].reshape(2, 128, D))
        m["wmods"] = np.ascontiguousarray(
            wmod[:, r * JSH:(r + 1) * JSH].reshape(NC, 128, JSH))
        m["cosm"] = np.ascontiguousarray(cos_t[r * T:(r + 1) * T].T)
        m["sinm"] = np.ascontiguousarray(sin_t[r * T:(r + 1) * T].T)
        in_maps.append(m)
    return in_maps


def _e4f():
    m = np.zeros((4, 512), np.float32)
    for jt in range(4):
        m[jt, jt * 128:(jt + 1) * 128] = 1.0
    return m


def _ehb():
    e_all = np.zeros((128, H * H), np.float32)
    for oc in range(H):
        e_all[:, oc * H + oc] = 1.0
    return e_all.astype(ml_dtypes.bfloat16)


def kernel(**inputs):
    in_maps = _prep_inputs(inputs)
    nc = build_nc()
    res = run_bass_kernel_spmd(nc, in_maps, core_ids=list(range(CORES)))
    out = np.concatenate([np.asarray(r["out"]) for r in res.results], axis=0)
    return out.reshape(1, S, D).astype(np.float32)


# version 16 (final)
# speedup vs baseline: 1.1974x; 1.0774x over previous
import numpy as np
import ml_dtypes
from contextlib import ExitStack

import concourse.bass as bass
import concourse.mybir as mybir
from concourse.bass_utils import run_bass_kernel_spmd

F32 = mybir.dt.float32
BF16 = mybir.dt.bfloat16
AF = mybir.ActivationFunctionType
ALU = mybir.AluOpType
AXX = mybir.AxisListType.X

CORES = 8
T = 1024
D = 2048
H = 16
HD = 128
S = 8192
NT = T // 128
NC = D // 128
JSH = 6 * D // CORES
NSH = D // CORES
SCALE = float(HD) ** -0.5

_DEBUG = []


class Sem:
    def __init__(self, ctx, nc, name):
        self.h = ctx.enter_context(nc.semaphore(name))
        self.n = 0

    def inc(self, inst, k=1):
        inst.then_inc(self.h, k)
        self.n += k
        return self.n


def _dma(eng, sem, out, in_):
    eng.dma_start(out=out, in_=in_).then_inc(sem.h, 16)
    sem.n += 16
    return sem.n


def build_nc():
    nc = bass.Bass("TRN2", target_bir_lowering=False,
                   detect_race_conditions=False)

    x_in = nc.dram_tensor("x", [T, D], F32, kind="ExternalInput").ap()
    wq4 = nc.dram_tensor("wq4", [NC, 128, NC, 128], F32, kind="ExternalInput").ap()
    wk4 = nc.dram_tensor("wk4", [NC, 128, NC, 128], F32, kind="ExternalInput").ap()
    wo4 = nc.dram_tensor("wo4", [NC, 128, NC, 128], F32, kind="ExternalInput").ap()
    wv_r = nc.dram_tensor("wv_r", [NC, 128, D], F32, kind="ExternalInput").ap()
    bqm_in = nc.dram_tensor("bqm", [128, NC], F32, kind="ExternalInput").ap()
    bkm_in = nc.dram_tensor("bkm", [128, NC], F32, kind="ExternalInput").ap()
    bvrow_in = nc.dram_tensor("bvrow", [4, 512], F32, kind="ExternalInput").ap()
    borow_in = nc.dram_tensor("borow", [4, 512], F32, kind="ExternalInput").ap()
    fbrow_in = nc.dram_tensor("fbrow", [4, 512], F32, kind="ExternalInput").ap()
    fws_in = nc.dram_tensor("fws", [2, 128, D], F32, kind="ExternalInput").ap()
    cm_in = nc.dram_tensor("cm", [128, NC], F32, kind="ExternalInput").ap()
    wmods_in = nc.dram_tensor("wmods", [NC, 128, JSH], F32, kind="ExternalInput").ap()
    cos_in = nc.dram_tensor("cosm", [128, T], F32, kind="ExternalInput").ap()
    sin_in = nc.dram_tensor("sinm", [128, T], F32, kind="ExternalInput").ap()
    eyeb_in = nc.dram_tensor("eyeb", [128, 128], BF16, kind="ExternalInput").ap()
    eyef_in = nc.dram_tensor("eyef", [128, 128], F32, kind="ExternalInput").ap()
    onesf_in = nc.dram_tensor("onesf", [1, 128], F32, kind="ExternalInput").ap()
    e4f_in = nc.dram_tensor("e4f", [4, 512], F32, kind="ExternalInput").ap()
    ehb_in = nc.dram_tensor("ehb", [128, H * H], BF16, kind="ExternalInput").ap()

    out_t = nc.dram_tensor("out", [T, D], F32, kind="ExternalOutput")

    KSZ = NC * 128 * T
    VSZ = H * 128 * NT * 128
    KVN = KSZ + VSZ + H
    kv_in_k = nc.dram_tensor("kv_in_k", [KSZ + H], BF16)
    kv_in_v = nc.dram_tensor("kv_in_v", [VSZ], BF16)
    kvg_k = nc.dram_tensor("kvg_k", [CORES * (KSZ + H)], BF16,
                           addr_space="Shared")
    kvg_v = nc.dram_tensor("kvg_v", [CORES * VSZ], BF16,
                           addr_space="Shared")
    q_st = nc.dram_tensor("q_st", [H, 128, T], BF16)
    mod_in = nc.dram_tensor("mod_in", [JSH], F32)
    mod_g = nc.dram_tensor("mod_g", [CORES * JSH], F32, addr_space="Shared")
    gam_in = nc.dram_tensor("gam_in", [NSH], F32)
    gam_g = nc.dram_tensor("gam_g", [CORES * NSH], F32, addr_space="Shared")
    tern_in = nc.dram_tensor("tern_in", [NC, 128, NSH], BF16)
    tern_g = nc.dram_tensor("tern_g", [CORES, NC, 128, NSH], BF16, addr_space="Shared")

    dbg = {}
    def dbg_out(name, shape, dtype=F32):
        if name in _DEBUG:
            dbg[name] = nc.dram_tensor("dbg_" + name, shape, dtype,
                                       kind="ExternalOutput").ap()
            return dbg[name]
        return None

    ctx = ExitStack()
    with ctx:
        e = ctx.enter_context
        eyeb = e(nc.sbuf_tensor("sb_eyeb", [128, 128], BF16))
        eyef = e(nc.sbuf_tensor("sb_eyef", [128, 128], F32))
        onesf = e(nc.sbuf_tensor("sb_onesf", [1, 128], F32))
        e4f = e(nc.sbuf_tensor("sb_e4f", [4, 512], F32))
        ehb = e(nc.sbuf_tensor("sb_ehb", [128, H * H], BF16))
        bqm = e(nc.sbuf_tensor("sb_bqm", [128, NC], F32))
        bkm = e(nc.sbuf_tensor("sb_bkm", [128, NC], F32))
        def row4(name):
            return e(nc.sbuf_tensor(name, [4, 512], F32))
        bv4, bo4, fb4, gabo4, gbg4, gbfb4, rs0, rs1 = (
            row4(n) for n in ("bv4", "bo4", "fb4", "gabo4", "gbg4",
                              "gbfb4", "rs0", "rs1"))
        modr = [e(nc.sbuf_tensor(f"modr{m}", [4, 512], F32)) for m in range(6)]
        slot = [e(nc.sbuf_tensor(f"slot{i}", [128, D], F32)) for i in range(2)]
        negcq = e(nc.sbuf_tensor("negcq", [128, NT * H], F32))
        stats = e(nc.sbuf_tensor("stats", [128, 64], F32))
        kn2 = e(nc.sbuf_tensor("kn2", [H, T], F32))
        qn2 = e(nc.sbuf_tensor("qn2", [H, T], F32))
        bks = e(nc.sbuf_tensor("bks", [H, 8], F32))
        gam2 = e(nc.sbuf_tensor("gam2", [128, 4], F32))
        scalev = e(nc.sbuf_tensor("scalev", [128, 1], F32))
        grow = e(nc.sbuf_tensor("grow", [4, 512], F32))

        GD = Sem(ctx, nc, "GD")
        CC = Sem(ctx, nc, "CC")
        MM = Sem(ctx, nc, "MM")
        VD = Sem(ctx, nc, "VD")
        SD = Sem(ctx, nc, "SD")
        ST = Sem(ctx, nc, "ST")
        LSM = Sem(ctx, nc, "LSM")
        LX = Sem(ctx, nc, "LX")
        LWM = Sem(ctx, nc, "LWM")
        LMOD = Sem(ctx, nc, "LMOD")
        STG = Sem(ctx, nc, "STG")

        block = e(nc.Block())

        @block.vector
        def _(v):
            v.memset(scalev[:], SCALE)

        sA = ExitStack()
        eA = sA.enter_context
        h1f = eA(nc.sbuf_tensor("h1f", [128, NC * 1024], BF16))
        cosm = eA(nc.sbuf_tensor("sb_cosm", [128, T], F32))
        sinm = eA(nc.sbuf_tensor("sb_sinm", [128, T], F32))
        sB = ExitStack()
        eB = sB.enter_context
        xbuf = eB(nc.sbuf_tensor("xbuf", [128, 4 * D], F32))
        h1t = eB(nc.sbuf_tensor("h1t", [128, 4 * D], BF16))
        wmb = eB(nc.sbuf_tensor("wmb", [128, 2 * JSH], F32))
        scr = eB(nc.sbuf_tensor("scr", [128, D], F32))
        cm_sb = eB(nc.sbuf_tensor("cm_sb", [128, NC], F32))
        scm = eB(nc.sbuf_tensor("scm", [128, NC], F32))
        modsl = eB(nc.sbuf_tensor("modsl", [1, JSH], F32))

        @block.sync
        def _(sy):
            _dma(sy, LSM, cm_sb[:], cm_in)
            _dma(sy, LSM, cosm[:], cos_in)
            _dma(sy, LSM, sinm[:], sin_in)
            _dma(sy, LSM, bqm[:], bqm_in)
            _dma(sy, LSM, bkm[:], bkm_in)
            _dma(sy, LSM, bv4[:], bvrow_in)
            _dma(sy, LSM, bo4[:], borow_in)
            _dma(sy, LSM, fb4[:], fbrow_in)
            _dma(sy, LSM, eyeb[:], eyeb_in)
            _dma(sy, LSM, eyef[:], eyef_in)
            _dma(sy, LSM, onesf[:], onesf_in)
            _dma(sy, LSM, e4f[:], e4f_in)
            _dma(sy, LSM, ehb[:], ehb_in)
        ld_p0 = LSM.n

        @block.sync
        def _(sy):
            for i in range(4):
                _dma(sy, LX, xbuf[:, i * D:(i + 1) * D],
                     x_in[i * 128:(i + 1) * 128, :])
        ld_xw = [LX.n]

        with nc.psum_tensor("pmod", [1, JSH], F32) as pmod:
            @block.scalar
            def _(sc):
                sc.wait_ge(LSM.h, ld_p0)
                SD.inc(sc.activation(scm[:], cm_sb[:], AF.Silu))
            sd_silu = SD.n

            ld_wq = []
            mm_wq = []
            for stp in range(8):
                @block.sync
                def _(sy, stp=stp):
                    if stp >= 1:
                        sy.wait_ge(MM.h, mm_wq[stp - 1])
                    for i in range(2):
                        _dma(sy, LWM, wmb[:, i * JSH:(i + 1) * JSH],
                             wmods_in[stp * 2 + i])
                ld_wq.append(LWM.n)

                @block.tensor
                def _(tn, stp=stp):
                    tn.wait_ge(LWM.h, ld_wq[stp])
                    if stp == 0:
                        tn.wait_ge(SD.h, sd_silu)
                    last = None
                    for i in range(2):
                        ic = stp * 2 + i
                        for jt in range(JSH // 512):
                            last = tn.matmul(pmod[:, jt * 512:(jt + 1) * 512],
                                             scm[:, ic:ic + 1],
                                             wmb[:, i * JSH + jt * 512:
                                                 i * JSH + (jt + 1) * 512],
                                             start=(ic == 0), stop=(ic == NC - 1))
                    MM.inc(last)
                mm_wq.append(MM.n)
            mm_mod = MM.n

            @block.scalar
            def _(sc):
                sc.wait_ge(MM.h, mm_mod)
                SD.inc(sc.copy(modsl[:], pmod[:]))
            sd_modsl = SD.n

        @block.gpsimd
        def _(gp):
            gp.wait_ge(SD.h, sd_modsl)
            _dma(gp, GD, mod_in.ap().rearrange("(a b) -> a b", a=1), modsl[:])
            gp.wait_ge(GD.h, GD.n)
            CC.inc(gp.collective_compute(
                "AllGather", ALU.bypass,
                replica_groups=[list(range(CORES))],
                ins=[mod_in.ap()], outs=[mod_g.ap()]))
        cc_mod = CC.n

        @block.sync
        def _(sy):
            sy.wait_ge(CC.h, cc_mod)
            for m in range(6):
                _dma(sy, LMOD, modr[m][:],
                     mod_g.ap()[m * D:(m + 1) * D].rearrange("(p b) -> p b", b=512))
        ld_mod = LMOD.n

        MSA, MCA, MGA, MSB, MCB, MGB = range(6)

        def mrow(m):
            return modr[m][:]

        @block.vector
        def _(v):
            v.wait_ge(LMOD.h, ld_mod)
            v.tensor_scalar_add(mrow(MCA), mrow(MCA), 1.0)
            v.tensor_scalar_add(mrow(MCB), mrow(MCB), 1.0)
            for mg in (MGA, MGB):
                g = mrow(mg)
                v.tensor_scalar(rs0[:], g, 0.0, None, ALU.is_ge)
                v.tensor_scalar(rs0[:], rs0[:], 2.0, -1.0, ALU.mult, ALU.add)
                v.tensor_scalar_mul(rs1[:], g, -1.0)
                v.tensor_max(rs1[:], rs1[:], g)
                v.tensor_scalar_max(rs1[:], rs1[:], 0.25)
                v.tensor_mul(g, rs0[:], rs1[:])
            VD.inc(v.tensor_mul(gabo4[:], mrow(MGA), bo4[:]))
        vd_mod = VD.n

        def bcast_row(row4_ap, dst, tag, wait_sems):
            with nc.psum_tensor("pb_" + tag, [128, D], F32) as pb:
                prev_sd = SD.n

                @block.tensor
                def _(tn):
                    tn.wait_ge(SD.h, prev_sd)
                    for sem, val in wait_sems:
                        tn.wait_ge(sem.h, val)
                    last = None
                    for jt in range(4):
                        last = tn.matmul(pb[:, jt * 512:(jt + 1) * 512],
                                         e4f[:, jt * 128:(jt + 1) * 128],
                                         row4_ap[:], start=True, stop=True)
                    MM.inc(last)
                mmv = MM.n

                @block.scalar
                def _(sc):
                    sc.wait_ge(MM.h, mmv)
                    SD.inc(sc.copy(dst[:], pb[:]))
            return SD.n

        sd_ca1 = bcast_row(mrow(MCA), slot[0], "ca1", [(VD, vd_mod)])
        sd_sa = bcast_row(mrow(MSA), slot[1], "sa", [(VD, vd_mod)])

        vd_h1w = []
        dbg_st_gate = []
        for w in range(2):
            cs = range(w * 4, w * 4 + 4)

            if w == 1:
                @block.sync
                def _(sy):
                    sy.wait_ge(VD.h, vd_h1w[0])
                    for i in range(4):
                        _dma(sy, LX, xbuf[:, i * D:(i + 1) * D],
                             x_in[(4 + i) * 128:(5 + i) * 128, :])
                ld_xw.append(LX.n)

            @block.vector
            def _(v, w=w):
                v.wait_ge(LX.h, ld_xw[w])
                if w == 1 and dbg_st_gate:
                    v.wait_ge(ST.h, dbg_st_gate[-1])
                last = None
                for i in cs:
                    xc = xbuf[:, (i % 4) * D:((i % 4) + 1) * D]
                    last = v.reduce_sum(stats[:, i:i + 1], xc, axis=AXX)
                VD.inc(last)
            vd_s1 = VD.n

            @block.scalar
            def _(sc, w=w):
                sc.wait_ge(LX.h, ld_xw[w])
                last = None
                for i in cs:
                    xc = xbuf[:, (i % 4) * D:((i % 4) + 1) * D]
                    last = sc.activation(scr[:], xc, AF.Square,
                                         accum_out=stats[:, 8 + i:9 + i])
                SD.inc(last)
            sd_s2 = SD.n

            a, b = w * 4, w * 4 + 4

            @block.vector
            def _(v):
                v.wait_ge(VD.h, vd_s1)
                v.wait_ge(SD.h, sd_s2)
                v.tensor_scalar_mul(stats[:, 16 + a:16 + b], stats[:, a:b], 1.0 / D)
                v.tensor_scalar_mul(stats[:, 24 + a:24 + b], stats[:, 8 + a:8 + b], 1.0 / D)
                v.drain()
                v.tensor_mul(stats[:, 32 + a:32 + b], stats[:, 16 + a:16 + b],
                             stats[:, 16 + a:16 + b])
                v.drain()
                v.tensor_sub(stats[:, 24 + a:24 + b], stats[:, 24 + a:24 + b],
                             stats[:, 32 + a:32 + b])
                v.drain()
                VD.inc(v.tensor_scalar_add(stats[:, 24 + a:24 + b],
                                           stats[:, 24 + a:24 + b], 1e-5))
            vd_var = VD.n

            @block.scalar
            def _(sc):
                sc.wait_ge(VD.h, vd_var)
                SD.inc(sc.activation(stats[:, 32 + a:32 + b],
                                     stats[:, 24 + a:24 + b], AF.Sqrt))
            sd_sqrt = SD.n

            @block.vector
            def _(v):
                v.wait_ge(SD.h, sd_sqrt)
                v.reciprocal(stats[:, 40 + a:40 + b], stats[:, 32 + a:32 + b])
                v.drain()
                v.tensor_mul(stats[:, 48 + a:48 + b], stats[:, 16 + a:16 + b],
                             stats[:, 40 + a:40 + b])
                v.drain()
                VD.inc(v.tensor_scalar_mul(stats[:, 48 + a:48 + b],
                                           stats[:, 48 + a:48 + b], -1.0))
            vd_rstd = VD.n

            @block.scalar
            def _(sc):
                sc.wait_ge(VD.h, vd_rstd)
                last = None
                for i in cs:
                    last = sc.activation(xbuf[:, (i % 4) * D:((i % 4) + 1) * D],
                                         xbuf[:, (i % 4) * D:((i % 4) + 1) * D],
                                         AF.Identity,
                                         bias=stats[:, 48 + i:49 + i],
                                         scale=stats[:, 40 + i:41 + i])
                SD.inc(last)
            sd_u = SD.n

            d_h1 = dbg_out("h1", [T, D], BF16) if w == 0 else d_h1

            @block.vector
            def _(v):
                v.wait_ge(SD.h, sd_u)
                v.wait_ge(SD.h, max(sd_ca1, sd_sa))
                last = None
                for i in cs:
                    u = xbuf[:, (i % 4) * D:((i % 4) + 1) * D]
                    v.tensor_mul(u, u, slot[0][:])
                    last = v.tensor_add(h1t[:, (i % 4) * D:((i % 4) + 1) * D],
                                        u, slot[1][:])
                VD.inc(last)
            vd_h1w.append(VD.n)
            vd_h1wv = VD.n

            if d_h1 is not None:
                @block.sync
                def _(sy):
                    sy.wait_ge(VD.h, vd_h1wv)
                    for i in cs:
                        _dma(sy, ST, d_h1[i * 128:(i + 1) * 128, :],
                             h1t[:, (i % 4) * D:((i % 4) + 1) * D])
                dbg_st_gate.append(ST.n)

            with nc.psum_tensor(f"pt1a{w}", [128, 512], BF16) as pt1a, \
                 nc.psum_tensor(f"pt1b{w}", [128, 512], BF16) as pt1b:
                pts = [pt1a, pt1b]

                mm_t1base = MM.n

                @block.tensor
                def _(tn):
                    tn.wait_ge(VD.h, vd_h1wv)
                    for fc in range(NC):
                        if fc >= 2:
                            tn.wait_ge(VD.h, vd_h1wv + fc - 1)
                        p = pts[fc % 2]
                        last = None
                        for i in range(4):
                            last = tn.transpose(p[:, i * 128:(i + 1) * 128],
                                                h1t[:, i * D + fc * 128:
                                                    i * D + (fc + 1) * 128],
                                                eyeb[:])
                        MM.inc(last)
                mm_t1 = MM.n

                @block.vector
                def _(v):
                    for fc in range(NC):
                        v.wait_ge(MM.h, mm_t1base + fc + 1)
                        VD.inc(v.tensor_copy(
                            h1f[:, fc * T + w * 512: fc * T + (w + 1) * 512],
                            pts[fc % 2][:]))
                vd_h1w[-1] = VD.n

        vd_t1 = VD.n

        d_stats = dbg_out("stats", [128, 64], F32)
        if d_stats is not None:
            @block.sync
            def _(sy):
                sy.wait_ge(VD.h, vd_t1)
                _dma(sy, ST, d_stats, stats[:])

        d_slot0 = dbg_out("slot0", [128, D], F32)
        if d_slot0 is not None:
            @block.sync
            def _(sy):
                sy.wait_ge(VD.h, vd_t1)
                _dma(sy, ST, d_slot0, slot[0][:])

        d_slot1 = dbg_out("slot1", [128, D], F32)
        if d_slot1 is not None:
            @block.sync
            def _(sy):
                sy.wait_ge(VD.h, vd_t1)
                _dma(sy, ST, d_slot1, slot[1][:])

        sB.close()

        sC = ExitStack()
        eC = sC.enter_context
        wkb = eC(nc.sbuf_tensor("wkb", [128, 2 * D], BF16))
        bvb = eC(nc.sbuf_tensor("bvb", [128, D], F32))
        qf = eC(nc.sbuf_tensor("qf", [128, T], F32))
        rot = eC(nc.sbuf_tensor("rot", [128, T], F32))
        ta_b = eC(nc.sbuf_tensor("ta_b", [128, T], F32))
        kbf = eC(nc.sbuf_tensor("kbf", [128, T], BF16))
        k2b = eC(nc.sbuf_tensor("k2b", [128, T], BF16))
        knb = eC(nc.sbuf_tensor("knb", [16, 1], BF16))
        sCV = ExitStack()
        eCV = sCV.enter_context
        wvb = eCV(nc.sbuf_tensor("wvb", [128, NC * D], BF16))
        vtm = eCV(nc.sbuf_tensor("vtm", [128, D], BF16))

        LWK = Sem(ctx, nc, "LWK")
        LWV = Sem(ctx, nc, "LWV")
        TSK = Sem(ctx, nc, "TSK")
        SKE = Sem(ctx, nc, "SKE")
        VKR = Sem(ctx, nc, "VKR")
        TKN = Sem(ctx, nc, "TKN")
        TV = Sem(ctx, nc, "TV")
        VV = Sem(ctx, nc, "VV")

        LWKP = [Sem(ctx, nc, "LWKP0"), Sem(ctx, nc, "LWKP1")]
        STK = Sem(ctx, nc, "STK")

        @block.gpsimd
        def _(gp):
            gp.wait_ge(VD.h, vd_t1)
            gp.wait_ge(MM.h, mm_mod)

        sd_bv = bcast_row(bv4[:], bvb, "bv", [(LSM, ld_p0), (VD, vd_t1),
                                              (MM, mm_mod)])

        def kq_phase(w4_in, bias_m, stage_region, tag, qn_psum_dst, norm_sem_t):
            tsk0, ske0, vkr0 = TSK.n, SKE.n, VKR.n
            nrm0 = norm_sem_t.n
            stk0 = STK.n
            lwk0 = [LWKP[0].n, LWKP[1].n]
            with nc.psum_tensor("pqk_" + tag, [128, 2 * T], F32) as pqk:
                @block.gpsimd
                def _(gp):
                    for oc in range(NC):
                        if oc >= 2:
                            gp.wait_ge(TSK.h, tsk0 + oc - 1)
                        _dma(gp, LWKP[oc % 2],
                             wkb[:, (oc % 2) * D:(oc % 2 + 1) * D],
                             w4_in[oc])

                @block.sync
                def _(sy):
                    for oc in range(1, NC + 1):
                        sy.wait_ge(VKR.h, vkr0 + oc)
                        _dma(sy, STK, stage_region(oc - 1), kbf[:])

                def knorm(tn, oc):
                    tn.wait_ge(VKR.h, vkr0 + oc + 1)
                    last = None
                    for tt in range(2):
                        last = tn.matmul(
                            qn_psum_dst[:, tt * 512:(tt + 1) * 512],
                            ehb[:, oc * H:(oc + 1) * H],
                            k2b[:, tt * 512:(tt + 1) * 512],
                            start=(oc == 0), stop=(oc == NC - 1))
                    norm_sem_t.inc(last)

                @block.tensor
                def _(tn):
                    for oc in range(NC):
                        tn.wait_ge(LWKP[oc % 2].h,
                                   lwk0[oc % 2] + 16 * (oc // 2 + 1))
                        tn.wait_ge(SKE.h, max(ske0 + oc - 1, ske0))
                        tn.wait_ge(SD.h, SD.n)
                        last = None
                        for tt in range(2):
                            for ic in range(NC):
                                last = tn.matmul(
                                    pqk[:, (oc % 2) * T + tt * 512:
                                        (oc % 2) * T + (tt + 1) * 512],
                                    wkb[:, (oc % 2) * D + ic * 128:
                                        (oc % 2) * D + (ic + 1) * 128],
                                    h1f[:, ic * T + tt * 512:
                                        ic * T + (tt + 1) * 512],
                                    start=(ic == 0), stop=(ic == NC - 1))
                        TSK.inc(last)
                        if oc >= 2:
                            knorm(tn, oc - 2)
                    knorm(tn, NC - 2)
                    knorm(tn, NC - 1)

                @block.scalar
                def _(sc):
                    for oc in range(NC):
                        sc.wait_ge(TSK.h, tsk0 + oc + 1)
                        if oc >= 1:
                            sc.wait_ge(VKR.h, vkr0 + oc)
                        SKE.inc(sc.activation(
                            qf[:], pqk[:, (oc % 2) * T:(oc % 2 + 1) * T],
                            AF.Identity, bias=bias_m[:, oc:oc + 1]))

                @block.vector
                def _(v):
                    for oc in range(NC):
                        v.wait_ge(SKE.h, ske0 + oc + 1)
                        v.wait_ge(STK.h, stk0 + 16 * oc)
                        if oc >= 1:
                            v.wait_ge(norm_sem_t.h, nrm0 + oc)
                        v.tensor_scalar_mul(rot[0:64, :], qf[64:128, :], -1.0)
                        v.tensor_copy(rot[64:128, :], qf[0:64, :])
                        v.tensor_mul(ta_b[:], qf[:], cosm[:])
                        v.tensor_mul(rot[:], rot[:], sinm[:])
                        v.tensor_add(kbf[:], ta_b[:], rot[:])
                        VKR.inc(v.tensor_mul(k2b[:], kbf[:], kbf[:]))
            return tsk0, ske0, vkr0

        with nc.psum_tensor("pkn", [16, T], F32) as pkn:
            kq_phase_gd0 = GD.n
            tsk0, ske0, vkr0 = kq_phase(
                wk4, bkm,
                lambda oc: kv_in_k.ap()[oc * 131072:(oc + 1) * 131072]
                .rearrange("(p f) -> p f", f=T),
                "k", pkn, TKN)

            @block.scalar
            def _(sc):
                sc.wait_ge(TKN.h, TKN.n)
                SKE.inc(sc.copy(kn2[:], pkn[:]))
            ske_kn = SKE.n

        @block.vector
        def _(v):
            v.wait_ge(SKE.h, ske_kn)
            v.reduce_max(bks[:, 0:1], kn2[:], axis=AXX)
            v.drain()
            VKR.inc(v.tensor_copy(knb[:], bks[:, 0:1]))
        vkr_kn = VKR.n

        @block.gpsimd
        def _(gp):
            gp.wait_ge(VKR.h, vkr_kn)
            _dma(gp, GD, kv_in_k.ap()[KSZ:KSZ + H]
                 .rearrange("(a b) -> a b", a=16), knb[:])

        @block.gpsimd
        def _(gp):
            gp.wait_ge(GD.h, GD.n)
            gp.wait_ge(STK.h, STK.n)
            CC.inc(gp.collective_compute(
                "AllGather", ALU.bypass,
                replica_groups=[list(range(CORES))],
                ins=[kv_in_k.ap()], outs=[kvg_k.ap()]))
        cc_agk = CC.n

        @block.gpsimd
        def _(gp):
            for ic in range(NC):
                _dma(gp, LWV, wvb[:, ic * D:(ic + 1) * D], wv_r[ic])
        ld_wv = LWV.n

        vr = kv_in_v.ap()[0:VSZ].rearrange("(h p c d) -> h p c d",
                                           h=H, p=128, c=NT)
        with nc.psum_tensor("pv", [128, 2 * D], F32) as pv:
            tv0, vv0 = TV.n, VV.n
            gdv0 = GD.n

            @block.tensor
            def _(tn):
                tn.wait_ge(LWV.h, ld_wv)
                for tc in range(NT):
                    if tc >= 2:
                        tn.wait_ge(VV.h, vv0 + tc - 1)
                    last = None
                    for ic in range(NC):
                        for ot in range(4):
                            last = tn.matmul(
                                pv[:, (tc % 2) * D + ot * 512:
                                   (tc % 2) * D + (ot + 1) * 512],
                                h1f[:, ic * T + tc * 128:ic * T + (tc + 1) * 128],
                                wvb[:, ic * D + ot * 512:ic * D + (ot + 1) * 512],
                                start=(ic == 0), stop=(ic == NC - 1))
                    TV.inc(last)

            @block.vector
            def _(v):
                v.wait_ge(SD.h, sd_bv)
                for tc in range(NT):
                    v.wait_ge(TV.h, tv0 + tc + 1)
                    if tc >= 1:
                        v.wait_ge(GD.h, gdv0 + 16 * tc)
                    VV.inc(v.tensor_add(vtm[:], pv[:, (tc % 2) * D:
                                                   (tc % 2 + 1) * D],
                                        bvb[:]))

            @block.gpsimd
            def _(gp):
                for tc in range(NT):
                    gp.wait_ge(VV.h, vv0 + tc + 1)
                    _dma(gp, GD, vr[:, :, tc, :].transpose([1, 0, 2]),
                         vtm[:].rearrange("p (h d) -> p h d", h=H))

        @block.gpsimd
        def _(gp):
            gp.wait_ge(GD.h, GD.n)
            CC.inc(gp.collective_compute(
                "AllGather", ALU.bypass,
                replica_groups=[list(range(CORES))],
                ins=[kv_in_v.ap()], outs=[kvg_v.ap()]))
        cc_kv = CC.n
        sCV.close()

        LFW = Sem(ctx, nc, "LFW")
        STN = Sem(ctx, nc, "STN")
        SVN = Sem(ctx, nc, "SVN")
        SSN = Sem(ctx, nc, "SSN")
        fwsb = eC(nc.sbuf_tensor("fwsb", [128, 2 * D], F32))
        tscr = eC(nc.sbuf_tensor("tscr", [128, D], F32))
        trat = eC(nc.sbuf_tensor("trat", [128, D], F32))
        tgab = eC(nc.sbuf_tensor("tgab", [128, D], BF16))
        tgbb = eC(nc.sbuf_tensor("tgbb", [128, D], BF16))
        wtb = eC(nc.sbuf_tensor("wtb", [128, 2 * D], BF16))
        ttr = eC(nc.sbuf_tensor("ttr", [128, NSH], BF16))

        gd_p5 = GD.n

        @block.sync
        def _(sy):
            sy.wait_ge(TV.h, TV.n)
            sy.wait_ge(VV.h, VV.n)
            sy.wait_ge(GD.h, gd_p5)
            for ch in range(2):
                _dma(sy, LFW, fwsb[:, ch * D:(ch + 1) * D], fws_in[ch])
        ld_fw = LFW.n

        @block.scalar
        def _(sc):
            sc.wait_ge(LFW.h, ld_fw)
            last = None
            for ch in range(2):
                last = sc.activation(tscr[:], fwsb[:, ch * D:(ch + 1) * D],
                                     AF.Abs, accum_out=gam2[:, ch:ch + 1])
            SSN.inc(last)
        ssn_g = SSN.n

        @block.vector
        def _(v):
            v.wait_ge(SSN.h, ssn_g)
            v.tensor_scalar_mul(gam2[:, 0:2], gam2[:, 0:2], 1.0 / D)
            v.drain()
            v.tensor_scalar_add(gam2[:, 2:4], gam2[:, 0:2], 1e-8)
            v.drain()
            v.reciprocal(gam2[:, 2:4], gam2[:, 2:4])
            v.drain()
            last = None
            for ch in range(2):
                v.tensor_scalar_mul(trat[:], fwsb[:, ch * D:(ch + 1) * D],
                                    gam2[:, 2 + ch:3 + ch])
                v.tensor_scalar(tgab[:], trat[:], 0.5, None, ALU.is_ge)
                v.tensor_scalar(tgbb[:], trat[:], -1.0, 0.5, ALU.mult, ALU.is_ge)
                last = v.tensor_sub(wtb[:, ch * D:(ch + 1) * D], tgab[:],
                                    tgbb[:])
            SVN.inc(last)
        svn_t = SVN.n

        with nc.psum_tensor("ptn", [128, NSH], BF16) as ptn:
            gdt0 = GD.n
            @block.tensor
            def _(tn):
                tn.wait_ge(SVN.h, svn_t)
                for kc in range(NC):
                    if kc >= 1:
                        tn.wait_ge(SVN.h, svn_t + kc)
                    last = None
                    for ch in range(2):
                        last = tn.transpose(
                            ptn[:, ch * 128:(ch + 1) * 128],
                            wtb[:, ch * D + kc * 128:ch * D + (kc + 1) * 128],
                            eyeb[:])
                    STN.inc(last)
            stn0 = STN.n - NC

            @block.vector
            def _(v):
                for kc in range(NC):
                    v.wait_ge(STN.h, stn0 + kc + 1)
                    if kc >= 1:
                        v.wait_ge(GD.h, gdt0 + 16 * kc)
                    SVN.inc(v.tensor_copy(ttr[:], ptn[:]))

            @block.gpsimd
            def _(gp):
                for kc in range(NC):
                    gp.wait_ge(SVN.h, svn_t + kc + 1)
                    _dma(gp, GD, tern_in[kc], ttr[:])
                gp.wait_ge(SVN.h, svn_t)
                gp.dma_start(out=gam_in.ap().rearrange("(c p) -> p c", p=128),
                             in_=gam2[:, 0:2],
                             allow_slow_non_contiguous=True).then_inc(GD.h, 16)
                GD.n += 16
                gp.wait_ge(GD.h, GD.n)
                CC.inc(gp.collective_compute(
                    "AllGather", ALU.bypass,
                    replica_groups=[list(range(CORES))],
                    ins=[gam_in.ap()], outs=[gam_g.ap()]))
                CC.inc(gp.collective_compute(
                    "AllGather", ALU.bypass,
                    replica_groups=[list(range(CORES))],
                    ins=[tern_in.ap()], outs=[tern_g.ap()]))
        cc_all = CC.n

        with nc.psum_tensor("pqn", [16, T], F32) as pqn:
            TQN = Sem(ctx, nc, "TQN")
            tskq0, skeq0, vkrq0 = kq_phase(
                wq4, bqm,
                lambda oc: q_st.ap()[oc],
                "q", pqn, TQN)

            @block.scalar
            def _(sc):
                sc.wait_ge(TQN.h, TQN.n)
                SKE.inc(sc.copy(qn2[:], pqn[:]))
            ske_qn = SKE.n

        LKN = Sem(ctx, nc, "LKN")
        knall = eC(nc.sbuf_tensor("knall", [16, 8], BF16))
        qn2t = eC(nc.sbuf_tensor("qn2t", [128, 128], F32))
        bksrow = eC(nc.sbuf_tensor("bksrow", [1, 16], F32))
        bksb = eC(nc.sbuf_tensor("bksb", [128, 16], F32))

        @block.sync
        def _(sy):
            sy.wait_ge(CC.h, cc_agk)
            kn_ap = kvg_k.ap().rearrange("(r e) -> e r", r=CORES)[
                KSZ:KSZ + H, :]
            sy.dma_start(out=knall[:], in_=kn_ap,
                         allow_slow_non_contiguous=True).then_inc(LKN.h, 16)
            LKN.n += 16
        ld_kn = LKN.n

        @block.vector
        def _(v):
            v.wait_ge(LKN.h, ld_kn)
            VKR.inc(v.reduce_max(bks[:, 1:2], knall[:], axis=AXX))
        vkr_bk = VKR.n

        @block.scalar
        def _(sc):
            sc.wait_ge(VKR.h, vkr_bk)
            SKE.inc(sc.activation(bks[:, 2:3], bks[:, 1:2], AF.Sqrt))
        ske_bk = SKE.n

        @block.vector
        def _(v):
            v.wait_ge(SKE.h, ske_bk)
            VKR.inc(v.tensor_scalar_mul(bks[:, 3:4], bks[:, 2:3], SCALE))
        vkr_bk2 = VKR.n

        with nc.psum_tensor("pnq", [128, 128], F32) as pnq, \
             nc.psum_tensor("pbr", [1, 16], F32) as pbr, \
             nc.psum_tensor("pbb", [128, 16], F32) as pbb:
            TNC = Sem(ctx, nc, "TNC")

            @block.tensor
            def _(tn):
                tn.wait_ge(SKE.h, ske_qn)
                last = None
                for qt in range(NT):
                    last = tn.transpose(pnq[:, qt * 16:(qt + 1) * 16],
                                        qn2[:, qt * 128:(qt + 1) * 128],
                                        eyef[0:16, 0:16])
                TNC.inc(last)
                tn.wait_ge(VKR.h, vkr_bk2)
                TNC.inc(tn.transpose(pbr[:], bks[:, 3:4],
                                     eyef[0:16, 0:16]))
            tnc_t = TNC.n

            @block.scalar
            def _(sc):
                sc.wait_ge(TNC.h, tnc_t)
                sc.copy(qn2t[:], pnq[:])
                SKE.inc(sc.copy(bksrow[:], pbr[:]))
            ske_nt = SKE.n

            @block.tensor
            def _(tn):
                tn.wait_ge(SKE.h, ske_nt)
                TNC.inc(tn.matmul(pbb[:], onesf[:], bksrow[:],
                                  start=True, stop=True))
            tnc_bb = TNC.n

            @block.scalar
            def _(sc):
                sc.wait_ge(TNC.h, tnc_bb)
                sc.copy(bksb[:], pbb[:])
                SKE.inc(sc.activation(qn2t[:], qn2t[:], AF.Sqrt))
            ske_sq = SKE.n

            @block.vector
            def _(v):
                v.wait_ge(SKE.h, ske_sq)
                last = None
                for qt in range(NT):
                    last = v.scalar_tensor_tensor(
                        negcq[:, qt * H:(qt + 1) * H],
                        qn2t[:, qt * H:(qt + 1) * H], -1.0, bksb[:],
                        ALU.mult, ALU.mult)
                VKR.inc(last)
            vkr_ncq = VKR.n

        d_kv = dbg_out("kv", [KVN], BF16)
        if d_kv is not None:
            @block.sync
            def _(sy):
                sy.wait_ge(CC.h, cc_kv)
                _dma(sy, ST, d_kv[0:KSZ], kv_in_k.ap()[0:KSZ])
                _dma(sy, ST, d_kv[KSZ:KSZ + VSZ], kv_in_v.ap())
                _dma(sy, ST, d_kv[KSZ + VSZ:KSZ + VSZ + H],
                     kv_in_k.ap()[KSZ:KSZ + H])
        d_q = dbg_out("q", [H, 128, T], BF16)
        if d_q is not None:
            @block.sync
            def _(sy):
                sy.wait_ge(SKE.h, ske_qn)
                for hh in range(H):
                    _dma(sy, ST, d_q[hh], q_st.ap()[hh])
        d_tern = dbg_out("tern", [NC, 128, NSH], BF16)
        if d_tern is not None:
            @block.sync
            def _(sy):
                sy.wait_ge(CC.h, cc_all)
                for kc in range(NC):
                    _dma(sy, ST, d_tern[kc], tern_in[kc])
        d_gam = dbg_out("gam", [NSH], F32)
        if d_gam is not None:
            @block.sync
            def _(sy):
                sy.wait_ge(CC.h, cc_all)
                _dma(sy, ST, d_gam, gam_in.ap())
        d_ncq = dbg_out("ncq", [128, NT * H], F32)
        if d_ncq is not None:
            @block.sync
            def _(sy):
                sy.wait_ge(VKR.h, vkr_ncq)
                _dma(sy, ST, d_ncq, negcq[:])

        sC.close()
        sA.close()

        sD1 = ExitStack()
        eD1 = sD1.enter_context
        O_fm = eD1(nc.sbuf_tensor("O_fm", [128, H * T], BF16, side="right"))
        sD2 = ExitStack()
        eD2 = sD2.enter_context
        kh = [eD2(nc.sbuf_tensor(f"kh{i}", [128, S], BF16)) for i in range(2)]
        vh = [eD2(nc.sbuf_tensor(f"vh{i}", [128, S], BF16)) for i in range(2)]
        qh = [eD2(nc.sbuf_tensor(f"qh{i}", [128, T], BF16)) for i in range(2)]
        pbuf = eD2(nc.sbuf_tensor("pbuf", [128, 4 * 4096], BF16))
        ptb = [eD2(nc.sbuf_tensor(f"ptb{i}", [128, 512], BF16)) for i in range(2)]
        dacc = eD2(nc.sbuf_tensor("dacc", [128, 32], F32))
        dent = eD2(nc.sbuf_tensor("dent", [128, 8], F32))
        denrow = eD2(nc.sbuf_tensor("denrow", [4, 128], F32))
        rbs = eD2(nc.sbuf_tensor("rbs", [128, 512], F32))

        LKV = [Sem(ctx, nc, "LKV0"), Sem(ctx, nc, "LKV1")]
        LVV = [Sem(ctx, nc, "LVV0"), Sem(ctx, nc, "LVV1")]
        ATS = Sem(ctx, nc, "ATS")
        AEX = Sem(ctx, nc, "AEX")
        ATT = Sem(ctx, nc, "ATT")
        ATE = Sem(ctx, nc, "ATE")
        APV = Sem(ctx, nc, "APV")
        ADN = Sem(ctx, nc, "ADN")
        TDT = Sem(ctx, nc, "TDT")
        SDR = Sem(ctx, nc, "SDR")
        TRB = Sem(ctx, nc, "TRB")
        ARB = Sem(ctx, nc, "ARB")
        AOE = Sem(ctx, nc, "AOE")

        gates = [(VKR, VKR.n), (SKE, SKE.n), (TQN, TQN.n), (SVN, SVN.n),
                 (STN, STN.n), (GD, GD.n), (STK, STK.n), (CC, cc_agk)]
        gd_attn = GD.n

        @block.sync
        def _(sy):
            for sem, val in gates:
                sy.wait_ge(sem.h, val)
            KN2 = KSZ + H
            for h in range(H):
                if h >= 2:
                    sy.wait_ge(APV.h, 128 * (h - 1))
                p = h % 2
                for r in range(CORES):
                    _dma(sy, LKV[p], kh[p][:, r * T:(r + 1) * T],
                         kvg_k.ap()[r * KN2 + h * 131072:
                                    r * KN2 + (h + 1) * 131072]
                         .rearrange("(p f) -> p f", f=T))
                _dma(sy, LKV[p], qh[p][:], q_st.ap()[h])
                if h == 0:
                    sy.wait_ge(CC.h, cc_kv)
                for r in range(CORES):
                    _dma(sy, LVV[p], vh[p][:, r * T:(r + 1) * T],
                         kvg_v.ap()[r * VSZ + h * 131072:
                                    r * VSZ + (h + 1) * 131072]
                         .rearrange("(p f) -> p f", f=T))

        with nc.psum_tensor("s2p", [128, 2 * T], F32) as s2p, \
             nc.psum_tensor("tpA", [128, T], BF16) as tpA, \
             nc.psum_tensor("tpB", [128, T], BF16) as tpB, \
             nc.psum_tensor("o_ps", [128, 512], F32) as o_ps, \
             nc.psum_tensor("pxtr", [128, 512], F32) as pxtr:
            tps = [tpA, tpB]

            NQG = 2 * H

            @block.tensor
            def _(tn):
                for h in range(H):
                    p = h % 2
                    tn.wait_ge(LKV[p].h, 144 * (h // 2 + 1))
                    for qg in range(2):
                        u = h * 2 + qg
                        ats0, aex0 = 32 * u, 32 * u
                        att0, ate0, apv0 = 64 * u, 64 * u, 64 * u

                        def tpv_batch(ksp):
                            for kt in range(8 * ksp, 8 * ksp + 8):
                                if kt == 8 * ksp:
                                    tn.wait_ge(AEX.h, aex0 + 4 * ksp + 4)
                                if ate0 + kt - 1 > 0:
                                    tn.wait_ge(ATE.h, ate0 + kt - 1)
                                last = None
                                for qt in range(4):
                                    last = tn.transpose(
                                        tps[kt % 2][:, qt * 128:(qt + 1) * 128],
                                        pbuf[:, qt * 4096 + (kt % 32) * 128:
                                             qt * 4096 + (kt % 32 + 1) * 128],
                                        eyeb[:])
                                ATT.inc(last)
                                pv_kt = kt - 1
                                if pv_kt >= 0:
                                    emit_pv(pv_kt)

                        def emit_pv(kt):
                            if kt == 0:
                                tn.wait_ge(AOE.h, u)
                            tn.wait_ge(ATE.h, ate0 + kt + 1)
                            APV.inc(tn.matmul(
                                o_ps[:], vh[p][:, kt * 128:(kt + 1) * 128],
                                ptb[kt % 2][:], start=(kt == 0),
                                stop=(kt == 63)))

                        for ks in range(8):
                            for qt in range(4):
                                ui = ks * 4 + qt
                                if aex0 + ui - 1 > 0:
                                    tn.wait_ge(AEX.h, aex0 + ui - 1)
                                last = None
                                for j in range(2):
                                    last = tn.matmul(
                                        s2p[:, (ui % 2) * T + j * 512:
                                            (ui % 2) * T + (j + 1) * 512],
                                        qh[p][:, qg * 512 + qt * 128:
                                              qg * 512 + (qt + 1) * 128],
                                        kh[p][:, ks * T + j * 512:
                                              ks * T + (j + 1) * 512],
                                        start=True, stop=True)
                                ATS.inc(last)
                            if ks >= 2:
                                if ks == 2 and qg == 0:
                                    tn.wait_ge(LVV[p].h, 128 * (h // 2 + 1))
                                tpv_batch(ks - 2)
                        tpv_batch(6)
                        tpv_batch(7)
                        emit_pv(63)
                        tn.wait_ge(ADN.h, u + 1)
                        TDT.inc(tn.transpose(pxtr[0:4, 0:128], dent[:, 4:8],
                                             eyef[:]))
                        tn.wait_ge(SDR.h, u + 1)
                        last = None
                        for qt in range(4):
                            last = tn.matmul(
                                pxtr[:, qt * 128:(qt + 1) * 128],
                                e4f[:, qt * 128:(qt + 1) * 128],
                                denrow[:], start=True, stop=True)
                        TRB.inc(last)

            @block.scalar
            def _(sc):
                sc.wait_ge(VKR.h, vkr_ncq)
                for h in range(H):
                    for qg in range(2):
                        u = h * 2 + qg
                        ats0, aex0, att0 = 32 * u, 32 * u, 64 * u
                        sc.wait_ge(ADN.h, u)
                        for ks in range(8):
                            for qt in range(4):
                                ui = ks * 4 + qt
                                sc.wait_ge(ATS.h, ats0 + ui + 1)
                                if ks >= 4 and qt == 0:
                                    sc.wait_ge(ATT.h, att0 + 8 * (ks - 3))
                                AEX.inc(sc.activation(
                                    pbuf[:, qt * 4096 + (ks % 4) * 1024:
                                         qt * 4096 + (ks % 4 + 1) * 1024],
                                    s2p[:, (ui % 2) * T:(ui % 2 + 1) * T],
                                    AF.Exp, scale=scalev[:, 0:1],
                                    bias=negcq[:, (qg * 4 + qt) * H + h:
                                               (qg * 4 + qt) * H + h + 1],
                                    accum_out=dacc[:, qt * 8 + ks:
                                                   qt * 8 + ks + 1]))
                        sc.wait_ge(TDT.h, u + 1)
                        SDR.inc(sc.copy(denrow[:], pxtr[0:4, 0:128]))
                        sc.wait_ge(TRB.h, u + 1)
                        ARB.inc(sc.copy(rbs[:], pxtr[:]))

            @block.vector
            def _(v):
                for h in range(H):
                    for qg in range(2):
                        u = h * 2 + qg
                        att0, ate0, apv0 = 64 * u, 64 * u, 64 * u
                        for kt in range(64):
                            v.wait_ge(ATT.h, att0 + kt + 1)
                            if apv0 + kt - 1 > 0:
                                v.wait_ge(APV.h, apv0 + kt - 1)
                            ATE.inc(v.tensor_copy(ptb[kt % 2][:],
                                                  tps[kt % 2][:, 0:512]))
                        v.wait_ge(AEX.h, 32 * u + 32)
                        if u >= 1:
                            v.wait_ge(TRB.h, u)
                        for qt in range(4):
                            v.reduce_sum(dent[:, qt:qt + 1],
                                         dacc[:, qt * 8:(qt + 1) * 8],
                                         axis=AXX)
                        v.drain()
                        ADN.inc(v.reciprocal(dent[:, 4:8], dent[:, 0:4]))
                        v.wait_ge(APV.h, apv0 + 64)
                        v.wait_ge(ARB.h, u + 1)
                        AOE.inc(v.tensor_mul(
                            O_fm[:, h * T + qg * 512:h * T + (qg + 1) * 512],
                            o_ps[:], rbs[:]))

        d_O = dbg_out("O", [128, H * T], BF16)
        if d_O is not None:
            @block.sync
            def _(sy):
                sy.wait_ge(AOE.h, AOE.n)
                _dma(sy, ST, d_O, O_fm[:])
        aoe_end = AOE.n
        sD2.close()

        x2 = e(nc.sbuf_tensor("x2", [128, NT * D], F32))
        sD3 = ExitStack()
        eD3 = sD3.enter_context
        attn_tm = eD3(nc.sbuf_tensor("attn_tm", [128, NT * D], BF16))
        wob = eD3(nc.sbuf_tensor("wob", [128, 2 * D], BF16))
        afm = eD3(nc.sbuf_tensor("afm", [128, T], BF16))
        xrb = eD3(nc.sbuf_tensor("xrb", [128, D], F32))

        LWO = [Sem(ctx, nc, "LWO0"), Sem(ctx, nc, "LWO1")]
        LXR = Sem(ctx, nc, "LXR")
        TWO = Sem(ctx, nc, "TWO")
        SWO = Sem(ctx, nc, "SWO")
        TWT = Sem(ctx, nc, "TWT")
        VWO = Sem(ctx, nc, "VWO")
        VX2 = Sem(ctx, nc, "VX2")

        sd_gab = bcast_row(mrow(MGA), slot[0], "gab", [(VD, vd_mod), (AOE, aoe_end)])
        sd_gabo = bcast_row(gabo4[:], slot[1], "gabo", [(VD, vd_mod), (AOE, aoe_end)])

        @block.gpsimd
        def _(gp):
            gp.wait_ge(AOE.h, aoe_end)
            for fc in range(NC):
                if fc >= 2:
                    gp.wait_ge(TWO.h, fc - 1)
                _dma(gp, LWO[fc % 2], wob[:, (fc % 2) * D:(fc % 2 + 1) * D],
                     wo4[fc])

        @block.sync
        def _(sy):
            sy.wait_ge(AOE.h, aoe_end)
            for tt in range(NT):
                if tt >= 1:
                    sy.wait_ge(VX2.h, tt)
                _dma(sy, LXR, xrb[:], x_in[tt * 128:(tt + 1) * 128, :])

        with nc.psum_tensor("pwo", [128, 2 * T], F32) as pwo, \
             nc.psum_tensor("ptw", [128, 2 * T], BF16) as ptw:
            def emit_wot(tn, fc):
                tn.wait_ge(SWO.h, fc + 1)
                if fc - 1 > 0:
                    tn.wait_ge(VWO.h, fc - 1)
                last = None
                for tc in range(NT):
                    last = tn.transpose(
                        ptw[:, (fc % 2) * T + tc * 128:
                            (fc % 2) * T + (tc + 1) * 128],
                        afm[:, tc * 128:(tc + 1) * 128], eyeb[:])
                TWT.inc(last)

            @block.tensor
            def _(tn):
                for fc in range(NC):
                    tn.wait_ge(LWO[fc % 2].h, 16 * (fc // 2 + 1))
                    if fc < 2:
                        tn.wait_ge(SD.h, max(sd_gab, sd_gabo))
                    if fc >= 2:
                        tn.wait_ge(SWO.h, fc - 1)
                    last = None
                    for tt2 in range(2):
                        for dc in range(NC):
                            last = tn.matmul(
                                pwo[:, (fc % 2) * T + tt2 * 512:
                                    (fc % 2) * T + (tt2 + 1) * 512],
                                wob[:, (fc % 2) * D + dc * 128:
                                    (fc % 2) * D + (dc + 1) * 128],
                                O_fm[:, dc * T + tt2 * 512:
                                     dc * T + (tt2 + 1) * 512],
                                start=(dc == 0), stop=(dc == NC - 1))
                    TWO.inc(last)
                    if fc >= 1:
                        emit_wot(tn, fc - 1)
                emit_wot(tn, NC - 1)

            @block.scalar
            def _(sc):
                for fc in range(NC):
                    sc.wait_ge(TWO.h, fc + 1)
                    if fc >= 1:
                        sc.wait_ge(TWT.h, fc)
                    SWO.inc(sc.copy(afm[:], pwo[:, (fc % 2) * T:
                                                (fc % 2 + 1) * T]))

            @block.vector
            def _(v):
                for fc in range(NC):
                    v.wait_ge(TWT.h, fc + 1)
                    at_view = attn_tm[:].rearrange(
                        "p (c f) -> p c f", c=NT)[:, :, fc * 128:(fc + 1) * 128]
                    VWO.inc(v.tensor_copy(
                        at_view, ptw[:, (fc % 2) * T:(fc % 2 + 1) * T]
                        .rearrange("p (c f) -> p c f", c=NT)))

        @block.vector
        def _(v):
            v.wait_ge(SD.h, max(sd_gab, sd_gabo))
            v.wait_ge(VWO.h, VWO.n)
            for tt in range(NT):
                v.wait_ge(LXR.h, 16 * (tt + 1))
                x2c = x2[:, tt * D:(tt + 1) * D]
                v.tensor_mul(x2c, attn_tm[:, tt * D:(tt + 1) * D], slot[0][:])
                v.tensor_add(x2c, x2c, slot[1][:])
                VX2.inc(v.tensor_add(x2c, x2c, xrb[:]))

        d_x2 = dbg_out("x2", [T, D], F32)
        if d_x2 is not None:
            @block.sync
            def _(sy):
                sy.wait_ge(VX2.h, VX2.n)
                for i in range(NT):
                    _dma(sy, ST, d_x2[i * 128:(i + 1) * 128, :],
                         x2[:, i * D:(i + 1) * D])

        sD3.close()
        sD1.close()

        sD4 = ExitStack()
        eD4 = sD4.enter_context
        xqf = eD4(nc.sbuf_tensor("xqf", [128, NC * T], BF16))
        tern_sb = eD4(nc.sbuf_tensor("tern_sb", [128, NC * 512], BF16))
        h2s = eD4(nc.sbuf_tensor("h2s", [128, D], F32))
        u9 = eD4(nc.sbuf_tensor("u9", [128, T], F32))
        v9b = eD4(nc.sbuf_tensor("v9b", [128, T], F32))
        d9 = eD4(nc.sbuf_tensor("d9", [128, T], F32))
        xqt = eD4(nc.sbuf_tensor("xqt", [128, D], BF16))
        fout = eD4(nc.sbuf_tensor("fout", [128, 512], F32))
        gbgb = eD4(nc.sbuf_tensor("gbgb", [128, 512], F32))
        gbfbb = eD4(nc.sbuf_tensor("gbfbb", [128, 512], F32))

        DXQ = Sem(ctx, nc, "DXQ")
        d_xqall = dbg_out("xqall", [NT, 128, D], BF16)
        LGG = Sem(ctx, nc, "LGG")
        LTH = Sem(ctx, nc, "LTH")
        S9 = Sem(ctx, nc, "S9")
        V9 = Sem(ctx, nc, "V9")
        T9 = Sem(ctx, nc, "T9")
        TF = Sem(ctx, nc, "TF")
        SF = Sem(ctx, nc, "SF")
        VF = Sem(ctx, nc, "VF")

        @block.sync
        def _(sy):
            sy.wait_ge(CC.h, cc_all)
            _dma(sy, LGG, grow[:], gam_g.ap().rearrange("(p b) -> p b", b=512))
        ld_gg = LGG.n

        @block.vector
        def _(v):
            v.wait_ge(LGG.h, ld_gg)
            v.wait_ge(VX2.h, VX2.n)
            v.tensor_mul(gbg4[:], mrow(MGB), grow[:])
            VD.inc(v.tensor_mul(gbfb4[:], mrow(MGB), fb4[:]))
        vd_gb = VD.n

        sd_cb1 = bcast_row(mrow(MCB), slot[0], "cb1", [(VD, vd_gb)])
        sd_sb = bcast_row(mrow(MSB), slot[1], "sbb", [(VD, vd_gb)])


        def load_tern_q(q, gate_sem, gate_val):
            @block.sync
            def _(sy):
                sy.wait_ge(CC.h, cc_all)
                if gate_sem is not None:
                    sy.wait_ge(gate_sem.h, gate_val)
                for kc in range(NC):
                    for r in range(2):
                        _dma(sy, LTH, tern_sb[:, kc * 512 + r * NSH:
                                              kc * 512 + (r + 1) * NSH],
                             tern_g[q * 2 + r, kc])
            return LTH.n

        ld_th0 = load_tern_q(0, None, 0)

        v9_xq_marks = []
        @block.vector
        def _(v):
            v.wait_ge(SD.h, max(sd_cb1, sd_sb))
            V9.inc(v.tensor_copy(stats[:, 0:1], stats[:, 0:1]))
        v9_bc = V9.n

        with nc.psum_tensor("pxq", [128, D], BF16) as pxq, \
             nc.psum_tensor("pf", [128, 2 * 512], F32) as pf:
            for tt in range(NT):
                v9_prev = V9.n

                @block.scalar
                def _(sc, tt=tt):
                    if tt == 0:
                        sc.wait_ge(V9.h, v9_bc)
                    sc.wait_ge(VX2.h, tt + 1)
                    sc.wait_ge(V9.h, v9_prev)
                    x2c = x2[:, tt * D:(tt + 1) * D]
                    S9.inc(sc.activation(h2s[:], x2c, AF.Square,
                                         accum_out=stats[:, 1:2]))
                s9_sq = S9.n

                @block.vector
                def _(v, tt=tt):
                    x2c = x2[:, tt * D:(tt + 1) * D]
                    v.wait_ge(S9.h, s9_sq)
                    v.reduce_sum(stats[:, 0:1], x2c, axis=AXX)
                    v.drain()
                    v.tensor_scalar_mul(stats[:, 2:3], stats[:, 0:1], 1.0 / D)
                    v.tensor_scalar_mul(stats[:, 3:4], stats[:, 1:2], 1.0 / D)
                    v.drain()
                    v.tensor_mul(stats[:, 4:5], stats[:, 2:3], stats[:, 2:3])
                    v.drain()
                    v.tensor_sub(stats[:, 3:4], stats[:, 3:4], stats[:, 4:5])
                    v.drain()
                    V9.inc(v.tensor_scalar_add(stats[:, 3:4], stats[:, 3:4],
                                               1e-5))
                v9_var = V9.n

                @block.scalar
                def _(sc, tt=tt):
                    sc.wait_ge(V9.h, v9_var)
                    S9.inc(sc.activation(stats[:, 4:5], stats[:, 3:4],
                                         AF.Sqrt))
                s9_sqrt = S9.n

                @block.vector
                def _(v, tt=tt):
                    v.wait_ge(S9.h, s9_sqrt)
                    v.reciprocal(stats[:, 5:6], stats[:, 4:5])
                    v.drain()
                    v.tensor_mul(stats[:, 6:7], stats[:, 2:3], stats[:, 5:6])
                    v.drain()
                    V9.inc(v.tensor_scalar_mul(stats[:, 6:7], stats[:, 6:7],
                                               -1.0))
                v9_rstd = V9.n

                s9_u0 = S9.n
                v9_h20 = V9.n

                @block.scalar
                def _(sc, tt=tt):
                    sc.wait_ge(V9.h, v9_rstd)
                    for jh in range(2):
                        if jh == 1:
                            sc.wait_ge(V9.h, v9_h20 + 1)
                        S9.inc(sc.activation(
                            u9[:], x2[:, tt * D + jh * T:tt * D + (jh + 1) * T],
                            AF.Identity, bias=stats[:, 6:7],
                            scale=stats[:, 5:6]))

                @block.vector
                def _(v, tt=tt):
                    for jh in range(2):
                        v.wait_ge(S9.h, s9_u0 + jh + 1)
                        hs = h2s[:, jh * T:(jh + 1) * T]
                        v.tensor_mul(hs, u9[:], slot[0][:, jh * T:(jh + 1) * T])
                        V9.inc(v.tensor_add(hs, hs,
                                            slot[1][:, jh * T:(jh + 1) * T]))
                v9_h2 = V9.n

                @block.scalar
                def _(sc, tt=tt):
                    sc.wait_ge(V9.h, v9_h2)
                    last = None
                    for jh in range(2):
                        last = sc.activation(u9[:], h2s[:, jh * T:(jh + 1) * T],
                                             AF.Square,
                                             accum_out=stats[:, 13 + jh:14 + jh])
                    S9.inc(last)
                s9_r2 = S9.n

                @block.vector
                def _(v, tt=tt):
                    v.wait_ge(S9.h, s9_r2)
                    v.tensor_add(stats[:, 7:8], stats[:, 13:14], stats[:, 14:15])
                    v.drain()
                    v.tensor_scalar(stats[:, 8:9], stats[:, 7:8], 1.0 / D,
                                    1e-8, ALU.mult, ALU.add)
                    V9.inc(v.engine_nop())
                v9_ra = V9.n

                @block.scalar
                def _(sc, tt=tt):
                    sc.wait_ge(V9.h, v9_ra)
                    S9.inc(sc.activation(stats[:, 8:9], stats[:, 8:9],
                                         AF.Sqrt))
                s9_rs = S9.n

                @block.vector
                def _(v, tt=tt):
                    v.wait_ge(S9.h, s9_rs)
                    V9.inc(v.reciprocal(stats[:, 9:10], stats[:, 8:9]))
                v9_ri = V9.n

                @block.scalar
                def _(sc, tt=tt):
                    sc.wait_ge(V9.h, v9_ri)
                    S9.inc(sc.activation(h2s[:], h2s[:], AF.Identity,
                                         scale=stats[:, 9:10]))
                s9_xn = S9.n

                t9_prev = T9.n

                @block.vector
                def _(v, tt=tt):
                    v.wait_ge(S9.h, s9_xn)
                    v.wait_ge(T9.h, t9_prev)
                    if d_xqall is not None and tt >= 1:
                        v.wait_ge(DXQ.h, 16 * tt)
                    v.reduce_max(stats[:, 10:11], h2s[:], axis=AXX,
                                 apply_absolute_value=True)
                    v.drain()
                    v.tensor_scalar(stats[:, 11:12], stats[:, 10:11],
                                    1.0 / 127.0, 1e-8, ALU.mult, ALU.add)
                    v.drain()
                    v.reciprocal(stats[:, 12:13], stats[:, 11:12])
                    v.drain()
                    v.tensor_scalar_mul(stats[:, 16 + tt:17 + tt],
                                        stats[:, 10:11], 1.0 / 127.0)
                    v.drain()
                    MAGIC = 12582912.0
                    last = None
                    for jh in range(2):
                        v.tensor_scalar(u9[:], h2s[:, jh * T:(jh + 1) * T],
                                        stats[:, 12:13], MAGIC,
                                        ALU.mult, ALU.add)
                        last = v.tensor_scalar_add(
                            xqt[:, jh * T:(jh + 1) * T], u9[:], -MAGIC)
                    V9.inc(last)
                v9_xq = V9.n

                @block.tensor
                def _(tn, tt=tt):
                    tn.wait_ge(V9.h, v9_xq)
                    last = None
                    for kc in range(NC):
                        last = tn.transpose(
                            pxq[:, kc * 128:(kc + 1) * 128],
                            xqt[:, kc * 128:(kc + 1) * 128], eyeb[:])
                    T9.inc(last)
                t9_t = T9.n

                if d_xqall is not None:
                    @block.sync
                    def _(sy, tt=tt):
                        sy.wait_ge(V9.h, v9_xq)
                        _dma(sy, DXQ, d_xqall[tt], xqt[:])

                @block.vector
                def _(v, tt=tt):
                    v.wait_ge(T9.h, t9_t)
                    xq_view = xqf[:].rearrange(
                        "p (c f) -> p c f", c=NC)[:, :, tt * 128:(tt + 1) * 128]
                    V9.inc(v.tensor_copy(
                        xq_view, pxq[:]
                        .rearrange("p (c f) -> p c f", c=NC)))
                v9_xq_marks.append(V9.n)

            st_ffn0 = ST.n
            SGQ = Sem(ctx, nc, "SGQ")
            TGQ = Sem(ctx, nc, "TGQ")
            for q in range(4):
                if q >= 1:
                    load_tern_q(q, TF, q * NT)

                with nc.psum_tensor(f"pgq{q}", [128, 2 * 512], F32) as pgq:
                    @block.tensor
                    def _(tn, q=q):
                        tn.wait_ge(VD.h, vd_gb)
                        if q >= 1:
                            tn.wait_ge(SGQ.h, 2 * q)
                        tn.matmul(pgq[:, 0:512],
                                  e4f[:, q * 128:(q + 1) * 128], gbg4[:],
                                  start=True, stop=True)
                        TGQ.inc(tn.matmul(pgq[:, 512:1024],
                                          e4f[:, q * 128:(q + 1) * 128],
                                          gbfb4[:], start=True, stop=True))

                    @block.scalar
                    def _(sc, q=q):
                        sc.wait_ge(TGQ.h, q + 1)
                        if q >= 1:
                            sc.wait_ge(VF.h, q * NT)
                        sc.copy(gbgb[:], pgq[:, 0:512])
                        SGQ.inc(sc.copy(gbfbb[:], pgq[:, 512:1024]), 2)

                for tt in range(NT):
                    @block.tensor
                    def _(tn, q=q, tt=tt):
                        u = q * NT + tt
                        tn.wait_ge(LTH.h, LTH.n)
                        tn.wait_ge(V9.h, v9_xq_marks[tt])
                        if u >= 2:
                            tn.wait_ge(SF.h, u - 1)
                        last = None
                        for kc in range(NC):
                            last = tn.matmul(
                                pf[:, (u % 2) * 512:(u % 2 + 1) * 512],
                                xqf[:, kc * T + tt * 128:
                                    kc * T + (tt + 1) * 128],
                                tern_sb[:, kc * 512:(kc + 1) * 512],
                                start=(kc == 0), stop=(kc == NC - 1))
                        TF.inc(last)

                    @block.scalar
                    def _(sc, q=q, tt=tt):
                        u = q * NT + tt
                        sc.wait_ge(TF.h, u + 1)
                        if u >= 1:
                            sc.wait_ge(VF.h, u)
                            sc.wait_ge(ST.h, st_ffn0 + 16 * u)
                        SF.inc(sc.activation(fout[:],
                                             pf[:, (u % 2) * 512:
                                                (u % 2 + 1) * 512],
                                             AF.Identity,
                                             scale=stats[:, 16 + tt:17 + tt]))

                    @block.vector
                    def _(v, q=q, tt=tt):
                        u = q * NT + tt
                        v.wait_ge(SF.h, u + 1)
                        v.wait_ge(SGQ.h, 2 * (q + 1))
                        v.tensor_mul(fout[:], fout[:], gbgb[:])
                        v.tensor_add(fout[:], fout[:], gbfbb[:])
                        VF.inc(v.tensor_add(
                            fout[:], fout[:],
                            x2[:, tt * D + q * 512:tt * D + (q + 1) * 512]))

                    @block.sync
                    def _(sy, q=q, tt=tt):
                        u = q * NT + tt
                        sy.wait_ge(VF.h, u + 1)
                        _dma(sy, ST, out_t.ap()[tt * 128:(tt + 1) * 128,
                                                q * 512:(q + 1) * 512],
                             fout[:])


        d_xqt = dbg_out("xqt", [128, D], BF16)
        if d_xqt is not None:
            @block.sync
            def _(sy):
                sy.wait_ge(V9.h, V9.n)
                _dma(sy, ST, d_xqt, xqt[:])

        d_xqf = dbg_out("xqf", [128, NC * T], BF16)
        if d_xqf is not None:
            @block.sync
            def _(sy):
                sy.wait_ge(V9.h, V9.n)
                _dma(sy, ST, d_xqf, xqf[:])

        d_st9 = dbg_out("st9", [128, 24], F32)
        if d_st9 is not None:
            @block.sync
            def _(sy):
                sy.wait_ge(V9.h, V9.n)
                _dma(sy, ST, d_st9, stats[:, 0:24])

        @block.sync
        def _(sy):
            sy.wait_ge(ST.h, ST.n)

        sD4.close()
    return nc


def _rope_tables():
    inv_freq = 1.0 / 10000.0 ** (np.arange(0, HD, 2, dtype=np.float32) / HD)
    pos = np.arange(S, dtype=np.float32)
    freqs = np.outer(pos, inv_freq)
    emb = np.concatenate([freqs, freqs], axis=-1)
    return np.cos(emb).astype(np.float32), np.sin(emb).astype(np.float32)


def _prep_inputs(inputs):
    x = np.asarray(inputs["L0_attn_x"], np.float32)[0]
    WqT = np.asarray(inputs["L0_attn_Wq"], np.float32).T
    WkT = np.asarray(inputs["L0_attn_Wk"], np.float32).T
    WvT = np.asarray(inputs["L0_attn_Wv"], np.float32).T
    WoT = np.asarray(inputs["L0_attn_Wo"], np.float32).T

    def w4(WT):
        return np.ascontiguousarray(
            WT.reshape(NC, 128, NC, 128).transpose(2, 1, 0, 3))

    cos_t, sin_t = _rope_tables()
    common = {
        "wq4": w4(WqT),
        "wk4": w4(WkT),
        "wo4": w4(WoT),
        "wv_r": np.ascontiguousarray(WvT.reshape(NC, 128, D)),
        "bqm": np.ascontiguousarray(
            np.asarray(inputs["L0_attn_bq"], np.float32).reshape(NC, 128).T),
        "bkm": np.ascontiguousarray(
            np.asarray(inputs["L0_attn_bk"], np.float32).reshape(NC, 128).T),
        "bvrow": np.asarray(inputs["L0_attn_bv"], np.float32).reshape(4, 512),
        "borow": np.asarray(inputs["L0_attn_bo"], np.float32).reshape(4, 512),
        "fbrow": np.asarray(inputs["L0_ffn_bias"], np.float32).reshape(4, 512),
        "cm": np.ascontiguousarray(
            np.asarray(inputs["L0_c"], np.float32).reshape(NC, 128).T),
        "eyeb": np.eye(128, dtype=np.float32).astype(ml_dtypes.bfloat16),
        "eyef": np.eye(128, dtype=np.float32),
        "onesf": np.ones((1, 128), np.float32),
        "e4f": _e4f(),
        "ehb": _ehb(),
    }
    fw = np.asarray(inputs["L0_ffn_weight"], np.float32)
    wmod = np.asarray(inputs["L0_wmod"], np.float32)
    in_maps = []
    for r in range(CORES):
        m = dict(common)
        m["x"] = np.ascontiguousarray(x[r * T:(r + 1) * T])
        m["fws"] = np.ascontiguousarray(
            fw[r * NSH:(r + 1) * NSH].reshape(2, 128, D))
        m["wmods"] = np.ascontiguousarray(
            wmod[:, r * JSH:(r + 1) * JSH].reshape(NC, 128, JSH))
        m["cosm"] = np.ascontiguousarray(cos_t[r * T:(r + 1) * T].T)
        m["sinm"] = np.ascontiguousarray(sin_t[r * T:(r + 1) * T].T)
        in_maps.append(m)
    return in_maps


def _e4f():
    m = np.zeros((4, 512), np.float32)
    for jt in range(4):
        m[jt, jt * 128:(jt + 1) * 128] = 1.0
    return m


def _ehb():
    e_all = np.zeros((128, H * H), np.float32)
    for oc in range(H):
        e_all[:, oc * H + oc] = 1.0
    return e_all.astype(ml_dtypes.bfloat16)


def kernel(**inputs):
    in_maps = _prep_inputs(inputs)
    nc = build_nc()
    res = run_bass_kernel_spmd(nc, in_maps, core_ids=list(range(CORES)))
    out = np.concatenate([np.asarray(r["out"]) for r in res.results], axis=0)
    return out.reshape(1, S, D).astype(np.float32)
